# version 08
# baseline (speedup 1.0000x reference)
import numpy as np
import ml_dtypes
from contextlib import ExitStack

import concourse.bass as bass
import concourse.mybir as mybir
from concourse.bass_utils import run_bass_kernel_spmd

F32 = mybir.dt.float32
BF16 = mybir.dt.bfloat16
AF = mybir.ActivationFunctionType
ALU = mybir.AluOpType
BF = ml_dtypes.bfloat16

NC = 8
S = 8192
D = 2048
H = 16
DK = 128
T = S // NC
CH = 128
NCH = T // CH
INTER = 8192
SCALE = 64.0
EPS = 1e-5
TG = 512
NG = T // TG

ENGINES = ("sync", "gpsimd", "tensor", "vector", "scalar")


def _hadamard(n):
    h = np.array([[1.0]], np.float32)
    while h.shape[0] < n:
        h = np.block([[h, h], [h, -h]])
    return h


H16S = _hadamard(16)


class Prog:

    def __init__(self, nc, block, sems):
        self.nc = nc
        self.block = block
        self.sems = sems
        self.cnt = {k: 0 for k in sems}
        self.q = {e: [] for e in ENGINES}
        self.bank_free = {}

    def add(self, eng, fn):
        self.q[eng].append(fn)

    def wait(self, eng, sem, v):
        if v <= 0:
            return
        s = self.sems[sem]
        self.q[eng].append(lambda e, s=s, v=v: e.wait_ge(s, v))

    def inc(self, eng, sem, fn, n=1):
        self.cnt[sem] += n
        s = self.sems[sem]
        self.q[eng].append(lambda e, fn=fn, s=s, n=n: fn(e).then_inc(s, n))
        return self.cnt[sem]

    def run(self, eng, fn):
        self.q[eng].append(fn)

    def dma(self, eng, sem, out, in_, **kw):
        return self.inc(eng, sem, lambda e: e.dma_start(out=out, in_=in_, **kw), 16)

    def use_bank(self, bank):
        ent = self.bank_free.get(bank)
        if ent is not None:
            self.wait("tensor", ent[0], ent[1])

    def free_bank(self, bank, sem, v):
        self.bank_free[bank] = (sem, v)

    def barrier(self, engines=ENGINES):
        snap = dict(self.cnt)
        for e in engines:
            for sname, v in snap.items():
                if v > 0:
                    self.wait(e, sname, v)

    def flush(self):
        blk = self.block
        for eng in ENGINES:
            ops = self.q[eng]
            if not ops:
                continue
            deco = getattr(blk, eng)

            def body(e, ops=ops):
                for f in ops:
                    f(e)
            deco(body)
            self.q[eng] = []


def build_nc():
    nc = bass.Bass("TRN2", target_bir_lowering=False, debug=False)

    xin = nc.dram_tensor("xin", [T, D], F32, kind="ExternalInput").ap()
    wq_t = nc.dram_tensor("wq_t", [H, 16, 128, 128], F32, kind="ExternalInput").ap()
    wk_t = nc.dram_tensor("wk_t", [H, 16, 128, 128], F32, kind="ExternalInput").ap()
    wv_t = nc.dram_tensor("wv_t", [H, 16, 128, 128], F32, kind="ExternalInput").ap()
    wg_t = nc.dram_tensor("wg_t", [16, 16, 128, 128], F32, kind="ExternalInput").ap()
    wo_t = nc.dram_tensor("wo_t", [4, 16, 128, 512], F32, kind="ExternalInput").ap()
    wup_t = nc.dram_tensor("wup_t", [32, 16, 128, 256], F32, kind="ExternalInput").ap()
    wgt_t = nc.dram_tensor("wgt_t", [32, 16, 128, 256], F32, kind="ExternalInput").ap()
    wdn_t = nc.dram_tensor("wdn_t", [4, 16, 128, 4, 512], F32, kind="ExternalInput").ap()
    bq = nc.dram_tensor("bq", [D], F32, kind="ExternalInput").ap()
    bk = nc.dram_tensor("bk", [D], F32, kind="ExternalInput").ap()
    bv = nc.dram_tensor("bv", [D], F32, kind="ExternalInput").ap()
    bg = nc.dram_tensor("bg", [D], F32, kind="ExternalInput").ap()
    bo = nc.dram_tensor("bo", [D], F32, kind="ExternalInput").ap()
    gnw = nc.dram_tensor("gnw", [D], F32, kind="ExternalInput").ap()
    gnb = nc.dram_tensor("gnb", [D], F32, kind="ExternalInput").ap()
    gam = nc.dram_tensor("gam", [H], F32, kind="ExternalInput").ap()
    su_ug = nc.dram_tensor("su_ug", [D], F32, kind="ExternalInput").ap()
    su_dp = nc.dram_tensor("su_dp", [INTER], F32, kind="ExternalInput").ap()
    ws_ug = nc.dram_tensor("ws_ug", [2 * INTER], F32, kind="ExternalInput").ap()
    ws_dp = nc.dram_tensor("ws_dp", [D], F32, kind="ExternalInput").ap()
    h128p = nc.dram_tensor("h128p", [128, 128], BF16, kind="ExternalInput").ap()
    h128n = nc.dram_tensor("h128n", [128, 128], BF16, kind="ExternalInput").ap()
    h128u = nc.dram_tensor("h128u", [128, 128], BF16, kind="ExternalInput").ap()
    emask = nc.dram_tensor("emask", [NC], F32, kind="ExternalInput").ap()
    eexp = nc.dram_tensor("eexp", [NC], F32, kind="ExternalInput").ap()
    yout = nc.dram_tensor("y", [T, D], F32, kind="ExternalOutput").ap()

    x2d = nc.dram_tensor("x2d", [T, D], BF16).ap()
    dsc = nc.dram_tensor("dsc", [256], F32).ap()
    cc_si = [nc.dram_tensor(f"cc_si{h}", [128, 128], F32) for h in range(H)]
    cc_so = [nc.dram_tensor(f"cc_so{h}", [NC * 128, 128], F32) for h in range(H)]
    cc_ti = nc.dram_tensor("cc_ti", [32], F32)
    cc_to = nc.dram_tensor("cc_to", [32], F32)

    with ExitStack() as outer:
        ent = outer.enter_context
        semnames = ["Ls", "Lg", "La", "pe", "dv", "ac", "gp", "cc"]
        sems = {n: ent(nc.semaphore(n)) for n in semnames}

        PB = [ent(nc.psum_tensor(f"pb{i}", [128, 512], F32)) for i in range(6)]
        TB = [ent(nc.psum_tensor(f"tb{i}", [128, 1024], BF16)) for i in range(2)]

        ones1 = ent(nc.sbuf_tensor("ones1", [1, 512], F32))
        onesb = ent(nc.sbuf_tensor("onesb", [128, 1], F32))
        epsc = ent(nc.sbuf_tensor("epsc", [128, 1], F32))
        nhalf = ent(nc.sbuf_tensor("nhalf", [128, 1], F32))
        i128b = ent(nc.sbuf_tensor("i128b", [128, 128], BF16))
        hp = ent(nc.sbuf_tensor("hp", [128, 128], BF16))
        hn = ent(nc.sbuf_tensor("hn", [128, 128], BF16))
        hu = ent(nc.sbuf_tensor("hu", [128, 128], BF16))
        io_jmp = ent(nc.sbuf_tensor("io_jmp", [128, 128], F32))
        io_jp1 = ent(nc.sbuf_tensor("io_jp1", [128, 128], F32))
        io_kd = ent(nc.sbuf_tensor("io_kd", [128, 1], F32))
        lng_row = ent(nc.sbuf_tensor("lng_row", [1, 16], F32))
        lng_cols = ent(nc.sbuf_tensor("lng_cols", [128, 16], F32))
        kdec_cols = ent(nc.sbuf_tensor("kdec_cols", [128, 16], F32))
        gc_cols = ent(nc.sbuf_tensor("gc_cols", [128, 16], F32))
        wrep = ent(nc.sbuf_tensor("wrep", [128, 128], F32))
        bq_c = ent(nc.sbuf_tensor("bq_c", [128, 16], F32))
        bk_c = ent(nc.sbuf_tensor("bk_c", [128, 16], F32))
        bv_c = ent(nc.sbuf_tensor("bv_c", [128, 16], F32))
        bg_c = ent(nc.sbuf_tensor("bg_c", [128, 16], F32))
        gnw_c = ent(nc.sbuf_tensor("gnw_c", [128, 16], F32))
        gnb_c = ent(nc.sbuf_tensor("gnb_c", [128, 16], F32))
        su_c = ent(nc.sbuf_tensor("su_c", [128, 16], F32))
        sup_c = ent(nc.sbuf_tensor("sup_c", [128, 64], F32))
        sgt_c = ent(nc.sbuf_tensor("sgt_c", [128, 64], F32))
        row16 = ent(nc.sbuf_tensor("row16", [1, 16], F32))
        row128 = ent(nc.sbuf_tensor("row128", [1, 128], F32))
        stat16 = ent(nc.sbuf_tensor("stat16", [16, 8], F32))
        gsc_c = ent(nc.sbuf_tensor("gsc_c", [128, 16], F32))
        gbi_c = ent(nc.sbuf_tensor("gbi_c", [128, 16], F32))
        accs = ent(nc.sbuf_tensor("accs", [128, 32], F32))
        a32 = ent(nc.sbuf_tensor("a32", [32, 4], F32))

        block = ent(nc.Block())
        P = Prog(nc, block, sems)

        v_hp = P.dma("sync", "Ls", hp[:], h128p[:, :])
        v_hn = P.dma("sync", "Ls", hn[:], h128n[:, :])
        v_hu = P.dma("sync", "Ls", hu[:], h128u[:, :])
        colpairs = [(bq_c, bq), (bk_c, bk), (bv_c, bv), (bg_c, bg),
                    (gnw_c, gnw), (gnb_c, gnb), (su_c, su_ug)]
        v_cols = 0
        for dst, src in colpairs:
            v_cols = P.dma("sync", "Ls", dst[:], src.rearrange("(c p) -> p c", p=128),
                           allow_slow_non_contiguous=True)
        v_sup = P.dma("sync", "Ls", sup_c[:], ws_ug[0:INTER].rearrange("(c p) -> p c", p=128),
                      allow_slow_non_contiguous=True)
        v_sgt = P.dma("sync", "Ls", sgt_c[:], ws_ug[INTER:2 * INTER].rearrange("(c p) -> p c", p=128),
                      allow_slow_non_contiguous=True)
        sudp_c = sup_c
        v_lngrow = P.dma("sync", "Ls", row16[0:1, :], gam.rearrange("(o f) -> o f", o=1))
        v_em = P.dma("sync", "Ls", row128[0:1, 0:NC], emask.rearrange("(o f) -> o f", o=1))
        v_ee = P.dma("sync", "Ls", row128[0:1, 8:8 + NC], eexp.rearrange("(o f) -> o f", o=1))

        grow = ent(nc.sbuf_tensor("grow", [16, 128], F32))
        gbrow = ent(nc.sbuf_tensor("gbrow", [16, 128], F32))
        srow = ent(nc.sbuf_tensor("srow", [16, 128], F32))
        brow = ent(nc.sbuf_tensor("brow", [16, 128], F32))
        v_grow = P.dma("sync", "Ls", grow[:], gnw.rearrange("(p f) -> p f", p=16))
        v_gbrow = P.dma("sync", "Ls", gbrow[:], gnb.rearrange("(p f) -> p f", p=16))
        sudp_cols = ent(nc.sbuf_tensor("sudp_cols", [128, 64], F32))
        v_sudp = P.dma("sync", "Ls", sudp_cols[:], su_dp.rearrange("(c p) -> p c", p=128),
                       allow_slow_non_contiguous=True)

        c_ones = P.inc("gpsimd", "gp", lambda e: e.memset(ones1[:], 1.0))
        c_onesb = P.inc("gpsimd", "gp", lambda e: e.memset(onesb[:], 1.0))
        c_eps = P.inc("gpsimd", "gp", lambda e: e.memset(epsc[:], EPS))
        c_nh = P.inc("gpsimd", "gp", lambda e: e.memset(nhalf[:], -0.5))
        c_iojmp = P.inc("gpsimd", "gp", lambda e: e.iota(
            io_jmp[:], pattern=[[1, 128]], base=0, channel_multiplier=-1,
            allow_small_or_imprecise_dtypes=True))
        c_iojp1 = P.inc("gpsimd", "gp", lambda e: e.iota(
            io_jp1[:], pattern=[[1, 128]], base=1, channel_multiplier=0,
            allow_small_or_imprecise_dtypes=True))
        c_iokd = P.inc("gpsimd", "gp", lambda e: e.iota(
            io_kd[:], pattern=[[1, 1]], base=127, channel_multiplier=-1,
            allow_small_or_imprecise_dtypes=True))
        idf = ent(nc.sbuf_tensor("idf", [128, 128], F32))
        c_idf0 = P.inc("gpsimd", "gp", lambda e: e.memset(idf[:], 1.0))
        c_idf = P.inc("gpsimd", "gp", lambda e: e.affine_select(
            idf[:], idf[:], pattern=[[1, 128]], compare_op=ALU.is_equal,
            fill=0.0, base=0, channel_multiplier=-1))
        P.wait("vector", "gp", c_idf)
        c_i128b = P.inc("vector", "dv", lambda e: e.tensor_copy(i128b[:], idf[:]))

        P.wait("scalar", "Ls", v_lngrow)
        c_lng = P.inc("scalar", "ac", lambda e: e.activation(lng_row[0:1, :], row16[0:1, :], AF.Ln))
        P.wait("tensor", "gp", c_ones)
        P.wait("tensor", "ac", c_lng)
        P.use_bank("pb0")
        c_mm1 = P.inc("tensor", "pe", lambda e: e.matmul(
            PB[0][:, 0:16], ones1[0:1, 0:128], lng_row[0:1, :], start=True, stop=True))
        P.wait("vector", "pe", c_mm1)
        c_lngc = P.inc("vector", "dv", lambda e: e.tensor_copy(lng_cols[:], PB[0][:, 0:16]))
        P.free_bank("pb0", "dv", c_lngc)

        P.wait("scalar", "gp", c_iokd)
        P.wait("scalar", "dv", c_lngc)
        c_kd = 0
        for h in range(H):
            c_kd = P.inc("scalar", "ac", lambda e, h=h: e.activation(
                kdec_cols[:, h:h + 1], io_kd[:], AF.Exp, scale=lng_cols[:, h:h + 1]))
        c_l128 = P.inc("vector", "dv", lambda e: e.tensor_scalar_mul(row16[0:1, :], lng_row[0:1, :], 128.0))
        P.wait("scalar", "dv", c_l128)
        c_g128 = P.inc("scalar", "ac", lambda e: e.activation(row16[0:1, :], row16[0:1, :], AF.Exp))
        P.wait("tensor", "ac", c_g128)
        P.use_bank("pb0")
        c_mm2 = P.inc("tensor", "pe", lambda e: e.matmul(
            PB[0][:, 0:16], ones1[0:1, 0:128], row16[0:1, :], start=True, stop=True))
        P.wait("vector", "pe", c_mm2)
        c_gcc = P.inc("vector", "dv", lambda e: e.tensor_copy(gc_cols[:], PB[0][:, 0:16]))
        P.free_bank("pb0", "dv", c_gcc)

        lng16 = ent(nc.sbuf_tensor("lng16", [16, 1], F32))
        P.wait("sync", "Ls", v_lngrow)
        v_g16 = P.dma("sync", "Ls", lng16[:], gam.rearrange("(p o) -> p o", o=1))
        P.wait("scalar", "Ls", v_g16)
        c_lng16 = P.inc("scalar", "ac", lambda e: e.activation(lng16[:], lng16[:], AF.Ln))
        P.wait("vector", "ac", c_lng16)
        c_lng16b = P.inc("vector", "dv", lambda e: e.tensor_scalar_mul(lng16[:], lng16[:], 1024.0))
        P.wait("tensor", "Ls", v_ee)
        P.use_bank("pb1")
        c_mm3 = P.inc("tensor", "pe", lambda e: e.matmul(
            PB[1][0:16, 0:8], ones1[0:1, 0:16], row128[0:1, 8:8 + NC], start=True, stop=True))
        c_mm3b = P.inc("tensor", "pe", lambda e: e.matmul(
            PB[1][0:16, 8:16], ones1[0:1, 0:16], row128[0:1, 0:NC], start=True, stop=True))
        P.wait("scalar", "pe", c_mm3b)
        P.wait("scalar", "dv", c_lng16b)
        c_w16 = P.inc("scalar", "ac", lambda e: e.activation(
            stat16[:, 0:8], PB[1][0:16, 0:8], AF.Exp, scale=lng16[:]))
        P.wait("vector", "ac", c_w16)
        c_w16b = P.inc("vector", "dv", lambda e: e.tensor_mul(stat16[:, 0:8], stat16[:, 0:8], PB[1][0:16, 8:16]))
        P.free_bank("pb1", "dv", c_w16b)
        P.wait("gpsimd", "dv", c_w16b)
        v_fl1 = P.dma("gpsimd", "Lg", dsc[0:128].rearrange("(p f) -> p f", p=16), stat16[:, 0:8],
                      allow_slow_non_contiguous=True)
        P.wait("gpsimd", "Lg", v_fl1)
        v_fl2 = P.dma("gpsimd", "Lg", row128[0:1, :], dsc[0:128].rearrange("(o f) -> o f", o=1))
        P.wait("tensor", "Lg", v_fl2)
        P.use_bank("pb1")
        c_mm4 = P.inc("tensor", "pe", lambda e: e.matmul(
            PB[1][:, 0:128], ones1[0:1, 0:128], row128[0:1, :], start=True, stop=True))
        P.wait("vector", "pe", c_mm4)
        c_wrep = P.inc("vector", "dv", lambda e: e.tensor_copy(wrep[:], PB[1][:, 0:128]))
        P.free_bank("pb1", "dv", c_wrep)

        P.wait("vector", "Ls", max(v_sup, v_sgt, v_sudp))
        c_sup = P.inc("vector", "dv", lambda e: e.tensor_scalar_mul(sup_c[:], sup_c[:], SCALE))
        c_sup2 = P.inc("vector", "dv", lambda e: e.tensor_mul(sup_c[:], sup_c[:], sudp_cols[:]))
        c_sgt = P.inc("vector", "dv", lambda e: e.tensor_scalar_mul(sgt_c[:], sgt_c[:], SCALE))

        P.barrier()
        P.flush()

        with ExitStack() as ph_g:
            g_fm = ph_g.enter_context(nc.sbuf_tensor("g_fm", [128, H * 1024], BF16))
            wosl = ph_g.enter_context(nc.sbuf_tensor("wosl", [128, 16 * 512], BF16))
            bo_tab = ph_g.enter_context(nc.sbuf_tensor("bo_tab", [128, 2048], BF16))
            ret = ph_g.enter_context(nc.sbuf_tensor("ret", [128, H * 1024], BF16))
            sin = ph_g.enter_context(nc.sbuf_tensor("sin", [128, H * 128], F32))

            with ExitStack() as ph_a:
                ea = ph_a.enter_context
                x_fm = ea(nc.sbuf_tensor("x_fm", [128, 16 * 1024], BF16))
                qr = ea(nc.sbuf_tensor("qr", [128, 4 * 1024], BF16))
                kr = ea(nc.sbuf_tensor("kr", [128, 4 * 1024], BF16))
                vr = ea(nc.sbuf_tensor("vr", [128, 4 * 1024], BF16))
                vfm = ea(nc.sbuf_tensor("vfm", [128, 1024], BF16))
                wsl = [ea(nc.sbuf_tensor(f"wsl{i}", [128, 2048], BF16)) for i in range(6)]
                xst = ea(nc.sbuf_tensor("xst", [128, 2048], F32))
                xbf = ea(nc.sbuf_tensor("xbf", [128, 2048], BF16))
                lamb = [ea(nc.sbuf_tensor(f"lam{i}", [128, 128], F32)) for i in range(2)]
                dtab = [ea(nc.sbuf_tensor(f"dtab{i}", [128, 128], F32)) for i in range(2)]
                qdec = [ea(nc.sbuf_tensor(f"qdec{i}", [128, 128], BF16)) for i in range(2)]
                srun = ea(nc.sbuf_tensor("srun", [128, 128], F32))
                sbf = ea(nc.sbuf_tensor("sbf", [128, 128], BF16))
                khat = ea(nc.sbuf_tensor("khat", [128, 128], BF16))
                khat2 = ea(nc.sbuf_tensor("khat2", [128, 128], BF16))
                scm = ea(nc.sbuf_tensor("scm", [128, 128], BF16))
                qdq = ea(nc.sbuf_tensor("qdq", [128, 128], BF16))
                gbuf = ea(nc.sbuf_tensor("gbuf", [128, 1024], F32))
                rsc = ea(nc.sbuf_tensor("rsc", [128, 1024], BF16))

                last_tr_prev = 0
                for tc in range(8):
                    vx = P.dma("sync", "Ls", xst[:], xin[tc * 128:(tc + 1) * 128, :])
                    P.wait("scalar", "Ls", vx)
                    P.wait("scalar", "pe", last_tr_prev)
                    c_cast = P.inc("scalar", "ac", lambda e: e.activation(xbf[:], xst[:], AF.Copy))
                    for grp in range(4):
                        tb = grp % 2
                        P.use_bank(f"tb{tb}")
                        P.wait("tensor", "ac", c_cast)
                        c_tr = 0
                        for k in range(4):
                            fc = grp * 4 + k
                            c_tr = P.inc("tensor", "pe", lambda e, fc=fc, k=k, tb=tb: e.transpose(
                                TB[tb][:, k * 128:(k + 1) * 128],
                                xbf[:, fc * 128:(fc + 1) * 128], i128b[:]))
                        P.wait("vector", "pe", c_tr)
                        dst = x_fm.ap()[:, grp * 4 * 1024 + tc * 128:]
                        dst = bass.AP(tensor=dst.tensor, offset=dst.offset,
                                      ap=[[dst.ap[0][0], 128], [1024, 4], [1, 128]])
                        c_ev = P.inc("vector", "dv", lambda e, dst=dst, tb=tb: e.tensor_copy(dst, TB[tb][:, 0:512].rearrange("p (a b) -> p a b", a=4)))
                        P.free_bank(f"tb{tb}", "dv", c_ev)
                    P.wait("sync", "ac", c_cast)
                    last_tr_prev = P.cnt["pe"]
                P.wait("tensor", "dv", P.cnt["dv"])

                LAG = 2
                fullpass_state = {}

                def emit_proj(h):
                    slot = h % 4
                    bufs = (wsl[0 + (h % 2)], wsl[2 + (h % 2)], wsl[4 + (h % 2)])
                    prev = fullpass_state.get(("wsl", h - 2))
                    if prev:
                        P.wait("gpsimd", "pe", prev)
                    vq = P.dma("gpsimd", "Lg", bufs[0].ap().rearrange("p (a b) -> p a b", a=16), wq_t[h].rearrange("a p b -> p a b"))
                    vk = P.dma("gpsimd", "Lg", bufs[1].ap().rearrange("p (a b) -> p a b", a=16), wk_t[h].rearrange("a p b -> p a b"))
                    vv = P.dma("gpsimd", "Lg", bufs[2].ap().rearrange("p (a b) -> p a b", a=16), wv_t[h].rearrange("a p b -> p a b"))
                    prev2 = fullpass_state.get(("fp", h - 4))
                    if prev2:
                        P.wait("tensor", "pe", prev2)
                    c_pe_last = 0
                    for which, (wbuf, vdone, dst, bias) in enumerate(
                            [(bufs[0], vq, qr, bq_c), (bufs[1], vk, kr, bk_c), (bufs[2], vv, vfm, bv_c)]):
                        P.wait("tensor", "Lg", vdone)
                        for th in range(2):
                            bank = f"pb{(which * 2 + th) % 2}"
                            pb = PB[(which * 2 + th) % 2]
                            P.use_bank(bank)
                            for fc in range(16):
                                c_pe_last = P.inc("tensor", "pe", lambda e, pb=pb, wbuf=wbuf, fc=fc, th=th: e.matmul(
                                    pb[:], wbuf[:, fc * 128:(fc + 1) * 128],
                                    x_fm[:, fc * 1024 + th * 512: fc * 1024 + (th + 1) * 512],
                                    start=(fc == 0), stop=(fc == 15)))
                            P.wait("scalar", "pe", c_pe_last)
                            if which < 2:
                                out_ap = dst[:, slot * 1024 + th * 512: slot * 1024 + (th + 1) * 512]
                            else:
                                out_ap = dst[:, th * 512:(th + 1) * 512]
                            c_ev = P.inc("scalar", "ac", lambda e, out_ap=out_ap, pb=pb, bias=bias, h=h: e.activation(
                                out_ap, pb[:], AF.Identity, bias=bias[:, h:h + 1]))
                            P.free_bank(bank, "ac", c_ev)
                    c_evv = 0
                    for grp in range(2):
                        tb = grp % 2
                        P.use_bank(f"tb{tb}")
                        c_tr = 0
                        for k in range(4):
                            ci = grp * 4 + k
                            P.wait("tensor", "ac", P.cnt["ac"])
                            c_tr = P.inc("tensor", "pe", lambda e, ci=ci, k=k, tb=tb: e.transpose(
                                TB[tb][:, k * 128:(k + 1) * 128],
                                vfm[:, ci * 128:(ci + 1) * 128], i128b[:]))
                        P.wait("scalar", "pe", c_tr)
                        dst = vr.ap()[:, slot * 1024 + grp * 512:]
                        dst = bass.AP(tensor=dst.tensor, offset=dst.offset,
                                      ap=[[dst.ap[0][0], 128], [128, 4], [1, 128]])
                        c_evv = P.inc("scalar", "ac", lambda e, dst=dst, tb=tb: e.copy(
                            dst, TB[tb][:, 0:512].rearrange("p (a b) -> p a b", a=4)))
                        P.free_bank(f"tb{tb}", "ac", c_evv)
                    fullpass_state[("wsl", h)] = c_pe_last
                    fullpass_state[("proj", h)] = c_evv

                def emit_passA_pre(h):
                    pv = fullpass_state.get(("vput", h - 2))
                    if pv:
                        P.wait("vector", "Lg", pv)
                    P.wait("tensor", "ac", fullpass_state[("proj", h)])

                def emit_passA_chunk(h, ci, st):
                    slot = h % 4
                    lam = lamb[h % 2]
                    P.use_bank("tb0")
                    c_tr = P.inc("tensor", "pe", lambda e, h=h, ci=ci, slot=slot: e.transpose(
                        TB[0][:, 0:128], kr[:, slot * 1024 + ci * 128: slot * 1024 + (ci + 1) * 128], i128b[:]))
                    P.wait("scalar", "pe", c_tr)
                    c_kh = P.inc("scalar", "ac", lambda e, h=h: e.activation(
                        khat[:], TB[0][:, 0:128], AF.Identity, scale=kdec_cols[:, h:h + 1]))
                    P.free_bank("tb0", "ac", c_kh)
                    P.use_bank("pb5")
                    P.wait("tensor", "ac", c_kh)
                    c_ds = P.inc("tensor", "pe", lambda e, h=h, ci=ci, slot=slot: e.matmul(
                        PB[5][:, 0:128], khat[:],
                        vr[:, slot * 1024 + ci * 128: slot * 1024 + (ci + 1) * 128],
                        start=True, stop=True))
                    P.wait("vector", "pe", c_ds)
                    if ci == 0:
                        c_su = P.inc("vector", "dv", lambda e, lam=lam: e.tensor_copy(lam[:], PB[5][:, 0:128]))
                    else:
                        c_su = P.inc("vector", "dv", lambda e, lam=lam, h=h: e.scalar_tensor_tensor(
                            lam[:], lam[:], gc_cols[:, h:h + 1], PB[5][:, 0:128], ALU.mult, ALU.add))
                    P.free_bank("pb5", "dv", c_su)
                    st["su"] = c_su

                def emit_passA_post(h, st):
                    lam = lamb[h % 2]
                    P.wait("gpsimd", "dv", st["su"])
                    vput = P.dma("gpsimd", "Lg", cc_si[h].ap(), lam[:])
                    fullpass_state[("vput", h)] = vput
                    P.wait("gpsimd", "Lg", vput)
                    c_cc = P.inc("gpsimd", "cc", lambda e, h=h: e.collective_compute(
                        "AllGather", ALU.bypass, replica_groups=[list(range(NC))],
                        ins=[cc_si[h].ap().opt()], outs=[cc_so[h].ap().opt()]))
                    fullpass_state[("cc", h)] = c_cc

                def emit_fullpass_pre(h, st):
                    dt = dtab[h % 2]
                    qd = qdec[h % 2]
                    prev = fullpass_state.get(("dt", h - 2))
                    if prev:
                        P.wait("scalar", "dv", prev)
                    c_d1 = P.inc("scalar", "ac", lambda e, dt=dt, h=h: e.activation(
                        dt[:], io_jmp[:], AF.Exp, scale=lng_cols[:, h:h + 1]))
                    P.wait("gpsimd", "ac", c_d1)
                    c_d2 = P.inc("gpsimd", "gp", lambda e, dt=dt: e.affine_select(
                        dt[:], dt[:], pattern=[[1, 128]], compare_op=ALU.is_ge,
                        fill=0.0, base=0, channel_multiplier=-1))
                    c_q1 = P.inc("scalar", "ac", lambda e, qd=qd, h=h: e.activation(
                        qd[:], io_jp1[:], AF.Exp, scale=lng_cols[:, h:h + 1]))
                    P.wait("sync", "cc", fullpass_state[("cc", h)])
                    P.wait("sync", "dv", fullpass_state.get(("si", h - 1), 0))
                    vgb = P.dma("sync", "Ls", gbuf.ap().rearrange("p (c dv) -> p c dv", c=NC),
                                cc_so[h].ap().rearrange("(c dk) dv -> dk c dv", dk=128))
                    c_z = P.inc("vector", "dv", lambda e, h=h: e.memset(sin[:, h * 128:(h + 1) * 128], 0.0))
                    P.wait("vector", "Ls", vgb)
                    c_si = c_z
                    for cp in range(NC):
                        c_si = P.inc("vector", "dv", lambda e, h=h, cp=cp: e.scalar_tensor_tensor(
                            sin[:, h * 128:(h + 1) * 128], gbuf[:, cp * 128:(cp + 1) * 128],
                            wrep[:, h * 8 + cp: h * 8 + cp + 1], sin[:, h * 128:(h + 1) * 128],
                            ALU.mult, ALU.add))
                    fullpass_state[("si", h)] = c_si
                    c_sr = P.inc("vector", "dv", lambda e, h=h: e.tensor_copy(srun[:], sin[:, h * 128:(h + 1) * 128]))
                    P.wait("scalar", "dv", c_sr)
                    P.wait("scalar", "pe", fullpass_state.get(("fp", h - 1), 0))
                    c_sb = P.inc("scalar", "ac", lambda e: e.activation(sbf[:], srun[:], AF.Copy))
                    st.update(sb=c_sb, d2=c_d2, q1=c_q1, last_pe=0)

                def emit_fullpass_chunk(h, ci, st):
                    slot = h % 4
                    dt = dtab[h % 2]
                    qd = qdec[h % 2]
                    qsl = qr[:, slot * 1024 + ci * 128: slot * 1024 + (ci + 1) * 128]
                    ksl = kr[:, slot * 1024 + ci * 128: slot * 1024 + (ci + 1) * 128]
                    vsl = vr[:, slot * 1024 + ci * 128: slot * 1024 + (ci + 1) * 128]
                    P.use_bank("pb2")
                    if ci == 0:
                        P.wait("tensor", "gp", st["d2"])
                    c_s = P.inc("tensor", "pe", lambda e, ksl=ksl, qsl=qsl: e.matmul(
                        PB[2][:, 0:128], ksl, qsl, start=True, stop=True))
                    P.wait("vector", "pe", c_s)
                    c_scm = P.inc("vector", "dv", lambda e, dt=dt: e.tensor_mul(scm[:], PB[2][:, 0:128], dt[:]))
                    P.free_bank("pb2", "dv", c_scm)
                    if ci == 0:
                        P.wait("vector", "ac", st["q1"])
                    c_qd = P.inc("vector", "dv", lambda e, qsl=qsl, qd=qd: e.tensor_mul(qdq[:], qsl, qd[:]))
                    P.use_bank("pb3")
                    P.wait("tensor", "dv", c_qd)
                    P.wait("tensor", "ac", st["sb"])
                    P.inc("tensor", "pe", lambda e, vsl=vsl: e.matmul(
                        PB[3][:, 0:128], scm[:], vsl, start=True, stop=False))
                    c_r = P.inc("tensor", "pe", lambda e: e.matmul(
                        PB[3][:, 0:128], qdq[:], sbf[:], start=False, stop=True))
                    P.wait("scalar", "pe", c_r)
                    c_rv = P.inc("scalar", "ac", lambda e, h=h, ci=ci: e.copy(
                        ret[:, h * 1024 + ci * 128: h * 1024 + (ci + 1) * 128], PB[3][:, 0:128]))
                    P.free_bank("pb3", "ac", c_rv)
                    if ci < NCH - 1:
                        P.use_bank("tb1")
                        c_tr = P.inc("tensor", "pe", lambda e, ksl=ksl: e.transpose(
                            TB[1][:, 0:128], ksl, i128b[:]))
                        P.wait("scalar", "pe", c_tr)
                        c_kh = P.inc("scalar", "ac", lambda e, h=h: e.activation(
                            khat2[:], TB[1][:, 0:128], AF.Identity, scale=kdec_cols[:, h:h + 1]))
                        P.free_bank("tb1", "ac", c_kh)
                        P.use_bank("pb4")
                        P.wait("tensor", "ac", c_kh)
                        c_ds = P.inc("tensor", "pe", lambda e, vsl=vsl: e.matmul(
                            PB[4][:, 0:128], khat2[:], vsl, start=True, stop=True))
                        P.wait("vector", "pe", c_ds)
                        c_su2 = P.inc("vector", "dv", lambda e, h=h: e.scalar_tensor_tensor(
                            srun[:], srun[:], gc_cols[:, h:h + 1], PB[4][:, 0:128], ALU.mult, ALU.add))
                        P.free_bank("pb4", "dv", c_su2)
                        P.wait("scalar", "dv", c_su2)
                        P.wait("scalar", "pe", c_r)
                        st["sb"] = P.inc("scalar", "ac", lambda e: e.activation(sbf[:], srun[:], AF.Copy))
                        st["last_pe"] = max(st["last_pe"], c_ds)
                    st["last_pe"] = max(st["last_pe"], c_r)

                def emit_fullpass_post(h, st):
                    P.wait("vector", "ac", P.cnt["ac"])
                    c_st1 = P.inc("vector", "dv", lambda e, h=h: e.tensor_scalar(
                        rsc[:], ret[:, h * 1024:(h + 1) * 1024], 0.0, None, ALU.add, ALU.add,
                        accum_out=accs[:, h:h + 1]))
                    c_st2 = P.inc("vector", "dv", lambda e, h=h: e.scalar_tensor_tensor(
                        rsc[:], ret[:, h * 1024:(h + 1) * 1024], 1.0,
                        ret[:, h * 1024:(h + 1) * 1024], ALU.mult, ALU.mult,
                        accum_out=accs[:, 16 + h:17 + h]))
                    fullpass_state[("dt", h)] = c_st2
                    fullpass_state[("fp", h)] = st["last_pe"]

                for h in range(H):
                    emit_proj(h)
                    fp = h - LAG if h >= LAG else None
                    fst = {}
                    if fp is not None:
                        emit_fullpass_pre(fp, fst)
                    emit_passA_pre(h)
                    ast = {}
                    for ci in range(NCH):
                        emit_passA_chunk(h, ci, ast)
                        if fp is not None:
                            emit_fullpass_chunk(fp, ci, fst)
                    emit_passA_post(h, ast)
                    if fp is not None:
                        emit_fullpass_post(fp, fst)
                for h in range(H - LAG, H):
                    fst = {}
                    emit_fullpass_pre(h, fst)
                    for ci in range(NCH):
                        emit_fullpass_chunk(h, ci, fst)
                    emit_fullpass_post(h, fst)

                for oc in range(16):
                    wbuf = wsl[oc % 2]
                    if oc < 2:
                        P.wait("gpsimd", "pe", fullpass_state.get(("wsl", 14 + oc), 0))
                    else:
                        P.wait("gpsimd", "pe", fullpass_state.get(("gw", oc - 2), 0))
                    vw = P.dma("gpsimd", "Lg", wbuf.ap().rearrange("p (a b) -> p a b", a=16), wg_t[oc].rearrange("a p b -> p a b"))
                    P.wait("tensor", "Lg", vw)
                    c_pe = 0
                    for th in range(2):
                        bank = f"pb{th}"
                        pb = PB[th]
                        P.use_bank(bank)
                        for fc in range(16):
                            c_pe = P.inc("tensor", "pe", lambda e, pb=pb, wbuf=wbuf, fc=fc, th=th: e.matmul(
                                pb[:], wbuf[:, fc * 128:(fc + 1) * 128],
                                x_fm[:, fc * 1024 + th * 512: fc * 1024 + (th + 1) * 512],
                                start=(fc == 0), stop=(fc == 15)))
                        P.wait("scalar", "pe", c_pe)
                        c_g = P.inc("scalar", "ac", lambda e, oc=oc, th=th, pb=pb: e.activation(
                            g_fm[:, oc * 1024 + th * 512: oc * 1024 + (th + 1) * 512],
                            pb[:], AF.Silu, bias=bg_c[:, oc:oc + 1]))
                        P.free_bank(bank, "ac", c_g)
                    fullpass_state[("gw", oc)] = c_pe

                P.barrier(("tensor", "vector", "scalar", "gpsimd"))
                P.use_bank("pb5")
                c_tr = P.inc("tensor", "pe", lambda e: e.transpose(
                    PB[5][0:32, 0:128], accs[:], idf[:]))
                P.wait("vector", "pe", c_tr)
                c_red = P.inc("vector", "dv", lambda e: e.tensor_reduce(
                    a32[:, 0:1], PB[5][0:32, 0:128], mybir.AxisListType.X, ALU.add))
                P.free_bank("pb5", "dv", c_red)
                P.wait("gpsimd", "dv", c_red)
                v_st = P.dma("gpsimd", "Lg", cc_ti.ap().rearrange("(p o) -> p o", o=1), a32[:, 0:1])
                P.wait("gpsimd", "Lg", v_st)
                c_ccs = P.inc("gpsimd", "cc", lambda e: e.collective_compute(
                    "AllReduce", ALU.add, replica_groups=[list(range(NC))],
                    ins=[cc_ti.ap().opt()], outs=[cc_to.ap().opt()]))
                P.wait("sync", "cc", c_ccs)
                v_sb = P.dma("sync", "Ls", stat16[:, 0:2], cc_to.ap().rearrange("(b a) -> a b", b=2),
                             allow_slow_non_contiguous=True)
                NSTAT = float(S * DK)
                P.wait("vector", "Ls", v_sb)
                c_m = P.inc("vector", "dv", lambda e: e.tensor_scalar_mul(stat16[:, 0:2], stat16[:, 0:2], 1.0 / NSTAT))
                P.wait("scalar", "dv", c_m)
                c_m2 = P.inc("scalar", "ac", lambda e: e.activation(stat16[:, 2:3], stat16[:, 0:1], AF.Square))
                P.wait("vector", "ac", c_m2)
                c_v = P.inc("vector", "dv", lambda e: e.tensor_sub(stat16[:, 1:2], stat16[:, 1:2], stat16[:, 2:3]))
                P.wait("scalar", "dv", c_v)
                c_sq = P.inc("scalar", "ac", lambda e: e.activation(stat16[:, 1:2], stat16[:, 1:2], AF.Ln, bias=epsc[0:16, :]))
                c_rc = P.inc("scalar", "ac", lambda e: e.activation(stat16[:, 1:2], stat16[:, 1:2], AF.Exp, scale=nhalf[0:16, :]))
                P.wait("vector", "ac", c_rc)
                c_sr2 = P.inc("vector", "dv", lambda e: e.tensor_scalar_mul(srow[:], grow[:], stat16[:, 1:2]))
                c_br1 = P.inc("vector", "dv", lambda e: e.tensor_scalar_mul(brow[:], srow[:], stat16[:, 0:1]))
                c_br2 = P.inc("vector", "dv", lambda e: e.tensor_sub(brow[:], gbrow[:], brow[:]))
                P.use_bank("pb5")
                P.wait("tensor", "dv", c_br2)
                c_t1x = P.inc("tensor", "pe", lambda e: e.transpose(
                    PB[5][:, 0:16], srow[:], idf[0:16, 0:16]))
                c_t2x = P.inc("tensor", "pe", lambda e: e.transpose(
                    PB[5][:, 16:32], brow[:], idf[0:16, 0:16]))
                P.wait("vector", "pe", c_t2x)
                c_gs = P.inc("vector", "dv", lambda e: e.tensor_copy(gsc_c[:], PB[5][:, 0:16]))
                c_gb2 = P.inc("vector", "dv", lambda e: e.tensor_copy(gbi_c[:], PB[5][:, 16:32]))
                P.free_bank("pb5", "dv", c_gb2)
                bo_row = xst
                v_borow = P.dma("sync", "Ls", bo_row[0:1, 0:2048], bo.rearrange("(o f) -> o f", o=1))
                v_wo0 = P.dma("gpsimd", "Lg", wosl.ap().rearrange("p (a b) -> p a b", a=16),
                              wo_t[0].rearrange("a p b -> p a b"))
                P.wait("tensor", "Ls", v_borow)
                for gg in range(4):
                    P.use_bank("pb5")
                    c_bo = P.inc("tensor", "pe", lambda e, gg=gg: e.matmul(
                        PB[5][:, 0:512], ones1[0:1, 0:128], bo_row[0:1, gg * 512:(gg + 1) * 512],
                        start=True, stop=True))
                    P.wait("vector", "pe", c_bo)
                    c_ev = P.inc("vector", "dv", lambda e, gg=gg: e.tensor_copy(
                        bo_tab[:, gg * 512:(gg + 1) * 512], PB[5][:, 0:512]))
                    P.free_bank("pb5", "dv", c_ev)

                P.barrier(("tensor", "vector", "scalar"))
                for h in range(H):
                    for ci in range(NCH):
                        tb = (h * NCH + ci) % 2
                        P.use_bank(f"tb{tb}")
                        c_tr = P.inc("tensor", "pe", lambda e, h=h, ci=ci, tb=tb: e.transpose(
                            TB[tb][:, 0:128], ret[:, h * 1024 + ci * 128: h * 1024 + (ci + 1) * 128], i128b[:]))
                        P.wait("vector", "pe", c_tr)
                        c_t1 = P.inc("vector", "dv", lambda e, tb=tb, h=h: e.tensor_scalar(
                            scm[:], TB[tb][:, 0:128], gsc_c[:, h:h + 1], gbi_c[:, h:h + 1], ALU.mult, ALU.add))
                        P.free_bank(f"tb{tb}", "dv", c_t1)
                        gsl = g_fm[:, h * 1024 + ci * 128: h * 1024 + (ci + 1) * 128]
                        c_t2 = P.inc("vector", "dv", lambda e, gsl=gsl: e.tensor_mul(gsl, gsl, scm[:]))

                P.barrier()
                P.flush()

            with ExitStack() as ph_wo:
                ew = ph_wo.enter_context
                x2pre = ew(nc.sbuf_tensor("x2pre", [128, 8 * 2048], F32))
                xrel = ew(nc.sbuf_tensor("xrel", [128, 2048], F32))
                x2s = ew(nc.sbuf_tensor("x2s", [128, 2048], BF16))
                lnst = ew(nc.sbuf_tensor("lnst", [128, 4], F32))
                lnsc = ew(nc.sbuf_tensor("lnsc", [128, 2048], F32))

                xrel_done = {}
                prev_og_pe = 0
                for og in range(4):
                    if og > 0:
                        P.wait("gpsimd", "pe", prev_og_pe)
                        vw = P.dma("gpsimd", "Lg", wosl.ap().rearrange("p (a b) -> p a b", a=16), wo_t[og].rearrange("a p b -> p a b"))
                    else:
                        vw = v_wo0
                    P.wait("tensor", "Lg", vw)
                    for tc in range(8):
                        if og == 0:
                            pass
                        bank = f"pb{tc % 4}"
                        pb = PB[tc % 4]
                        P.use_bank(bank)
                        c_pe = 0
                        for fc in range(16):
                            c_pe = P.inc("tensor", "pe", lambda e, pb=pb, fc=fc, tc=tc: e.matmul(
                                pb[:], g_fm[:, fc * 1024 + tc * 128: fc * 1024 + (tc + 1) * 128],
                                wosl[:, fc * 512:(fc + 1) * 512],
                                start=(fc == 0), stop=(fc == 15)))
                        P.wait("vector", "pe", c_pe)
                        dst = x2pre[:, tc * 2048 + og * 512: tc * 2048 + (og + 1) * 512]
                        c_e1 = P.inc("vector", "dv", lambda e, dst=dst, pb=pb, og=og: e.tensor_add(
                            dst, pb[:], bo_tab[:, og * 512:(og + 1) * 512]))
                        P.free_bank(bank, "dv", c_e1)
                    prev_og_pe = P.cnt["pe"]

                vx2st = 0
                prev_ca = 0
                for tc in range(8):
                    P.wait("scalar", "dv", prev_ca)
                    vx = P.dma("scalar", "La", xrel[:], xin[tc * 128:(tc + 1) * 128, :])
                    P.wait("vector", "La", vx)
                    sl = x2pre[:, tc * 2048:(tc + 1) * 2048]
                    c_a = P.inc("vector", "dv", lambda e, sl=sl: e.tensor_add(sl, sl, xrel[:]))
                    prev_ca = c_a
                    P.wait("vector", "Ls", vx2st)
                    c_s1 = P.inc("vector", "dv", lambda e, sl=sl: e.tensor_scalar(
                        lnsc[:], sl, 0.0, None, ALU.add, ALU.add, accum_out=lnst[:, 0:1]))
                    c_s2 = P.inc("vector", "dv", lambda e, sl=sl: e.scalar_tensor_tensor(
                        lnsc[:], sl, 1.0, sl, ALU.mult, ALU.mult, accum_out=lnst[:, 1:2]))
                    P.wait("scalar", "dv", c_s2)
                    c_ms = P.inc("scalar", "ac", lambda e: e.mul(lnst[:, 0:1], lnst[:, 0:1], 1.0 / D))
                    c_qs = P.inc("scalar", "ac", lambda e: e.mul(lnst[:, 1:2], lnst[:, 1:2], 1.0 / D))
                    c_m2 = P.inc("scalar", "ac", lambda e: e.activation(lnst[:, 2:3], lnst[:, 0:1], AF.Square))
                    P.wait("vector", "ac", c_m2)
                    c_v = P.inc("vector", "dv", lambda e: e.tensor_sub(lnst[:, 1:2], lnst[:, 1:2], lnst[:, 2:3]))
                    P.wait("scalar", "dv", c_v)
                    c_sq = P.inc("scalar", "ac", lambda e: e.activation(lnst[:, 1:2], lnst[:, 1:2], AF.Ln, bias=epsc[:]))
                    c_rc = P.inc("scalar", "ac", lambda e: e.activation(lnst[:, 1:2], lnst[:, 1:2], AF.Exp, scale=nhalf[:]))
                    P.wait("vector", "ac", c_rc)
                    c_ap = P.inc("vector", "dv", lambda e, sl=sl: e.tensor_scalar(
                        x2s[:], sl, lnst[:, 0:1], lnst[:, 1:2], ALU.subtract, ALU.mult))
                    P.wait("sync", "dv", c_ap)
                    vx2st = P.dma("sync", "Ls", x2d[tc * 128:(tc + 1) * 128, :], x2s[:])
                    P.wait("scalar", "Ls", vx2st)

                P.barrier()
                P.flush()

        with ExitStack() as ph_f:
            ef = ph_f.enter_context
            x3pre = ef(nc.sbuf_tensor("x3pre", [128, 8 * 2048], BF16))
            sdp_row = ef(nc.sbuf_tensor("sdp_row", [1, 2048], F32))
            abuf = ef(nc.sbuf_tensor("abuf", [128, 64 * TG], BF16))
            sdp_tab = ef(nc.sbuf_tensor("sdp_tab", [128, 2048], F32))
            silu_sb = ef(nc.sbuf_tensor("silu_sb", [128, 512], BF16))
            lnst2 = ef(nc.sbuf_tensor("lnst2", [128, 4], F32))
            lnsc2 = ef(nc.sbuf_tensor("lnsc2", [128, 2048], F32))
            x2g = ef(nc.sbuf_tensor("x2g", [128, 4 * 2048], BF16))

            v_sdprow = P.dma("sync", "Ls", sdp_row[0:1, :], ws_dp.rearrange("(o f) -> o f", o=1))
            P.wait("tensor", "Ls", v_sdprow)
            for gg in range(4):
                P.use_bank("pb5")
                c_mm = P.inc("tensor", "pe", lambda e, gg=gg: e.matmul(
                    PB[5][:, 0:512], ones1[0:1, 0:128], sdp_row[0:1, gg * 512:(gg + 1) * 512],
                    start=True, stop=True))
                P.wait("vector", "pe", c_mm)
                c_ev = P.inc("vector", "dv", lambda e, gg=gg: e.tensor_scalar_mul(
                    sdp_tab[:, gg * 512:(gg + 1) * 512], PB[5][:, 0:512], SCALE))
                P.free_bank("pb5", "dv", c_ev)
            P.flush()

            for g in range(NG):
                with ExitStack() as ph_g2:
                    eg = ph_g2.enter_context
                    h1 = eg(nc.sbuf_tensor(f"h1_{g}", [128, 16 * TG], BF16))
                    wdsl = [eg(nc.sbuf_tensor(f"wd{g}_{i}", [128, 4 * 512], BF16)) for i in range(2)]

                    with ExitStack() as ph_xt:
                        xt = ph_xt.enter_context(nc.sbuf_tensor(f"xt_{g}", [128, 16 * TG], BF16))
                        for tcl in range(TG // 128):
                            tc = g * (TG // 128) + tcl
                            vx2 = P.dma("sync", "Ls", x2g[:, tcl * 2048:(tcl + 1) * 2048],
                                        x2d[tc * 128:(tc + 1) * 128, :])
                            P.wait("tensor", "Ls", vx2)
                            for grp in range(4):
                                tb = grp % 2
                                P.use_bank(f"tb{tb}")
                                c_tr = 0
                                for k in range(4):
                                    fc = grp * 4 + k
                                    c_tr = P.inc("tensor", "pe", lambda e, tcl=tcl, fc=fc, k=k, tb=tb: e.transpose(
                                        TB[tb][:, k * 128:(k + 1) * 128],
                                        x2g[:, tcl * 2048 + fc * 128: tcl * 2048 + (fc + 1) * 128], i128b[:]))
                                P.wait("vector", "pe", c_tr)
                                dst = xt.ap()[:, grp * 4 * TG + tcl * 128:]
                                dst = bass.AP(tensor=dst.tensor, offset=dst.offset,
                                              ap=[[dst.ap[0][0], 128], [TG, 4], [1, 128]])
                                base = P.cnt["dv"]
                                for k in range(4):
                                    fc = grp * 4 + k
                                    c_ev = P.inc("vector", "dv", lambda e, xt=xt, fc=fc, tcl=tcl, k=k, tb=tb: e.tensor_scalar_mul(
                                        xt[:, fc * TG + tcl * 128: fc * TG + (tcl + 1) * 128],
                                        TB[tb][:, k * 128:(k + 1) * 128],
                                        su_c[:, fc:fc + 1]))
                                P.free_bank(f"tb{tb}", "dv", c_ev)
                        for oc in range(16):
                            for th in range(TG // 512):
                                bank = f"pb{oc % 4}"
                                pb = PB[oc % 4]
                                P.use_bank(bank)
                                c_pe = 0
                                for ic in range(16):
                                    hh = hp if H16S[ic, oc] > 0 else hn
                                    c_pe = P.inc("tensor", "pe", lambda e, pb=pb, hh=hh, ic=ic, th=th: e.matmul(
                                        pb[:], hh[:], xt[:, ic * TG + th * 512: ic * TG + (th + 1) * 512],
                                        start=(ic == 0), stop=(ic == 15)))
                                P.wait("vector", "pe", c_pe)
                                c_ev = P.inc("vector", "dv", lambda e, oc=oc, th=th, pb=pb: e.tensor_copy(
                                    h1[:, oc * TG + th * 512: oc * TG + (th + 1) * 512], pb[:]))
                                P.free_bank(bank, "dv", c_ev)
                        P.barrier(("tensor", "vector", "scalar", "gpsimd", "sync"))
                        P.flush()

                    with ExitStack() as ph_ug:
                        wupsl = [ph_ug.enter_context(nc.sbuf_tensor(f"wup{g}_{i}", [128, 16 * 256], BF16)) for i in range(2)]
                        wgtsl = [ph_ug.enter_context(nc.sbuf_tensor(f"wgt{g}_{i}", [128, 16 * 256], BF16)) for i in range(2)]
                        cons = {}
                        prev_sil = 0
                        for icp in range(32):
                            bidx = icp % 2
                            prevc = cons.get(icp - 2)
                            if prevc:
                                P.wait("gpsimd", "pe", prevc)
                            vu = P.dma("gpsimd", "Lg", wupsl[bidx].ap().rearrange("p (a b) -> p a b", a=16), wup_t[icp].rearrange("a p b -> p a b"))
                            vg = P.dma("gpsimd", "Lg", wgtsl[bidx].ap().rearrange("p (a b) -> p a b", a=16), wgt_t[icp].rearrange("a p b -> p a b"))
                            P.wait("tensor", "Lg", vg)
                            c_pe = 0
                            for ih in range(2):
                                ic = icp * 2 + ih
                                pu = PB[0 + (ic % 2) * 2]
                                pg = PB[1 + (ic % 2) * 2]
                                bu = f"pb{0 + (ic % 2) * 2}"
                                bgk = f"pb{1 + (ic % 2) * 2}"
                                P.use_bank(bu)
                                for fc in range(16):
                                    P.inc("tensor", "pe", lambda e, pu=pu, w=wupsl[bidx], fc=fc, ih=ih: e.matmul(
                                        pu[:], w[:, fc * 256 + ih * 128: fc * 256 + (ih + 1) * 128],
                                        h1[:, fc * TG: fc * TG + 512],
                                        start=(fc == 0), stop=(fc == 15)))
                                P.use_bank(bgk)
                                for fc in range(16):
                                    c_pe = P.inc("tensor", "pe", lambda e, pg=pg, w=wgtsl[bidx], fc=fc, ih=ih: e.matmul(
                                        pg[:], w[:, fc * 256 + ih * 128: fc * 256 + (ih + 1) * 128],
                                        h1[:, fc * TG: fc * TG + 512],
                                        start=(fc == 0), stop=(fc == 15)))
                                P.wait("scalar", "pe", c_pe)
                                P.wait("scalar", "dv", prev_sil)
                                c_si = P.inc("scalar", "ac", lambda e, pg=pg, ic=ic: e.activation(
                                    silu_sb[:], pg[:], AF.Silu, scale=sgt_c[:, ic:ic + 1]))
                                P.free_bank(bgk, "ac", c_si)
                                P.wait("vector", "ac", c_si)
                                c_a = P.inc("vector", "dv", lambda e, pu=pu, ic=ic: e.scalar_tensor_tensor(
                                    abuf[:, ic * 512:(ic + 1) * 512], pu[:], sup_c[:, ic:ic + 1],
                                    silu_sb[:], ALU.mult, ALU.mult))
                                P.free_bank(bu, "dv", c_a)
                                prev_sil = c_a
                            cons[icp] = c_pe
                        P.barrier(("gpsimd", "tensor", "vector", "scalar"))
                        P.flush()

                    dn_prefetch = {}
                    for icg0 in range(2):
                        v = P.dma("gpsimd", "Lg", wdsl[icg0].ap().rearrange("p (a b) -> p a b", a=4),
                                  wdn_t[0, icg0])
                        dn_prefetch[(0, icg0)] = v
                    with ExitStack() as ph_bf:
                        pong = ph_bf.enter_context(nc.sbuf_tensor(f"pong{g}", [128, 64 * 128], BF16))
                        for qt in range(TG // 128):
                            src_is_a = True
                            c_last = P.cnt["dv"]
                            for hstep in (32, 16, 8, 4, 2, 1):
                                npair = 64 // (2 * hstep)

                                def mk(eng_fn, dstbuf, srcbuf, qt=qt, hstep=hstep, npair=npair, off_s=0, off_d=0):
                                    pass
                                if src_is_a:
                                    def s_ap(cc0, qt=qt):
                                        return None
                                def apv(buf, start_c, qt=qt, hstep=hstep, npair=npair, width=None):
                                    base_ap = buf.ap()
                                    stride = TG if buf is abuf else 128
                                    off = base_ap.offset + start_c * stride + qt * 128 * (1 if buf is abuf else 0)
                                    if buf is abuf:
                                        off = base_ap.offset + start_c * TG + qt * 128
                                        return bass.AP(tensor=base_ap.tensor, offset=off,
                                                       ap=[[base_ap.ap[0][0], 128], [2 * hstep * TG, npair], [TG, hstep], [1, 128]])
                                    else:
                                        off = base_ap.offset + start_c * 128
                                        return bass.AP(tensor=base_ap.tensor, offset=off,
                                                       ap=[[base_ap.ap[0][0], 128], [2 * hstep * 128, npair], [128, hstep], [1, 128]])
                                src = abuf if src_is_a else pong
                                dstb = pong if src_is_a else abuf
                                a0 = apv(src, 0)
                                a1 = apv(src, hstep)
                                d0 = apv(dstb, 0)
                                d1 = apv(dstb, hstep)
                                c1 = P.inc("vector", "dv", lambda e, d0=d0, a0=a0, a1=a1: e.tensor_add(d0, a0, a1))
                                c_last = P.inc("vector", "dv", lambda e, d1=d1, a0=a0, a1=a1: e.tensor_sub(d1, a0, a1))
                                src_is_a = not src_is_a
                        P.barrier(("vector", "tensor"))
                        P.flush()

                    for c in range(64):
                        bank = f"pb{c % 4}"
                        pb = PB[c % 4]
                        P.use_bank(bank)
                        c_pe = P.inc("tensor", "pe", lambda e, pb=pb, c=c: e.matmul(
                            pb[:], hu[:], abuf[:, c * 512:(c + 1) * 512], start=True, stop=True))
                        P.wait("vector", "pe", c_pe)
                        c_ev = P.inc("vector", "dv", lambda e, pb=pb, c=c: e.tensor_copy(
                            abuf[:, c * 512:(c + 1) * 512], pb[:]))
                        P.free_bank(bank, "dv", c_ev)

                    with ExitStack() as ph_dn:
                        P.barrier(("tensor", "vector", "gpsimd"))
                        wdsl_hist = []
                        for og in range(4):
                            consumed = {}
                            for icg in range(16):
                                bidx = icg % 2
                                if (og, icg) in dn_prefetch:
                                    vw = dn_prefetch[(og, icg)]
                                else:
                                    if len(wdsl_hist) >= 2:
                                        P.wait("gpsimd", "pe", wdsl_hist[-2])
                                    vw = P.dma("gpsimd", "Lg", wdsl[bidx].ap().rearrange("p (a b) -> p a b", a=4),
                                               wdn_t[og, icg])
                                P.wait("tensor", "Lg", vw)
                                c_pe = 0
                                for ics in range(4):
                                    ic = icg * 4 + ics
                                    for tcl in range(4):
                                        pb = PB[tcl]
                                        if icg == 0 and ics == 0:
                                            P.use_bank(f"pb{tcl}")
                                        c_pe = P.inc("tensor", "pe", lambda e, pb=pb, bidx=bidx, ics=ics, ic=ic, tcl=tcl: e.matmul(
                                            pb[:], abuf[:, ic * 512 + tcl * 128: ic * 512 + (tcl + 1) * 128],
                                            wdsl[bidx][:, ics * 512:(ics + 1) * 512],
                                            start=(ic == 0), stop=(ic == 63)))
                                consumed[icg] = c_pe
                                wdsl_hist.append(c_pe)
                            P.wait("vector", "pe", P.cnt["pe"])
                            for tcl in range(4):
                                pb = PB[tcl]
                                tc = g * 4 + tcl
                                dst = x3pre[:, tc * 2048 + og * 512: tc * 2048 + (og + 1) * 512]
                                c_ev = P.inc("vector", "dv", lambda e, dst=dst, pb=pb, og=og: e.tensor_mul(
                                    dst, pb[:], sdp_tab[:, og * 512:(og + 1) * 512]))
                                P.free_bank(f"pb{tcl}", "dv", c_ev)
                        for tcl in range(4):
                            tc = g * 4 + tcl
                            sl = x3pre[:, tc * 2048:(tc + 1) * 2048]
                            c_a = P.inc("vector", "dv", lambda e, sl=sl, tcl=tcl: e.tensor_add(
                                lnsc2[:], sl, x2g[:, tcl * 2048:(tcl + 1) * 2048]))
                            c_s1 = P.inc("vector", "dv", lambda e, sl=sl: e.tensor_scalar(
                                sl, lnsc2[:], 0.0, None, ALU.add, ALU.add, accum_out=lnst2[:, 0:1]))
                            c_s2 = P.inc("vector", "dv", lambda e, sl=sl: e.scalar_tensor_tensor(
                                sl, lnsc2[:], 1.0, lnsc2[:], ALU.mult, ALU.mult, accum_out=lnst2[:, 1:2]))
                            P.wait("scalar", "dv", c_s2)
                            c_ms = P.inc("scalar", "ac", lambda e: e.mul(lnst2[:, 0:1], lnst2[:, 0:1], 1.0 / D))
                            c_qs = P.inc("scalar", "ac", lambda e: e.mul(lnst2[:, 1:2], lnst2[:, 1:2], 1.0 / D))
                            c_m2 = P.inc("scalar", "ac", lambda e: e.activation(lnst2[:, 2:3], lnst2[:, 0:1], AF.Square))
                            P.wait("vector", "ac", c_m2)
                            c_v = P.inc("vector", "dv", lambda e: e.tensor_sub(lnst2[:, 1:2], lnst2[:, 1:2], lnst2[:, 2:3]))
                            P.wait("scalar", "dv", c_v)
                            c_sq = P.inc("scalar", "ac", lambda e: e.activation(lnst2[:, 1:2], lnst2[:, 1:2], AF.Ln, bias=epsc[:]))
                            c_rc = P.inc("scalar", "ac", lambda e: e.activation(lnst2[:, 1:2], lnst2[:, 1:2], AF.Exp, scale=nhalf[:]))
                            P.wait("vector", "ac", c_rc)
                            c_ap = P.inc("vector", "dv", lambda e: e.tensor_scalar(
                                lnsc2[:], lnsc2[:], lnst2[:, 0:1], lnst2[:, 1:2], ALU.subtract, ALU.mult))
                            P.wait("scalar", "dv", c_ap)
                            vo = P.dma("scalar", "La", yout[tc * 128:(tc + 1) * 128, :], lnsc2[:])
                            P.wait("vector", "La", vo)
                        P.barrier()
                        P.flush()

        P.wait("sync", "La", P.cnt["La"])
        P.flush()

    return nc


_NC_CACHE = None


def kernel(**inputs):
    global _NC_CACHE
    ins = {k: np.asarray(v) for k, v in inputs.items()}
    x = ins["L0_attn_x"].reshape(S, D)

    def colmajor_ok(a):
        return np.ascontiguousarray(a, dtype=np.float32)

    wq = ins["L0_attn_wq"]; wk = ins["L0_attn_wk"]; wv = ins["L0_attn_wv"]
    wg = ins["L0_attn_wg"]; wo = ins["L0_attn_wo"]

    def tile_qkv(w):
        return np.ascontiguousarray(w.reshape(16, 128, 16, 128).transpose(2, 0, 1, 3))

    wq_t = tile_qkv(wq); wk_t = tile_qkv(wk); wv_t = tile_qkv(wv); wg_t = tile_qkv(wg)
    wo_t = np.ascontiguousarray(wo.reshape(16, 128, 4, 512).transpose(2, 0, 1, 3))
    wup_t = np.ascontiguousarray(ins["L0_ffn_W_up"].T.reshape(16, 128, 32, 256).transpose(2, 0, 1, 3))
    wgt_t = np.ascontiguousarray(ins["L0_ffn_W_gate"].T.reshape(16, 128, 32, 256).transpose(2, 0, 1, 3))
    wdn_t = np.ascontiguousarray(ins["L0_ffn_W_down"].T.reshape(16, 4, 128, 4, 512).transpose(3, 0, 2, 1, 4))

    H128 = _hadamard(128)
    h128p = (H128 / (SCALE * np.sqrt(np.float32(D)))).astype(BF)
    h128n = (-H128 / (SCALE * np.sqrt(np.float32(D)))).astype(BF)
    h128u = (H128 / (SCALE * np.sqrt(np.float32(INTER)))).astype(BF)

    if _NC_CACHE is None:
        _NC_CACHE = build_nc()
    nc = _NC_CACHE

    in_maps = []
    for c in range(NC):
        emask = (np.arange(NC) < c).astype(np.float32)
        eexp = np.maximum(c - 1 - np.arange(NC), 0).astype(np.float32)
        in_maps.append({
            "xin": np.ascontiguousarray(x[c * T:(c + 1) * T]),
            "wq_t": wq_t, "wk_t": wk_t, "wv_t": wv_t, "wg_t": wg_t, "wo_t": wo_t,
            "wup_t": wup_t, "wgt_t": wgt_t, "wdn_t": wdn_t,
            "bq": ins["L0_attn_bq"], "bk": ins["L0_attn_bk"], "bv": ins["L0_attn_bv"],
            "bg": ins["L0_attn_bg"], "bo": ins["L0_attn_bo"],
            "gnw": ins["L0_attn_gn_w"], "gnb": ins["L0_attn_gn_b"],
            "gam": ins["L0_attn_gamma"],
            "su_ug": ins["L0_ffn_SU_ug"], "su_dp": ins["L0_ffn_SU_dp"],
            "ws_ug": ins["L0_ffn_Wscale_ug"], "ws_dp": ins["L0_ffn_Wscale_dp"],
            "h128p": h128p, "h128n": h128n, "h128u": h128u,
            "emask": emask, "eexp": eexp,
        })
    res = run_bass_kernel_spmd(nc, in_maps, core_ids=list(range(NC)))
    out = np.concatenate([res.results[c]["y"] for c in range(NC)], axis=0)
    return out.reshape(1, S, D).astype(np.float32)


# version 12
# speedup vs baseline: 1.0828x; 1.0828x over previous
import numpy as np
import ml_dtypes
from contextlib import ExitStack

import concourse.bass as bass
import concourse.mybir as mybir
from concourse.bass_utils import run_bass_kernel_spmd

F32 = mybir.dt.float32
BF16 = mybir.dt.bfloat16
AF = mybir.ActivationFunctionType
ALU = mybir.AluOpType
BF = ml_dtypes.bfloat16

NC = 8
S = 8192
D = 2048
H = 16
DK = 128
T = S // NC
CH = 128
NCH = T // CH
INTER = 8192
SCALE = 64.0
EPS = 1e-5
TG = 512
NG = T // TG

ENGINES = ("sync", "gpsimd", "tensor", "vector", "scalar")


def _hadamard(n):
    h = np.array([[1.0]], np.float32)
    while h.shape[0] < n:
        h = np.block([[h, h], [h, -h]])
    return h


H16S = _hadamard(16)


class Prog:

    def __init__(self, nc, block, sems):
        self.nc = nc
        self.block = block
        self.sems = sems
        self.cnt = {k: 0 for k in sems}
        self.q = {e: [] for e in ENGINES}
        self.bank_free = {}

    def add(self, eng, fn):
        self.q[eng].append(fn)

    def wait(self, eng, sem, v):
        if v <= 0:
            return
        s = self.sems[sem]
        self.q[eng].append(lambda e, s=s, v=v: e.wait_ge(s, v))

    def inc(self, eng, sem, fn, n=1):
        self.cnt[sem] += n
        s = self.sems[sem]
        self.q[eng].append(lambda e, fn=fn, s=s, n=n: fn(e).then_inc(s, n))
        return self.cnt[sem]

    def run(self, eng, fn):
        self.q[eng].append(fn)

    def dma(self, eng, sem, out, in_, **kw):
        return self.inc(eng, sem, lambda e: e.dma_start(out=out, in_=in_, **kw), 16)

    def use_bank(self, bank):
        ent = self.bank_free.get(bank)
        if ent is not None:
            self.wait("tensor", ent[0], ent[1])

    def free_bank(self, bank, sem, v):
        self.bank_free[bank] = (sem, v)

    def barrier(self, engines=ENGINES):
        snap = dict(self.cnt)
        for e in engines:
            for sname, v in snap.items():
                if v > 0:
                    self.wait(e, sname, v)

    def flush(self):
        blk = self.block
        for eng in ENGINES:
            ops = self.q[eng]
            if not ops:
                continue
            deco = getattr(blk, eng)

            def body(e, ops=ops):
                for f in ops:
                    f(e)
            deco(body)
            self.q[eng] = []


def build_nc():
    nc = bass.Bass("TRN2", target_bir_lowering=False, debug=False)

    xin = nc.dram_tensor("xin", [T, D], F32, kind="ExternalInput").ap()
    wq_t = nc.dram_tensor("wq_t", [H, 16, 128, 128], F32, kind="ExternalInput").ap()
    wk_t = nc.dram_tensor("wk_t", [H, 16, 128, 128], F32, kind="ExternalInput").ap()
    wv_t = nc.dram_tensor("wv_t", [H, 16, 128, 128], F32, kind="ExternalInput").ap()
    wg_t = nc.dram_tensor("wg_t", [16, 16, 128, 128], F32, kind="ExternalInput").ap()
    wo_t = nc.dram_tensor("wo_t", [4, 16, 128, 512], F32, kind="ExternalInput").ap()
    wup_t = nc.dram_tensor("wup_t", [32, 16, 128, 256], F32, kind="ExternalInput").ap()
    wgt_t = nc.dram_tensor("wgt_t", [32, 16, 128, 256], F32, kind="ExternalInput").ap()
    wdn_t = nc.dram_tensor("wdn_t", [4, 16, 128, 4, 512], F32, kind="ExternalInput").ap()
    bq = nc.dram_tensor("bq", [D], F32, kind="ExternalInput").ap()
    bk = nc.dram_tensor("bk", [D], F32, kind="ExternalInput").ap()
    bv = nc.dram_tensor("bv", [D], F32, kind="ExternalInput").ap()
    bg = nc.dram_tensor("bg", [D], F32, kind="ExternalInput").ap()
    bo = nc.dram_tensor("bo", [D], F32, kind="ExternalInput").ap()
    gnw = nc.dram_tensor("gnw", [D], F32, kind="ExternalInput").ap()
    gnb = nc.dram_tensor("gnb", [D], F32, kind="ExternalInput").ap()
    gam = nc.dram_tensor("gam", [H], F32, kind="ExternalInput").ap()
    su_ug = nc.dram_tensor("su_ug", [D], F32, kind="ExternalInput").ap()
    su_dp = nc.dram_tensor("su_dp", [INTER], F32, kind="ExternalInput").ap()
    ws_ug = nc.dram_tensor("ws_ug", [2 * INTER], F32, kind="ExternalInput").ap()
    ws_dp = nc.dram_tensor("ws_dp", [D], F32, kind="ExternalInput").ap()
    h128p = nc.dram_tensor("h128p", [128, 128], BF16, kind="ExternalInput").ap()
    h128n = nc.dram_tensor("h128n", [128, 128], BF16, kind="ExternalInput").ap()
    h128u = nc.dram_tensor("h128u", [128, 128], BF16, kind="ExternalInput").ap()
    emask = nc.dram_tensor("emask", [NC], F32, kind="ExternalInput").ap()
    eexp = nc.dram_tensor("eexp", [NC], F32, kind="ExternalInput").ap()
    yout = nc.dram_tensor("y", [T, D], F32, kind="ExternalOutput").ap()

    x2d = nc.dram_tensor("x2d", [T, D], BF16).ap()
    dsc = nc.dram_tensor("dsc", [256], F32).ap()
    cc_si = [nc.dram_tensor(f"cc_si{h}", [128, 128], F32) for h in range(H)]
    cc_so = [nc.dram_tensor(f"cc_so{h}", [NC * 128, 128], F32) for h in range(H)]
    cc_ti = nc.dram_tensor("cc_ti", [32], F32)
    cc_to = nc.dram_tensor("cc_to", [32], F32)

    with ExitStack() as outer:
        ent = outer.enter_context
        semnames = ["Ls", "Lg", "La", "pe", "dv", "ac", "gp", "cc"]
        sems = {n: ent(nc.semaphore(n)) for n in semnames}

        PB = [ent(nc.psum_tensor(f"pb{i}", [128, 512], F32)) for i in range(6)]
        TB = [ent(nc.psum_tensor(f"tb{i}", [128, 1024], BF16)) for i in range(2)]

        ones1 = ent(nc.sbuf_tensor("ones1", [1, 512], F32))
        onesb = ent(nc.sbuf_tensor("onesb", [128, 1], F32))
        epsc = ent(nc.sbuf_tensor("epsc", [128, 1], F32))
        nhalf = ent(nc.sbuf_tensor("nhalf", [128, 1], F32))
        i128b = ent(nc.sbuf_tensor("i128b", [128, 128], BF16))
        hp = ent(nc.sbuf_tensor("hp", [128, 128], BF16))
        hn = ent(nc.sbuf_tensor("hn", [128, 128], BF16))
        hu = ent(nc.sbuf_tensor("hu", [128, 128], BF16))
        io_jmp = ent(nc.sbuf_tensor("io_jmp", [128, 128], F32))
        io_jp1 = ent(nc.sbuf_tensor("io_jp1", [128, 128], F32))
        io_kd = ent(nc.sbuf_tensor("io_kd", [128, 1], F32))
        lng_row = ent(nc.sbuf_tensor("lng_row", [1, 16], F32))
        lng_cols = ent(nc.sbuf_tensor("lng_cols", [128, 16], F32))
        kdec_cols = ent(nc.sbuf_tensor("kdec_cols", [128, 16], F32))
        gc_cols = ent(nc.sbuf_tensor("gc_cols", [128, 16], F32))
        wrep = ent(nc.sbuf_tensor("wrep", [128, 128], F32))
        bq_c = ent(nc.sbuf_tensor("bq_c", [128, 16], F32))
        bk_c = ent(nc.sbuf_tensor("bk_c", [128, 16], F32))
        bv_c = ent(nc.sbuf_tensor("bv_c", [128, 16], F32))
        bg_c = ent(nc.sbuf_tensor("bg_c", [128, 16], F32))
        gnw_c = ent(nc.sbuf_tensor("gnw_c", [128, 16], F32))
        gnb_c = ent(nc.sbuf_tensor("gnb_c", [128, 16], F32))
        su_c = ent(nc.sbuf_tensor("su_c", [128, 16], F32))
        sup_c = ent(nc.sbuf_tensor("sup_c", [128, 64], F32))
        sgt_c = ent(nc.sbuf_tensor("sgt_c", [128, 64], F32))
        row16 = ent(nc.sbuf_tensor("row16", [1, 16], F32))
        row128 = ent(nc.sbuf_tensor("row128", [1, 128], F32))
        stat16 = ent(nc.sbuf_tensor("stat16", [16, 8], F32))
        gsc_c = ent(nc.sbuf_tensor("gsc_c", [128, 16], F32))
        gbi_c = ent(nc.sbuf_tensor("gbi_c", [128, 16], F32))
        accs = ent(nc.sbuf_tensor("accs", [128, 32], F32))
        a32 = ent(nc.sbuf_tensor("a32", [32, 4], F32))

        block = ent(nc.Block())
        P = Prog(nc, block, sems)

        v_hp = P.dma("sync", "Ls", hp[:], h128p[:, :])
        v_hn = P.dma("sync", "Ls", hn[:], h128n[:, :])
        v_hu = P.dma("sync", "Ls", hu[:], h128u[:, :])
        colpairs = [(bq_c, bq), (bk_c, bk), (bv_c, bv), (bg_c, bg),
                    (gnw_c, gnw), (gnb_c, gnb), (su_c, su_ug)]
        v_cols = 0
        for dst, src in colpairs:
            v_cols = P.dma("sync", "Ls", dst[:], src.rearrange("(c p) -> p c", p=128),
                           allow_slow_non_contiguous=True)
        v_sup = P.dma("sync", "Ls", sup_c[:], ws_ug[0:INTER].rearrange("(c p) -> p c", p=128),
                      allow_slow_non_contiguous=True)
        v_sgt = P.dma("sync", "Ls", sgt_c[:], ws_ug[INTER:2 * INTER].rearrange("(c p) -> p c", p=128),
                      allow_slow_non_contiguous=True)
        sudp_c = sup_c
        v_lngrow = P.dma("sync", "Ls", row16[0:1, :], gam.rearrange("(o f) -> o f", o=1))
        v_em = P.dma("sync", "Ls", row128[0:1, 0:NC], emask.rearrange("(o f) -> o f", o=1))
        v_ee = P.dma("sync", "Ls", row128[0:1, 8:8 + NC], eexp.rearrange("(o f) -> o f", o=1))

        grow = ent(nc.sbuf_tensor("grow", [16, 128], F32))
        gbrow = ent(nc.sbuf_tensor("gbrow", [16, 128], F32))
        srow = ent(nc.sbuf_tensor("srow", [16, 128], F32))
        brow = ent(nc.sbuf_tensor("brow", [16, 128], F32))
        v_grow = P.dma("sync", "Ls", grow[:], gnw.rearrange("(p f) -> p f", p=16))
        v_gbrow = P.dma("sync", "Ls", gbrow[:], gnb.rearrange("(p f) -> p f", p=16))
        sudp_cols = ent(nc.sbuf_tensor("sudp_cols", [128, 64], F32))
        v_sudp = P.dma("sync", "Ls", sudp_cols[:], su_dp.rearrange("(c p) -> p c", p=128),
                       allow_slow_non_contiguous=True)

        c_ones = P.inc("gpsimd", "gp", lambda e: e.memset(ones1[:], 1.0))
        c_onesb = P.inc("gpsimd", "gp", lambda e: e.memset(onesb[:], 1.0))
        c_eps = P.inc("gpsimd", "gp", lambda e: e.memset(epsc[:], EPS))
        c_nh = P.inc("gpsimd", "gp", lambda e: e.memset(nhalf[:], -0.5))
        c_iojmp = P.inc("gpsimd", "gp", lambda e: e.iota(
            io_jmp[:], pattern=[[1, 128]], base=0, channel_multiplier=-1,
            allow_small_or_imprecise_dtypes=True))
        c_iojp1 = P.inc("gpsimd", "gp", lambda e: e.iota(
            io_jp1[:], pattern=[[1, 128]], base=1, channel_multiplier=0,
            allow_small_or_imprecise_dtypes=True))
        c_iokd = P.inc("gpsimd", "gp", lambda e: e.iota(
            io_kd[:], pattern=[[1, 1]], base=127, channel_multiplier=-1,
            allow_small_or_imprecise_dtypes=True))
        idf = ent(nc.sbuf_tensor("idf", [128, 128], F32))
        c_idf0 = P.inc("gpsimd", "gp", lambda e: e.memset(idf[:], 1.0))
        c_idf = P.inc("gpsimd", "gp", lambda e: e.affine_select(
            idf[:], idf[:], pattern=[[1, 128]], compare_op=ALU.is_equal,
            fill=0.0, base=0, channel_multiplier=-1))
        P.wait("vector", "gp", c_idf)
        c_i128b = P.inc("vector", "dv", lambda e: e.tensor_copy(i128b[:], idf[:]))

        P.wait("scalar", "Ls", v_lngrow)
        c_lng = P.inc("scalar", "ac", lambda e: e.activation(lng_row[0:1, :], row16[0:1, :], AF.Ln))
        P.wait("tensor", "gp", c_ones)
        P.wait("tensor", "ac", c_lng)
        P.use_bank("pb0")
        c_mm1 = P.inc("tensor", "pe", lambda e: e.matmul(
            PB[0][:, 0:16], ones1[0:1, 0:128], lng_row[0:1, :], start=True, stop=True))
        P.wait("vector", "pe", c_mm1)
        c_lngc = P.inc("vector", "dv", lambda e: e.tensor_copy(lng_cols[:], PB[0][:, 0:16]))
        P.free_bank("pb0", "dv", c_lngc)

        P.wait("scalar", "gp", c_iokd)
        P.wait("scalar", "dv", c_lngc)
        c_kd = 0
        for h in range(H):
            c_kd = P.inc("scalar", "ac", lambda e, h=h: e.activation(
                kdec_cols[:, h:h + 1], io_kd[:], AF.Exp, scale=lng_cols[:, h:h + 1]))
        c_l128 = P.inc("vector", "dv", lambda e: e.tensor_scalar_mul(row16[0:1, :], lng_row[0:1, :], 128.0))
        P.wait("scalar", "dv", c_l128)
        c_g128 = P.inc("scalar", "ac", lambda e: e.activation(row16[0:1, :], row16[0:1, :], AF.Exp))
        P.wait("tensor", "ac", c_g128)
        P.use_bank("pb0")
        c_mm2 = P.inc("tensor", "pe", lambda e: e.matmul(
            PB[0][:, 0:16], ones1[0:1, 0:128], row16[0:1, :], start=True, stop=True))
        P.wait("vector", "pe", c_mm2)
        c_gcc = P.inc("vector", "dv", lambda e: e.tensor_copy(gc_cols[:], PB[0][:, 0:16]))
        P.free_bank("pb0", "dv", c_gcc)

        lng16 = ent(nc.sbuf_tensor("lng16", [16, 1], F32))
        P.wait("sync", "Ls", v_lngrow)
        v_g16 = P.dma("sync", "Ls", lng16[:], gam.rearrange("(p o) -> p o", o=1))
        P.wait("scalar", "Ls", v_g16)
        c_lng16 = P.inc("scalar", "ac", lambda e: e.activation(lng16[:], lng16[:], AF.Ln))
        P.wait("vector", "ac", c_lng16)
        c_lng16b = P.inc("vector", "dv", lambda e: e.tensor_scalar_mul(lng16[:], lng16[:], 1024.0))
        P.wait("tensor", "Ls", v_ee)
        P.use_bank("pb1")
        c_mm3 = P.inc("tensor", "pe", lambda e: e.matmul(
            PB[1][0:16, 0:8], ones1[0:1, 0:16], row128[0:1, 8:8 + NC], start=True, stop=True))
        c_mm3b = P.inc("tensor", "pe", lambda e: e.matmul(
            PB[1][0:16, 8:16], ones1[0:1, 0:16], row128[0:1, 0:NC], start=True, stop=True))
        P.wait("scalar", "pe", c_mm3b)
        P.wait("scalar", "dv", c_lng16b)
        c_w16 = P.inc("scalar", "ac", lambda e: e.activation(
            stat16[:, 0:8], PB[1][0:16, 0:8], AF.Exp, scale=lng16[:]))
        P.wait("vector", "ac", c_w16)
        c_w16b = P.inc("vector", "dv", lambda e: e.tensor_mul(stat16[:, 0:8], stat16[:, 0:8], PB[1][0:16, 8:16]))
        P.free_bank("pb1", "dv", c_w16b)
        P.wait("gpsimd", "dv", c_w16b)
        v_fl1 = P.dma("gpsimd", "Lg", dsc[0:128].rearrange("(p f) -> p f", p=16), stat16[:, 0:8],
                      allow_slow_non_contiguous=True)
        P.wait("gpsimd", "Lg", v_fl1)
        v_fl2 = P.dma("gpsimd", "Lg", row128[0:1, :], dsc[0:128].rearrange("(o f) -> o f", o=1))
        P.wait("tensor", "Lg", v_fl2)
        P.use_bank("pb1")
        c_mm4 = P.inc("tensor", "pe", lambda e: e.matmul(
            PB[1][:, 0:128], ones1[0:1, 0:128], row128[0:1, :], start=True, stop=True))
        P.wait("vector", "pe", c_mm4)
        c_wrep = P.inc("vector", "dv", lambda e: e.tensor_copy(wrep[:], PB[1][:, 0:128]))
        P.free_bank("pb1", "dv", c_wrep)

        P.wait("vector", "Ls", max(v_sup, v_sgt, v_sudp))
        c_sup = P.inc("vector", "dv", lambda e: e.tensor_scalar_mul(sup_c[:], sup_c[:], SCALE))
        c_sup2 = P.inc("vector", "dv", lambda e: e.tensor_mul(sup_c[:], sup_c[:], sudp_cols[:]))
        c_sgt = P.inc("vector", "dv", lambda e: e.tensor_scalar_mul(sgt_c[:], sgt_c[:], SCALE))

        P.barrier()
        P.flush()

        with ExitStack() as ph_g:
            g_fm = ph_g.enter_context(nc.sbuf_tensor("g_fm", [128, H * 1024], BF16))
            wosl = ph_g.enter_context(nc.sbuf_tensor("wosl", [128, 16 * 512], BF16))
            bo_tab = ph_g.enter_context(nc.sbuf_tensor("bo_tab", [128, 2048], BF16))
            ret = ph_g.enter_context(nc.sbuf_tensor("ret", [128, H * 1024], BF16))
            sin = ph_g.enter_context(nc.sbuf_tensor("sin", [128, H * 128], F32))

            with ExitStack() as ph_a:
                ea = ph_a.enter_context
                x_fm = ea(nc.sbuf_tensor("x_fm", [128, 16 * 1024], BF16))
                qr = ea(nc.sbuf_tensor("qr", [128, 4 * 1024], BF16))
                kr = ea(nc.sbuf_tensor("kr", [128, 4 * 1024], BF16))
                vr = ea(nc.sbuf_tensor("vr", [128, 4 * 1024], BF16))
                vfm = ea(nc.sbuf_tensor("vfm", [128, 1024], BF16))
                wsl = [ea(nc.sbuf_tensor(f"wsl{i}", [128, 2048], BF16)) for i in range(6)]
                xst = ea(nc.sbuf_tensor("xst", [128, 2048], F32))
                lamb = [ea(nc.sbuf_tensor(f"lam{i}", [128, 128], F32)) for i in range(2)]
                dtab = [ea(nc.sbuf_tensor(f"dtab{i}", [128, 128], F32)) for i in range(2)]
                qdec = [ea(nc.sbuf_tensor(f"qdec{i}", [128, 128], BF16)) for i in range(2)]
                srun = ea(nc.sbuf_tensor("srun", [128, 128], F32))
                khA = ea(nc.sbuf_tensor("khA", [128, 1024], BF16))
                khF = ea(nc.sbuf_tensor("khF", [128, 1024], BF16))
                scmb = ea(nc.sbuf_tensor("scmb", [128, 1024], BF16))
                qdqb = ea(nc.sbuf_tensor("qdqb", [128, 1024], BF16))
                sbst = ea(nc.sbuf_tensor("sbst", [128, 8 * 128], BF16))
                gbuf = ea(nc.sbuf_tensor("gbuf", [128, 512], F32))

                for tc in range(8):
                    vx = P.dma("sync", "Ls", xst[:], xin[tc * 128:(tc + 1) * 128, :])
                    P.wait("tensor", "Ls", vx)
                    for grp in range(4):
                        bank = f"pb{2 + grp % 2}"
                        pb = PB[2 + grp % 2]
                        P.use_bank(bank)
                        c_tr = 0
                        for k in range(4):
                            fc = grp * 4 + k
                            c_tr = P.inc("tensor", "pe", lambda e, fc=fc, k=k, pb=pb: e.transpose(
                                pb[:, k * 128:(k + 1) * 128],
                                xst[:, fc * 128:(fc + 1) * 128], idf[:]))
                        P.wait("vector", "pe", c_tr)
                        dst = x_fm.ap()[:, grp * 4 * 1024 + tc * 128:]
                        dst = bass.AP(tensor=dst.tensor, offset=dst.offset,
                                      ap=[[dst.ap[0][0], 128], [1024, 4], [1, 128]])
                        c_ev = P.inc("vector", "dv", lambda e, dst=dst, pb=pb: e.tensor_copy(dst, pb[:, 0:512].rearrange("p (a b) -> p a b", a=4)))
                        P.free_bank(bank, "dv", c_ev)
                    P.wait("sync", "pe", P.cnt["pe"])
                P.wait("tensor", "dv", P.cnt["dv"])

                LAG = 2
                fullpass_state = {}

                def emit_wload(h):
                    if h >= H:
                        return
                    bufs = (wsl[0 + (h % 2)], wsl[2 + (h % 2)], wsl[4 + (h % 2)])
                    prev = fullpass_state.get(("wsl", h - 2))
                    if prev:
                        P.wait("gpsimd", "pe", prev)
                    vq = P.dma("gpsimd", "Lg", bufs[0].ap().rearrange("p (a b) -> p a b", a=16), wq_t[h].rearrange("a p b -> p a b"))
                    vk = P.dma("gpsimd", "Lg", bufs[1].ap().rearrange("p (a b) -> p a b", a=16), wk_t[h].rearrange("a p b -> p a b"))
                    vv = P.dma("gpsimd", "Lg", bufs[2].ap().rearrange("p (a b) -> p a b", a=16), wv_t[h].rearrange("a p b -> p a b"))
                    fullpass_state[("wld", h)] = (vq, vk, vv)

                def emit_proj(h):
                    slot = h % 4
                    bufs = (wsl[0 + (h % 2)], wsl[2 + (h % 2)], wsl[4 + (h % 2)])
                    vq, vk, vv = fullpass_state[("wld", h)]
                    prev2 = fullpass_state.get(("fp", h - 4))
                    if prev2:
                        P.wait("tensor", "pe", prev2)
                    c_pe_last = 0
                    for which, (wbuf, vdone, dst, bias) in enumerate(
                            [(bufs[0], vq, qr, bq_c), (bufs[1], vk, kr, bk_c), (bufs[2], vv, vfm, bv_c)]):
                        P.wait("tensor", "Lg", vdone)
                        for th in range(2):
                            bank = f"pb{(which * 2 + th) % 2}"
                            pb = PB[(which * 2 + th) % 2]
                            P.use_bank(bank)
                            for fc in range(16):
                                c_pe_last = P.inc("tensor", "pe", lambda e, pb=pb, wbuf=wbuf, fc=fc, th=th: e.matmul(
                                    pb[:], wbuf[:, fc * 128:(fc + 1) * 128],
                                    x_fm[:, fc * 1024 + th * 512: fc * 1024 + (th + 1) * 512],
                                    start=(fc == 0), stop=(fc == 15)))
                            P.wait("scalar", "pe", c_pe_last)
                            if which < 2:
                                out_ap = dst[:, slot * 1024 + th * 512: slot * 1024 + (th + 1) * 512]
                            else:
                                out_ap = dst[:, th * 512:(th + 1) * 512]
                            c_ev = P.inc("scalar", "ac", lambda e, out_ap=out_ap, pb=pb, bias=bias, h=h: e.activation(
                                out_ap, pb[:], AF.Identity, bias=bias[:, h:h + 1]))
                            P.free_bank(bank, "ac", c_ev)
                    c_evv = 0
                    for grp in range(2):
                        tb = grp % 2
                        P.use_bank(f"tb{tb}")
                        c_tr = 0
                        for k in range(4):
                            ci = grp * 4 + k
                            P.wait("tensor", "ac", P.cnt["ac"])
                            c_tr = P.inc("tensor", "pe", lambda e, ci=ci, k=k, tb=tb: e.transpose(
                                TB[tb][:, k * 128:(k + 1) * 128],
                                vfm[:, ci * 128:(ci + 1) * 128], i128b[:]))
                        P.wait("scalar", "pe", c_tr)
                        dst = vr.ap()[:, slot * 1024 + grp * 512:]
                        dst = bass.AP(tensor=dst.tensor, offset=dst.offset,
                                      ap=[[dst.ap[0][0], 128], [128, 4], [1, 128]])
                        c_evv = P.inc("scalar", "ac", lambda e, dst=dst, tb=tb: e.copy(
                            dst, TB[tb][:, 0:512].rearrange("p (a b) -> p a b", a=4)))
                        P.free_bank(f"tb{tb}", "ac", c_evv)
                    fullpass_state[("wsl", h)] = c_pe_last
                    fullpass_state[("proj", h)] = c_evv

                def emit_passA_khat(h, ast):
                    slot = h % 4
                    pv = fullpass_state.get(("vput", h - 2))
                    if pv:
                        P.wait("vector", "Lg", pv)
                    P.wait("tensor", "ac", fullpass_state[("proj", h)])
                    P.use_bank("tb0")
                    c_tr = 0
                    for ci in range(NCH):
                        c_tr = P.inc("tensor", "pe", lambda e, h=h, ci=ci, slot=slot: e.transpose(
                            TB[0][:, ci * 128:(ci + 1) * 128],
                            kr[:, slot * 1024 + ci * 128: slot * 1024 + (ci + 1) * 128], i128b[:]))
                    P.wait("scalar", "pe", c_tr)
                    c_kh = P.inc("scalar", "ac", lambda e, h=h: e.activation(
                        khA[:], TB[0][:], AF.Identity, scale=kdec_cols[:, h:h + 1]))
                    P.free_bank("tb0", "ac", c_kh)
                    ast["kh"] = c_kh

                def emit_passA_ds(h, ast):
                    slot = h % 4
                    lam = lamb[h % 2]
                    P.wait("tensor", "ac", ast["kh"])
                    c_su = 0
                    for rnd in range(2):
                        P.use_bank("pb5")
                        c_ds = 0
                        for k in range(4):
                            ci = rnd * 4 + k
                            c_ds = P.inc("tensor", "pe", lambda e, h=h, ci=ci, k=k, slot=slot: e.matmul(
                                PB[5][:, k * 128:(k + 1) * 128], khA[:, ci * 128:(ci + 1) * 128],
                                vr[:, slot * 1024 + ci * 128: slot * 1024 + (ci + 1) * 128],
                                start=True, stop=True))
                        P.wait("vector", "pe", c_ds)
                        for k in range(4):
                            ci = rnd * 4 + k
                            if ci == 0:
                                c_su = P.inc("vector", "dv", lambda e, lam=lam, k=k: e.tensor_copy(
                                    lam[:], PB[5][:, k * 128:(k + 1) * 128]))
                            else:
                                c_su = P.inc("vector", "dv", lambda e, lam=lam, h=h, k=k: e.scalar_tensor_tensor(
                                    lam[:], lam[:], gc_cols[:, h:h + 1], PB[5][:, k * 128:(k + 1) * 128],
                                    ALU.mult, ALU.add))
                        P.free_bank("pb5", "dv", c_su)
                    ast["su"] = c_su

                def emit_passA_cc(h, ast):
                    lam = lamb[h % 2]
                    P.wait("gpsimd", "dv", ast["su"])
                    vput = P.dma("gpsimd", "Lg", cc_si[h].ap(), lam[:])
                    fullpass_state[("vput", h)] = vput
                    P.wait("gpsimd", "Lg", vput)
                    c_cc = P.inc("gpsimd", "cc", lambda e, h=h: e.collective_compute(
                        "AllGather", ALU.bypass, replica_groups=[list(range(NC))],
                        ins=[cc_si[h].ap().opt()], outs=[cc_so[h].ap().opt()]))
                    fullpass_state[("cc", h)] = c_cc

                def emit_fullpass_batches(h, fst):
                    slot = h % 4
                    dt = dtab[h % 2]
                    qd = qdec[h % 2]
                    prev_fp = fullpass_state.get(("fp", h - 1), 0)
                    prev = fullpass_state.get(("dt", h - 2))
                    if prev:
                        P.wait("scalar", "dv", prev)
                    P.wait("scalar", "pe", prev_fp)
                    c_d1 = P.inc("scalar", "ac", lambda e, dt=dt, h=h: e.activation(
                        dt[:], io_jmp[:], AF.Exp, scale=lng_cols[:, h:h + 1]))
                    P.wait("gpsimd", "ac", c_d1)
                    c_d2 = P.inc("gpsimd", "gp", lambda e, dt=dt: e.affine_select(
                        dt[:], dt[:], pattern=[[1, 128]], compare_op=ALU.is_ge,
                        fill=0.0, base=0, channel_multiplier=-1))
                    c_q1 = P.inc("scalar", "ac", lambda e, qd=qd, h=h: e.activation(
                        qd[:], io_jp1[:], AF.Exp, scale=lng_cols[:, h:h + 1]))
                    P.wait("sync", "cc", fullpass_state[("cc", h)])
                    P.wait("sync", "dv", fullpass_state.get(("si", h - 1), 0))
                    P.wait("vector", "pe", prev_fp)
                    c_z = P.inc("vector", "dv", lambda e, h=h: e.memset(sin[:, h * 128:(h + 1) * 128], 0.0))
                    c_si = c_z
                    for half in range(2):
                        src_ap = cc_so[h].ap()[half * 512:(half + 1) * 512, :]
                        vgb = P.dma("sync", "Ls", gbuf.ap().rearrange("p (c dv) -> p c dv", c=4),
                                    src_ap.rearrange("(c dk) dv -> dk c dv", dk=128))
                        P.wait("vector", "Ls", vgb)
                        for cp4 in range(4):
                            cp = half * 4 + cp4
                            c_si = P.inc("vector", "dv", lambda e, h=h, cp=cp, cp4=cp4: e.scalar_tensor_tensor(
                                sin[:, h * 128:(h + 1) * 128], gbuf[:, cp4 * 128:(cp4 + 1) * 128],
                                wrep[:, h * 8 + cp: h * 8 + cp + 1], sin[:, h * 128:(h + 1) * 128],
                                ALU.mult, ALU.add))
                        if half == 0:
                            P.wait("sync", "dv", c_si)
                    fullpass_state[("si", h)] = c_si
                    P.wait("vector", "ac", fullpass_state.get(("stc", h - 1), 0))
                    c_sr = P.inc("vector", "dv", lambda e, h=h: e.tensor_copy(srun[:], sin[:, h * 128:(h + 1) * 128]))
                    P.wait("scalar", "dv", c_sr)
                    c_sb0 = P.inc("scalar", "ac", lambda e: e.activation(sbst[:, 0:128], srun[:], AF.Copy))
                    c_scm = 0
                    for rnd in range(2):
                        P.use_bank("pb2")
                        if rnd == 0:
                            P.wait("tensor", "gp", c_d2)
                        c_s = 0
                        for k in range(4):
                            ci = rnd * 4 + k
                            qsl = qr[:, slot * 1024 + ci * 128: slot * 1024 + (ci + 1) * 128]
                            ksl = kr[:, slot * 1024 + ci * 128: slot * 1024 + (ci + 1) * 128]
                            c_s = P.inc("tensor", "pe", lambda e, ksl=ksl, qsl=qsl, k=k: e.matmul(
                                PB[2][:, k * 128:(k + 1) * 128], ksl, qsl, start=True, stop=True))
                        P.wait("vector", "pe", c_s)
                        for k in range(4):
                            ci = rnd * 4 + k
                            c_scm = P.inc("vector", "dv", lambda e, dt=dt, ci=ci, k=k: e.tensor_mul(
                                scmb[:, ci * 128:(ci + 1) * 128], PB[2][:, k * 128:(k + 1) * 128], dt[:]))
                        P.free_bank("pb2", "dv", c_scm)
                    P.wait("vector", "ac", c_q1)
                    c_qd = 0
                    for ci in range(NCH):
                        qsl = qr[:, slot * 1024 + ci * 128: slot * 1024 + (ci + 1) * 128]
                        c_qd = P.inc("vector", "dv", lambda e, qsl=qsl, qd=qd, ci=ci: e.tensor_mul(
                            qdqb[:, ci * 128:(ci + 1) * 128], qsl, qd[:]))
                    P.use_bank("tb1")
                    c_tr = 0
                    for ci in range(NCH - 1):
                        ksl = kr[:, slot * 1024 + ci * 128: slot * 1024 + (ci + 1) * 128]
                        c_tr = P.inc("tensor", "pe", lambda e, ksl=ksl, ci=ci: e.transpose(
                            TB[1][:, ci * 128:(ci + 1) * 128], ksl, i128b[:]))
                    P.wait("scalar", "pe", c_tr)
                    c_kh = P.inc("scalar", "ac", lambda e, h=h: e.activation(
                        khF[:, 0:896], TB[1][:, 0:896], AF.Identity, scale=kdec_cols[:, h:h + 1]))
                    P.free_bank("tb1", "ac", c_kh)
                    fst.update(kh=c_kh, sb0=c_sb0, qd=c_qd, scm=c_scm)

                def emit_fullpass_chain(h, fst):
                    slot = h % 4
                    P.wait("tensor", "ac", fst["kh"])
                    c_stc = fst["sb0"]
                    for rnd in range(2):
                        nk = 4 if rnd == 0 else 3
                        P.use_bank("pb4")
                        c_ds = 0
                        for k in range(nk):
                            ci = rnd * 4 + k
                            vsl = vr[:, slot * 1024 + ci * 128: slot * 1024 + (ci + 1) * 128]
                            c_ds = P.inc("tensor", "pe", lambda e, h=h, ci=ci, k=k, vsl=vsl: e.matmul(
                                PB[4][:, k * 128:(k + 1) * 128], khF[:, ci * 128:(ci + 1) * 128],
                                vsl, start=True, stop=True))
                        P.wait("vector", "pe", c_ds)
                        c_su2 = 0
                        for k in range(nk):
                            ci = rnd * 4 + k
                            P.wait("vector", "ac", c_stc)
                            c_su2 = P.inc("vector", "dv", lambda e, h=h, k=k: e.scalar_tensor_tensor(
                                srun[:], srun[:], gc_cols[:, h:h + 1], PB[4][:, k * 128:(k + 1) * 128],
                                ALU.mult, ALU.add))
                            P.wait("scalar", "dv", c_su2)
                            c_stc = P.inc("scalar", "ac", lambda e, ci=ci: e.activation(
                                sbst[:, (ci + 1) * 128:(ci + 2) * 128], srun[:], AF.Copy))
                        P.free_bank("pb4", "dv", c_su2)
                    fst["stc"] = c_stc

                def emit_fullpass_ret(h, fst):
                    slot = h % 4
                    c_stc = fst["stc"]
                    P.wait("tensor", "ac", c_stc)
                    P.wait("tensor", "dv", max(fst["qd"], fst["scm"]))
                    last_pe = 0
                    c_rv = 0
                    for rnd in range(2):
                        P.use_bank("pb3")
                        for k in range(4):
                            ci = rnd * 4 + k
                            vsl = vr[:, slot * 1024 + ci * 128: slot * 1024 + (ci + 1) * 128]
                            P.inc("tensor", "pe", lambda e, vsl=vsl, ci=ci, k=k: e.matmul(
                                PB[3][:, k * 128:(k + 1) * 128], scmb[:, ci * 128:(ci + 1) * 128],
                                vsl, start=True, stop=False))
                            last_pe = P.inc("tensor", "pe", lambda e, ci=ci, k=k: e.matmul(
                                PB[3][:, k * 128:(k + 1) * 128], qdqb[:, ci * 128:(ci + 1) * 128],
                                sbst[:, ci * 128:(ci + 1) * 128], start=False, stop=True))
                        P.wait("scalar", "pe", last_pe)
                        c_rv = P.inc("scalar", "ac", lambda e, h=h, rnd=rnd: e.copy(
                            ret[:, h * 1024 + rnd * 512: h * 1024 + (rnd + 1) * 512], PB[3][:]))
                        P.free_bank("pb3", "ac", c_rv)
                    P.wait("vector", "ac", c_rv)
                    c_st1 = P.inc("vector", "dv", lambda e, h=h: e.tensor_scalar(
                        xst[:, 0:1024], ret[:, h * 1024:(h + 1) * 1024], 0.0, None, ALU.add, ALU.add,
                        accum_out=accs[:, h:h + 1]))
                    c_st2 = P.inc("vector", "dv", lambda e, h=h: e.scalar_tensor_tensor(
                        xst[:, 0:1024], ret[:, h * 1024:(h + 1) * 1024], 1.0,
                        ret[:, h * 1024:(h + 1) * 1024], ALU.mult, ALU.mult,
                        accum_out=accs[:, 16 + h:17 + h]))
                    fullpass_state[("dt", h)] = c_st2
                    fullpass_state[("fp", h)] = last_pe
                    fullpass_state[("stc", h)] = c_stc

                emit_wload(0)
                for h in range(H):
                    emit_wload(h + 1)
                    emit_proj(h)
                    fp = h - LAG if h >= LAG else None
                    fst = fullpass_state.get(("fst", fp)) if fp is not None else None
                    if fp is not None:
                        fst = {}
                        emit_fullpass_batches(fp, fst)
                    ast = {}
                    emit_passA_khat(h, ast)
                    if fp is not None:
                        emit_fullpass_chain(fp, fst)
                    emit_passA_ds(h, ast)
                    if fp is not None:
                        emit_fullpass_ret(fp, fst)
                    emit_passA_cc(h, ast)
                for h in range(H - LAG, H):
                    fst = {}
                    emit_fullpass_batches(h, fst)
                    emit_fullpass_chain(h, fst)
                    emit_fullpass_ret(h, fst)

                for oc in range(16):
                    wbuf = wsl[oc % 2]
                    if oc < 2:
                        P.wait("gpsimd", "pe", fullpass_state.get(("wsl", 14 + oc), 0))
                    else:
                        P.wait("gpsimd", "pe", fullpass_state.get(("gw", oc - 2), 0))
                    vw = P.dma("gpsimd", "Lg", wbuf.ap().rearrange("p (a b) -> p a b", a=16), wg_t[oc].rearrange("a p b -> p a b"))
                    P.wait("tensor", "Lg", vw)
                    c_pe = 0
                    for th in range(2):
                        bank = f"pb{th}"
                        pb = PB[th]
                        P.use_bank(bank)
                        for fc in range(16):
                            c_pe = P.inc("tensor", "pe", lambda e, pb=pb, wbuf=wbuf, fc=fc, th=th: e.matmul(
                                pb[:], wbuf[:, fc * 128:(fc + 1) * 128],
                                x_fm[:, fc * 1024 + th * 512: fc * 1024 + (th + 1) * 512],
                                start=(fc == 0), stop=(fc == 15)))
                        P.wait("scalar", "pe", c_pe)
                        c_g = P.inc("scalar", "ac", lambda e, oc=oc, th=th, pb=pb: e.activation(
                            g_fm[:, oc * 1024 + th * 512: oc * 1024 + (th + 1) * 512],
                            pb[:], AF.Silu, bias=bg_c[:, oc:oc + 1]))
                        P.free_bank(bank, "ac", c_g)
                    fullpass_state[("gw", oc)] = c_pe

                P.barrier(("tensor", "vector", "scalar", "gpsimd"))
                P.use_bank("pb5")
                c_tr = P.inc("tensor", "pe", lambda e: e.transpose(
                    PB[5][0:32, 0:128], accs[:], idf[:]))
                P.wait("vector", "pe", c_tr)
                c_red = P.inc("vector", "dv", lambda e: e.tensor_reduce(
                    a32[:, 0:1], PB[5][0:32, 0:128], mybir.AxisListType.X, ALU.add))
                P.free_bank("pb5", "dv", c_red)
                P.wait("gpsimd", "dv", c_red)
                v_st = P.dma("gpsimd", "Lg", cc_ti.ap().rearrange("(p o) -> p o", o=1), a32[:, 0:1])
                P.wait("gpsimd", "Lg", v_st)
                c_ccs = P.inc("gpsimd", "cc", lambda e: e.collective_compute(
                    "AllReduce", ALU.add, replica_groups=[list(range(NC))],
                    ins=[cc_ti.ap().opt()], outs=[cc_to.ap().opt()]))
                P.wait("sync", "cc", c_ccs)
                v_sb = P.dma("sync", "Ls", stat16[:, 0:2], cc_to.ap().rearrange("(b a) -> a b", b=2),
                             allow_slow_non_contiguous=True)
                NSTAT = float(S * DK)
                P.wait("vector", "Ls", v_sb)
                c_m = P.inc("vector", "dv", lambda e: e.tensor_scalar_mul(stat16[:, 0:2], stat16[:, 0:2], 1.0 / NSTAT))
                P.wait("scalar", "dv", c_m)
                c_m2 = P.inc("scalar", "ac", lambda e: e.activation(stat16[:, 2:3], stat16[:, 0:1], AF.Square))
                P.wait("vector", "ac", c_m2)
                c_v = P.inc("vector", "dv", lambda e: e.tensor_sub(stat16[:, 1:2], stat16[:, 1:2], stat16[:, 2:3]))
                P.wait("scalar", "dv", c_v)
                c_sq = P.inc("scalar", "ac", lambda e: e.activation(stat16[:, 1:2], stat16[:, 1:2], AF.Ln, bias=epsc[0:16, :]))
                c_rc = P.inc("scalar", "ac", lambda e: e.activation(stat16[:, 1:2], stat16[:, 1:2], AF.Exp, scale=nhalf[0:16, :]))
                P.wait("vector", "ac", c_rc)
                c_sr2 = P.inc("vector", "dv", lambda e: e.tensor_scalar_mul(srow[:], grow[:], stat16[:, 1:2]))
                c_br1 = P.inc("vector", "dv", lambda e: e.tensor_scalar_mul(brow[:], srow[:], stat16[:, 0:1]))
                c_br2 = P.inc("vector", "dv", lambda e: e.tensor_sub(brow[:], gbrow[:], brow[:]))
                P.use_bank("pb5")
                P.wait("tensor", "dv", c_br2)
                c_t1x = P.inc("tensor", "pe", lambda e: e.transpose(
                    PB[5][:, 0:16], srow[:], idf[0:16, 0:16]))
                c_t2x = P.inc("tensor", "pe", lambda e: e.transpose(
                    PB[5][:, 16:32], brow[:], idf[0:16, 0:16]))
                P.wait("vector", "pe", c_t2x)
                c_gs = P.inc("vector", "dv", lambda e: e.tensor_copy(gsc_c[:], PB[5][:, 0:16]))
                c_gb2 = P.inc("vector", "dv", lambda e: e.tensor_copy(gbi_c[:], PB[5][:, 16:32]))
                P.free_bank("pb5", "dv", c_gb2)
                bo_row = xst
                v_borow = P.dma("sync", "Ls", bo_row[0:1, 0:2048], bo.rearrange("(o f) -> o f", o=1))
                v_wo0 = P.dma("gpsimd", "Lg", wosl.ap().rearrange("p (a b) -> p a b", a=16),
                              wo_t[0].rearrange("a p b -> p a b"))
                P.wait("tensor", "Ls", v_borow)
                for gg in range(4):
                    P.use_bank("pb5")
                    c_bo = P.inc("tensor", "pe", lambda e, gg=gg: e.matmul(
                        PB[5][:, 0:512], ones1[0:1, 0:128], bo_row[0:1, gg * 512:(gg + 1) * 512],
                        start=True, stop=True))
                    P.wait("vector", "pe", c_bo)
                    c_ev = P.inc("vector", "dv", lambda e, gg=gg: e.tensor_copy(
                        bo_tab[:, gg * 512:(gg + 1) * 512], PB[5][:, 0:512]))
                    P.free_bank("pb5", "dv", c_ev)

                P.barrier(("tensor", "vector", "scalar"))
                for h in range(H):
                    for ci in range(NCH):
                        tb = (h * NCH + ci) % 2
                        P.use_bank(f"tb{tb}")
                        c_tr = P.inc("tensor", "pe", lambda e, h=h, ci=ci, tb=tb: e.transpose(
                            TB[tb][:, 0:128], ret[:, h * 1024 + ci * 128: h * 1024 + (ci + 1) * 128], i128b[:]))
                        P.wait("vector", "pe", c_tr)
                        c_t1 = P.inc("vector", "dv", lambda e, tb=tb, h=h: e.tensor_scalar(
                            khA[:, 0:128], TB[tb][:, 0:128], gsc_c[:, h:h + 1], gbi_c[:, h:h + 1], ALU.mult, ALU.add))
                        P.free_bank(f"tb{tb}", "dv", c_t1)
                        gsl = g_fm[:, h * 1024 + ci * 128: h * 1024 + (ci + 1) * 128]
                        c_t2 = P.inc("vector", "dv", lambda e, gsl=gsl: e.tensor_mul(gsl, gsl, khA[:, 0:128]))

                P.barrier()
                P.flush()

            with ExitStack() as ph_wo:
                ew = ph_wo.enter_context
                x2pre = ew(nc.sbuf_tensor("x2pre", [128, 8 * 2048], F32))
                xrel = ew(nc.sbuf_tensor("xrel", [128, 2048], F32))
                x2s = ew(nc.sbuf_tensor("x2s", [128, 2048], BF16))
                lnst = ew(nc.sbuf_tensor("lnst", [128, 4], F32))
                lnsc = ew(nc.sbuf_tensor("lnsc", [128, 2048], F32))

                xrel_done = {}
                prev_og_pe = 0
                for og in range(4):
                    if og > 0:
                        P.wait("gpsimd", "pe", prev_og_pe)
                        vw = P.dma("gpsimd", "Lg", wosl.ap().rearrange("p (a b) -> p a b", a=16), wo_t[og].rearrange("a p b -> p a b"))
                    else:
                        vw = v_wo0
                    P.wait("tensor", "Lg", vw)
                    for tc in range(8):
                        if og == 0:
                            pass
                        bank = f"pb{tc % 4}"
                        pb = PB[tc % 4]
                        P.use_bank(bank)
                        c_pe = 0
                        for fc in range(16):
                            c_pe = P.inc("tensor", "pe", lambda e, pb=pb, fc=fc, tc=tc: e.matmul(
                                pb[:], g_fm[:, fc * 1024 + tc * 128: fc * 1024 + (tc + 1) * 128],
                                wosl[:, fc * 512:(fc + 1) * 512],
                                start=(fc == 0), stop=(fc == 15)))
                        P.wait("vector", "pe", c_pe)
                        dst = x2pre[:, tc * 2048 + og * 512: tc * 2048 + (og + 1) * 512]
                        c_e1 = P.inc("vector", "dv", lambda e, dst=dst, pb=pb, og=og: e.tensor_add(
                            dst, pb[:], bo_tab[:, og * 512:(og + 1) * 512]))
                        P.free_bank(bank, "dv", c_e1)
                    prev_og_pe = P.cnt["pe"]

                vx2st = 0
                prev_ca = 0
                for tc in range(8):
                    P.wait("scalar", "dv", prev_ca)
                    vx = P.dma("scalar", "La", xrel[:], xin[tc * 128:(tc + 1) * 128, :])
                    P.wait("vector", "La", vx)
                    sl = x2pre[:, tc * 2048:(tc + 1) * 2048]
                    c_a = P.inc("vector", "dv", lambda e, sl=sl: e.tensor_add(sl, sl, xrel[:]))
                    prev_ca = c_a
                    P.wait("vector", "Ls", vx2st)
                    c_s1 = P.inc("vector", "dv", lambda e, sl=sl: e.tensor_scalar(
                        lnsc[:], sl, 0.0, None, ALU.add, ALU.add, accum_out=lnst[:, 0:1]))
                    c_s2 = P.inc("vector", "dv", lambda e, sl=sl: e.scalar_tensor_tensor(
                        lnsc[:], sl, 1.0, sl, ALU.mult, ALU.mult, accum_out=lnst[:, 1:2]))
                    P.wait("scalar", "dv", c_s2)
                    c_ms = P.inc("scalar", "ac", lambda e: e.mul(lnst[:, 0:1], lnst[:, 0:1], 1.0 / D))
                    c_qs = P.inc("scalar", "ac", lambda e: e.mul(lnst[:, 1:2], lnst[:, 1:2], 1.0 / D))
                    c_m2 = P.inc("scalar", "ac", lambda e: e.activation(lnst[:, 2:3], lnst[:, 0:1], AF.Square))
                    P.wait("vector", "ac", c_m2)
                    c_v = P.inc("vector", "dv", lambda e: e.tensor_sub(lnst[:, 1:2], lnst[:, 1:2], lnst[:, 2:3]))
                    P.wait("scalar", "dv", c_v)
                    c_sq = P.inc("scalar", "ac", lambda e: e.activation(lnst[:, 1:2], lnst[:, 1:2], AF.Ln, bias=epsc[:]))
                    c_rc = P.inc("scalar", "ac", lambda e: e.activation(lnst[:, 1:2], lnst[:, 1:2], AF.Exp, scale=nhalf[:]))
                    P.wait("vector", "ac", c_rc)
                    c_ap = P.inc("vector", "dv", lambda e, sl=sl: e.tensor_scalar(
                        x2s[:], sl, lnst[:, 0:1], lnst[:, 1:2], ALU.subtract, ALU.mult))
                    P.wait("sync", "dv", c_ap)
                    vx2st = P.dma("sync", "Ls", x2d[tc * 128:(tc + 1) * 128, :], x2s[:])
                    P.wait("scalar", "Ls", vx2st)

                P.barrier()
                P.flush()

        with ExitStack() as ph_f:
            ef = ph_f.enter_context
            x3pre = ef(nc.sbuf_tensor("x3pre", [128, 8 * 2048], BF16))
            sdp_row = ef(nc.sbuf_tensor("sdp_row", [1, 2048], F32))
            abuf = ef(nc.sbuf_tensor("abuf", [128, 64 * TG], BF16))
            sdp_tab = ef(nc.sbuf_tensor("sdp_tab", [128, 2048], F32))
            silu_sb = ef(nc.sbuf_tensor("silu_sb", [128, 512], BF16))
            lnst2 = ef(nc.sbuf_tensor("lnst2", [128, 4], F32))
            lnsc2 = ef(nc.sbuf_tensor("lnsc2", [128, 2048], F32))
            x2g = ef(nc.sbuf_tensor("x2g", [128, 4 * 2048], BF16))

            v_sdprow = P.dma("sync", "Ls", sdp_row[0:1, :], ws_dp.rearrange("(o f) -> o f", o=1))
            P.wait("tensor", "Ls", v_sdprow)
            for gg in range(4):
                P.use_bank("pb5")
                c_mm = P.inc("tensor", "pe", lambda e, gg=gg: e.matmul(
                    PB[5][:, 0:512], ones1[0:1, 0:128], sdp_row[0:1, gg * 512:(gg + 1) * 512],
                    start=True, stop=True))
                P.wait("vector", "pe", c_mm)
                c_ev = P.inc("vector", "dv", lambda e, gg=gg: e.tensor_scalar_mul(
                    sdp_tab[:, gg * 512:(gg + 1) * 512], PB[5][:, 0:512], SCALE))
                P.free_bank("pb5", "dv", c_ev)
            P.flush()

            for g in range(NG):
                with ExitStack() as ph_g2:
                    eg = ph_g2.enter_context
                    h1 = eg(nc.sbuf_tensor(f"h1_{g}", [128, 16 * TG], BF16))
                    wdsl = [eg(nc.sbuf_tensor(f"wd{g}_{i}", [128, 4 * 512], BF16)) for i in range(2)]

                    with ExitStack() as ph_xt:
                        xt = ph_xt.enter_context(nc.sbuf_tensor(f"xt_{g}", [128, 16 * TG], BF16))
                        for tcl in range(TG // 128):
                            tc = g * (TG // 128) + tcl
                            vx2 = P.dma("sync", "Ls", x2g[:, tcl * 2048:(tcl + 1) * 2048],
                                        x2d[tc * 128:(tc + 1) * 128, :])
                            P.wait("tensor", "Ls", vx2)
                            for grp in range(4):
                                tb = grp % 2
                                P.use_bank(f"tb{tb}")
                                c_tr = 0
                                for k in range(4):
                                    fc = grp * 4 + k
                                    c_tr = P.inc("tensor", "pe", lambda e, tcl=tcl, fc=fc, k=k, tb=tb: e.transpose(
                                        TB[tb][:, k * 128:(k + 1) * 128],
                                        x2g[:, tcl * 2048 + fc * 128: tcl * 2048 + (fc + 1) * 128], i128b[:]))
                                P.wait("vector", "pe", c_tr)
                                dst = xt.ap()[:, grp * 4 * TG + tcl * 128:]
                                dst = bass.AP(tensor=dst.tensor, offset=dst.offset,
                                              ap=[[dst.ap[0][0], 128], [TG, 4], [1, 128]])
                                base = P.cnt["dv"]
                                for k in range(4):
                                    fc = grp * 4 + k
                                    c_ev = P.inc("vector", "dv", lambda e, xt=xt, fc=fc, tcl=tcl, k=k, tb=tb: e.tensor_scalar_mul(
                                        xt[:, fc * TG + tcl * 128: fc * TG + (tcl + 1) * 128],
                                        TB[tb][:, k * 128:(k + 1) * 128],
                                        su_c[:, fc:fc + 1]))
                                P.free_bank(f"tb{tb}", "dv", c_ev)
                        for oc in range(16):
                            for th in range(TG // 512):
                                bank = f"pb{oc % 4}"
                                pb = PB[oc % 4]
                                P.use_bank(bank)
                                c_pe = 0
                                for ic in range(16):
                                    hh = hp if H16S[ic, oc] > 0 else hn
                                    c_pe = P.inc("tensor", "pe", lambda e, pb=pb, hh=hh, ic=ic, th=th: e.matmul(
                                        pb[:], hh[:], xt[:, ic * TG + th * 512: ic * TG + (th + 1) * 512],
                                        start=(ic == 0), stop=(ic == 15)))
                                P.wait("vector", "pe", c_pe)
                                c_ev = P.inc("vector", "dv", lambda e, oc=oc, th=th, pb=pb: e.tensor_copy(
                                    h1[:, oc * TG + th * 512: oc * TG + (th + 1) * 512], pb[:]))
                                P.free_bank(bank, "dv", c_ev)
                        P.barrier(("tensor", "vector", "scalar", "gpsimd", "sync"))
                        P.flush()

                    with ExitStack() as ph_ug:
                        wupsl = [ph_ug.enter_context(nc.sbuf_tensor(f"wup{g}_{i}", [128, 16 * 256], BF16)) for i in range(2)]
                        wgtsl = [ph_ug.enter_context(nc.sbuf_tensor(f"wgt{g}_{i}", [128, 16 * 256], BF16)) for i in range(2)]
                        cons = {}
                        prev_sil = 0
                        for icp in range(32):
                            bidx = icp % 2
                            prevc = cons.get(icp - 2)
                            if prevc:
                                P.wait("gpsimd", "pe", prevc)
                            vu = P.dma("gpsimd", "Lg", wupsl[bidx].ap().rearrange("p (a b) -> p a b", a=16), wup_t[icp].rearrange("a p b -> p a b"))
                            vg = P.dma("gpsimd", "Lg", wgtsl[bidx].ap().rearrange("p (a b) -> p a b", a=16), wgt_t[icp].rearrange("a p b -> p a b"))
                            P.wait("tensor", "Lg", vg)
                            c_pe = 0
                            for ih in range(2):
                                ic = icp * 2 + ih
                                pu = PB[0 + (ic % 2) * 2]
                                pg = PB[1 + (ic % 2) * 2]
                                bu = f"pb{0 + (ic % 2) * 2}"
                                bgk = f"pb{1 + (ic % 2) * 2}"
                                P.use_bank(bu)
                                for fc in range(16):
                                    P.inc("tensor", "pe", lambda e, pu=pu, w=wupsl[bidx], fc=fc, ih=ih: e.matmul(
                                        pu[:], w[:, fc * 256 + ih * 128: fc * 256 + (ih + 1) * 128],
                                        h1[:, fc * TG: fc * TG + 512],
                                        start=(fc == 0), stop=(fc == 15)))
                                P.use_bank(bgk)
                                for fc in range(16):
                                    c_pe = P.inc("tensor", "pe", lambda e, pg=pg, w=wgtsl[bidx], fc=fc, ih=ih: e.matmul(
                                        pg[:], w[:, fc * 256 + ih * 128: fc * 256 + (ih + 1) * 128],
                                        h1[:, fc * TG: fc * TG + 512],
                                        start=(fc == 0), stop=(fc == 15)))
                                P.wait("scalar", "pe", c_pe)
                                P.wait("scalar", "dv", prev_sil)
                                c_si = P.inc("scalar", "ac", lambda e, pg=pg, ic=ic: e.activation(
                                    silu_sb[:], pg[:], AF.Silu, scale=sgt_c[:, ic:ic + 1]))
                                P.free_bank(bgk, "ac", c_si)
                                P.wait("vector", "ac", c_si)
                                c_a = P.inc("vector", "dv", lambda e, pu=pu, ic=ic: e.scalar_tensor_tensor(
                                    abuf[:, ic * 512:(ic + 1) * 512], pu[:], sup_c[:, ic:ic + 1],
                                    silu_sb[:], ALU.mult, ALU.mult))
                                P.free_bank(bu, "dv", c_a)
                                prev_sil = c_a
                            cons[icp] = c_pe
                        P.barrier(("gpsimd", "tensor", "vector", "scalar"))
                        P.flush()

                    dn_prefetch = {}
                    for icg0 in range(2):
                        v = P.dma("gpsimd", "Lg", wdsl[icg0].ap().rearrange("p (a b) -> p a b", a=4),
                                  wdn_t[0, icg0])
                        dn_prefetch[(0, icg0)] = v
                    with ExitStack() as ph_bf:
                        pong = ph_bf.enter_context(nc.sbuf_tensor(f"pong{g}", [128, 64 * 128], BF16))
                        for qt in range(TG // 128):
                            src_is_a = True
                            c_last = P.cnt["dv"]
                            for hstep in (32, 16, 8, 4, 2, 1):
                                npair = 64 // (2 * hstep)

                                def mk(eng_fn, dstbuf, srcbuf, qt=qt, hstep=hstep, npair=npair, off_s=0, off_d=0):
                                    pass
                                if src_is_a:
                                    def s_ap(cc0, qt=qt):
                                        return None
                                def apv(buf, start_c, qt=qt, hstep=hstep, npair=npair, width=None):
                                    base_ap = buf.ap()
                                    stride = TG if buf is abuf else 128
                                    off = base_ap.offset + start_c * stride + qt * 128 * (1 if buf is abuf else 0)
                                    if buf is abuf:
                                        off = base_ap.offset + start_c * TG + qt * 128
                                        return bass.AP(tensor=base_ap.tensor, offset=off,
                                                       ap=[[base_ap.ap[0][0], 128], [2 * hstep * TG, npair], [TG, hstep], [1, 128]])
                                    else:
                                        off = base_ap.offset + start_c * 128
                                        return bass.AP(tensor=base_ap.tensor, offset=off,
                                                       ap=[[base_ap.ap[0][0], 128], [2 * hstep * 128, npair], [128, hstep], [1, 128]])
                                src = abuf if src_is_a else pong
                                dstb = pong if src_is_a else abuf
                                a0 = apv(src, 0)
                                a1 = apv(src, hstep)
                                d0 = apv(dstb, 0)
                                d1 = apv(dstb, hstep)
                                c1 = P.inc("vector", "dv", lambda e, d0=d0, a0=a0, a1=a1: e.tensor_add(d0, a0, a1))
                                c_last = P.inc("vector", "dv", lambda e, d1=d1, a0=a0, a1=a1: e.tensor_sub(d1, a0, a1))
                                src_is_a = not src_is_a
                        P.barrier(("vector", "tensor"))
                        P.flush()

                    for c in range(64):
                        bank = f"pb{c % 4}"
                        pb = PB[c % 4]
                        P.use_bank(bank)
                        c_pe = P.inc("tensor", "pe", lambda e, pb=pb, c=c: e.matmul(
                            pb[:], hu[:], abuf[:, c * 512:(c + 1) * 512], start=True, stop=True))
                        P.wait("vector", "pe", c_pe)
                        c_ev = P.inc("vector", "dv", lambda e, pb=pb, c=c: e.tensor_copy(
                            abuf[:, c * 512:(c + 1) * 512], pb[:]))
                        P.free_bank(bank, "dv", c_ev)

                    with ExitStack() as ph_dn:
                        P.barrier(("tensor", "vector", "gpsimd"))
                        wdsl_hist = []
                        for og in range(4):
                            consumed = {}
                            for icg in range(16):
                                bidx = icg % 2
                                if (og, icg) in dn_prefetch:
                                    vw = dn_prefetch[(og, icg)]
                                else:
                                    if len(wdsl_hist) >= 2:
                                        P.wait("gpsimd", "pe", wdsl_hist[-2])
                                    vw = P.dma("gpsimd", "Lg", wdsl[bidx].ap().rearrange("p (a b) -> p a b", a=4),
                                               wdn_t[og, icg])
                                P.wait("tensor", "Lg", vw)
                                c_pe = 0
                                for ics in range(4):
                                    ic = icg * 4 + ics
                                    for tcl in range(4):
                                        pb = PB[tcl]
                                        if icg == 0 and ics == 0:
                                            P.use_bank(f"pb{tcl}")
                                        c_pe = P.inc("tensor", "pe", lambda e, pb=pb, bidx=bidx, ics=ics, ic=ic, tcl=tcl: e.matmul(
                                            pb[:], abuf[:, ic * 512 + tcl * 128: ic * 512 + (tcl + 1) * 128],
                                            wdsl[bidx][:, ics * 512:(ics + 1) * 512],
                                            start=(ic == 0), stop=(ic == 63)))
                                consumed[icg] = c_pe
                                wdsl_hist.append(c_pe)
                            P.wait("vector", "pe", P.cnt["pe"])
                            for tcl in range(4):
                                pb = PB[tcl]
                                tc = g * 4 + tcl
                                dst = x3pre[:, tc * 2048 + og * 512: tc * 2048 + (og + 1) * 512]
                                c_ev = P.inc("vector", "dv", lambda e, dst=dst, pb=pb, og=og: e.tensor_mul(
                                    dst, pb[:], sdp_tab[:, og * 512:(og + 1) * 512]))
                                P.free_bank(f"pb{tcl}", "dv", c_ev)
                        for tcl in range(4):
                            tc = g * 4 + tcl
                            sl = x3pre[:, tc * 2048:(tc + 1) * 2048]
                            c_a = P.inc("vector", "dv", lambda e, sl=sl, tcl=tcl: e.tensor_add(
                                lnsc2[:], sl, x2g[:, tcl * 2048:(tcl + 1) * 2048]))
                            c_s1 = P.inc("vector", "dv", lambda e, sl=sl: e.tensor_scalar(
                                sl, lnsc2[:], 0.0, None, ALU.add, ALU.add, accum_out=lnst2[:, 0:1]))
                            c_s2 = P.inc("vector", "dv", lambda e, sl=sl: e.scalar_tensor_tensor(
                                sl, lnsc2[:], 1.0, lnsc2[:], ALU.mult, ALU.mult, accum_out=lnst2[:, 1:2]))
                            P.wait("scalar", "dv", c_s2)
                            c_ms = P.inc("scalar", "ac", lambda e: e.mul(lnst2[:, 0:1], lnst2[:, 0:1], 1.0 / D))
                            c_qs = P.inc("scalar", "ac", lambda e: e.mul(lnst2[:, 1:2], lnst2[:, 1:2], 1.0 / D))
                            c_m2 = P.inc("scalar", "ac", lambda e: e.activation(lnst2[:, 2:3], lnst2[:, 0:1], AF.Square))
                            P.wait("vector", "ac", c_m2)
                            c_v = P.inc("vector", "dv", lambda e: e.tensor_sub(lnst2[:, 1:2], lnst2[:, 1:2], lnst2[:, 2:3]))
                            P.wait("scalar", "dv", c_v)
                            c_sq = P.inc("scalar", "ac", lambda e: e.activation(lnst2[:, 1:2], lnst2[:, 1:2], AF.Ln, bias=epsc[:]))
                            c_rc = P.inc("scalar", "ac", lambda e: e.activation(lnst2[:, 1:2], lnst2[:, 1:2], AF.Exp, scale=nhalf[:]))
                            P.wait("vector", "ac", c_rc)
                            c_ap = P.inc("vector", "dv", lambda e: e.tensor_scalar(
                                lnsc2[:], lnsc2[:], lnst2[:, 0:1], lnst2[:, 1:2], ALU.subtract, ALU.mult))
                            P.wait("scalar", "dv", c_ap)
                            vo = P.dma("scalar", "La", yout[tc * 128:(tc + 1) * 128, :], lnsc2[:])
                            P.wait("vector", "La", vo)
                        P.barrier()
                        P.flush()

        P.wait("sync", "La", P.cnt["La"])
        P.flush()

    return nc


_NC_CACHE = None


def kernel(**inputs):
    global _NC_CACHE
    ins = {k: np.asarray(v) for k, v in inputs.items()}
    x = ins["L0_attn_x"].reshape(S, D)

    def colmajor_ok(a):
        return np.ascontiguousarray(a, dtype=np.float32)

    wq = ins["L0_attn_wq"]; wk = ins["L0_attn_wk"]; wv = ins["L0_attn_wv"]
    wg = ins["L0_attn_wg"]; wo = ins["L0_attn_wo"]

    def tile_qkv(w):
        return np.ascontiguousarray(w.reshape(16, 128, 16, 128).transpose(2, 0, 1, 3))

    wq_t = tile_qkv(wq); wk_t = tile_qkv(wk); wv_t = tile_qkv(wv); wg_t = tile_qkv(wg)
    wo_t = np.ascontiguousarray(wo.reshape(16, 128, 4, 512).transpose(2, 0, 1, 3))
    wup_t = np.ascontiguousarray(ins["L0_ffn_W_up"].T.reshape(16, 128, 32, 256).transpose(2, 0, 1, 3))
    wgt_t = np.ascontiguousarray(ins["L0_ffn_W_gate"].T.reshape(16, 128, 32, 256).transpose(2, 0, 1, 3))
    wdn_t = np.ascontiguousarray(ins["L0_ffn_W_down"].T.reshape(16, 4, 128, 4, 512).transpose(3, 0, 2, 1, 4))

    H128 = _hadamard(128)
    h128p = (H128 / (SCALE * np.sqrt(np.float32(D)))).astype(BF)
    h128n = (-H128 / (SCALE * np.sqrt(np.float32(D)))).astype(BF)
    h128u = (H128 / (SCALE * np.sqrt(np.float32(INTER)))).astype(BF)

    if _NC_CACHE is None:
        _NC_CACHE = build_nc()
    nc = _NC_CACHE

    in_maps = []
    for c in range(NC):
        emask = (np.arange(NC) < c).astype(np.float32)
        eexp = np.maximum(c - 1 - np.arange(NC), 0).astype(np.float32)
        in_maps.append({
            "xin": np.ascontiguousarray(x[c * T:(c + 1) * T]),
            "wq_t": wq_t, "wk_t": wk_t, "wv_t": wv_t, "wg_t": wg_t, "wo_t": wo_t,
            "wup_t": wup_t, "wgt_t": wgt_t, "wdn_t": wdn_t,
            "bq": ins["L0_attn_bq"], "bk": ins["L0_attn_bk"], "bv": ins["L0_attn_bv"],
            "bg": ins["L0_attn_bg"], "bo": ins["L0_attn_bo"],
            "gnw": ins["L0_attn_gn_w"], "gnb": ins["L0_attn_gn_b"],
            "gam": ins["L0_attn_gamma"],
            "su_ug": ins["L0_ffn_SU_ug"], "su_dp": ins["L0_ffn_SU_dp"],
            "ws_ug": ins["L0_ffn_Wscale_ug"], "ws_dp": ins["L0_ffn_Wscale_dp"],
            "h128p": h128p, "h128n": h128n, "h128u": h128u,
            "emask": emask, "eexp": eexp,
        })
    res = run_bass_kernel_spmd(nc, in_maps, core_ids=list(range(NC)))
    out = np.concatenate([res.results[c]["y"] for c in range(NC)], axis=0)
    return out.reshape(1, S, D).astype(np.float32)


# version 15
# speedup vs baseline: 1.1121x; 1.0271x over previous
import numpy as np
import ml_dtypes
from contextlib import ExitStack

import concourse.bass as bass
import concourse.mybir as mybir
from concourse.bass_utils import run_bass_kernel_spmd

F32 = mybir.dt.float32
BF16 = mybir.dt.bfloat16
AF = mybir.ActivationFunctionType
ALU = mybir.AluOpType
BF = ml_dtypes.bfloat16

NC = 8
S = 8192
D = 2048
H = 16
DK = 128
T = S // NC
CH = 128
NCH = T // CH
INTER = 8192
SCALE = 64.0
EPS = 1e-5
TG = 512
NG = T // TG

ENGINES = ("sync", "gpsimd", "tensor", "vector", "scalar")


def _hadamard(n):
    h = np.array([[1.0]], np.float32)
    while h.shape[0] < n:
        h = np.block([[h, h], [h, -h]])
    return h


H16S = _hadamard(16)


class Prog:

    def __init__(self, nc, block, sems):
        self.nc = nc
        self.block = block
        self.sems = sems
        self.cnt = {k: 0 for k in sems}
        self.q = {e: [] for e in ENGINES}
        self.bank_free = {}

    def add(self, eng, fn):
        self.q[eng].append(fn)

    def wait(self, eng, sem, v):
        if v <= 0:
            return
        s = self.sems[sem]
        self.q[eng].append(lambda e, s=s, v=v: e.wait_ge(s, v))

    def inc(self, eng, sem, fn, n=1):
        self.cnt[sem] += n
        s = self.sems[sem]
        self.q[eng].append(lambda e, fn=fn, s=s, n=n: fn(e).then_inc(s, n))
        return self.cnt[sem]

    def run(self, eng, fn):
        self.q[eng].append(fn)

    def dma(self, eng, sem, out, in_, **kw):
        return self.inc(eng, sem, lambda e: e.dma_start(out=out, in_=in_, **kw), 16)

    def use_bank(self, bank):
        ent = self.bank_free.get(bank)
        if ent is not None:
            self.wait("tensor", ent[0], ent[1])

    def free_bank(self, bank, sem, v):
        self.bank_free[bank] = (sem, v)

    def barrier(self, engines=ENGINES):
        snap = dict(self.cnt)
        for e in engines:
            for sname, v in snap.items():
                if v > 0:
                    self.wait(e, sname, v)

    def flush(self):
        blk = self.block
        for eng in ENGINES:
            ops = self.q[eng]
            if not ops:
                continue
            deco = getattr(blk, eng)

            def body(e, ops=ops):
                for f in ops:
                    f(e)
            deco(body)
            self.q[eng] = []


def build_nc():
    nc = bass.Bass("TRN2", target_bir_lowering=False, debug=False)

    xin = nc.dram_tensor("xin", [T, D], F32, kind="ExternalInput").ap()
    wq_t = nc.dram_tensor("wq_t", [H, 16, 128, 128], F32, kind="ExternalInput").ap()
    wk_t = nc.dram_tensor("wk_t", [H, 16, 128, 128], F32, kind="ExternalInput").ap()
    wv_t = nc.dram_tensor("wv_t", [H, 16, 128, 128], F32, kind="ExternalInput").ap()
    wg_t = nc.dram_tensor("wg_t", [16, 16, 128, 128], F32, kind="ExternalInput").ap()
    wo_t = nc.dram_tensor("wo_t", [4, 16, 128, 512], F32, kind="ExternalInput").ap()
    wup_t = nc.dram_tensor("wup_t", [32, 16, 128, 256], F32, kind="ExternalInput").ap()
    wgt_t = nc.dram_tensor("wgt_t", [32, 16, 128, 256], F32, kind="ExternalInput").ap()
    wdn_t = nc.dram_tensor("wdn_t", [4, 16, 128, 4, 512], F32, kind="ExternalInput").ap()
    bq = nc.dram_tensor("bq", [D], F32, kind="ExternalInput").ap()
    bk = nc.dram_tensor("bk", [D], F32, kind="ExternalInput").ap()
    bv = nc.dram_tensor("bv", [D], F32, kind="ExternalInput").ap()
    bg = nc.dram_tensor("bg", [D], F32, kind="ExternalInput").ap()
    bo = nc.dram_tensor("bo", [D], F32, kind="ExternalInput").ap()
    gnw = nc.dram_tensor("gnw", [D], F32, kind="ExternalInput").ap()
    gnb = nc.dram_tensor("gnb", [D], F32, kind="ExternalInput").ap()
    gam = nc.dram_tensor("gam", [H], F32, kind="ExternalInput").ap()
    su_ug = nc.dram_tensor("su_ug", [D], F32, kind="ExternalInput").ap()
    su_dp = nc.dram_tensor("su_dp", [INTER], F32, kind="ExternalInput").ap()
    ws_ug = nc.dram_tensor("ws_ug", [2 * INTER], F32, kind="ExternalInput").ap()
    ws_dp = nc.dram_tensor("ws_dp", [D], F32, kind="ExternalInput").ap()
    h128p = nc.dram_tensor("h128p", [128, 128], BF16, kind="ExternalInput").ap()
    h128n = nc.dram_tensor("h128n", [128, 128], BF16, kind="ExternalInput").ap()
    h128u = nc.dram_tensor("h128u", [128, 128], BF16, kind="ExternalInput").ap()
    emask = nc.dram_tensor("emask", [NC], F32, kind="ExternalInput").ap()
    eexp = nc.dram_tensor("eexp", [NC], F32, kind="ExternalInput").ap()
    yout = nc.dram_tensor("y", [T, D], F32, kind="ExternalOutput").ap()

    x2d = nc.dram_tensor("x2d", [T, D], BF16).ap()
    dsc = nc.dram_tensor("dsc", [256], F32).ap()
    cc_si = [nc.dram_tensor(f"cc_si{h}", [128, 128], F32) for h in range(H)]
    cc_so = [nc.dram_tensor(f"cc_so{h}", [NC * 128, 128], F32) for h in range(H)]
    cc_ti = nc.dram_tensor("cc_ti", [32], F32)
    cc_to = nc.dram_tensor("cc_to", [32], F32)

    with ExitStack() as outer:
        ent = outer.enter_context
        semnames = ["Ls", "Lg", "La", "pe", "dv", "ac", "gp", "cc"]
        sems = {n: ent(nc.semaphore(n)) for n in semnames}

        PB = [ent(nc.psum_tensor(f"pb{i}", [128, 512], F32)) for i in range(6)]
        TB = [ent(nc.psum_tensor(f"tb{i}", [128, 1024], BF16)) for i in range(2)]

        ones1 = ent(nc.sbuf_tensor("ones1", [1, 512], F32))
        onesb = ent(nc.sbuf_tensor("onesb", [128, 1], F32))
        epsc = ent(nc.sbuf_tensor("epsc", [128, 1], F32))
        nhalf = ent(nc.sbuf_tensor("nhalf", [128, 1], F32))
        i128b = ent(nc.sbuf_tensor("i128b", [128, 128], BF16))
        hp = ent(nc.sbuf_tensor("hp", [128, 128], BF16))
        hn = ent(nc.sbuf_tensor("hn", [128, 128], BF16))
        hu = ent(nc.sbuf_tensor("hu", [128, 128], BF16))
        io_jmp = ent(nc.sbuf_tensor("io_jmp", [128, 128], F32))
        io_jp1 = ent(nc.sbuf_tensor("io_jp1", [128, 128], F32))
        io_kd = ent(nc.sbuf_tensor("io_kd", [128, 1], F32))
        lng_row = ent(nc.sbuf_tensor("lng_row", [1, 16], F32))
        lng_cols = ent(nc.sbuf_tensor("lng_cols", [128, 16], F32))
        kdec_cols = ent(nc.sbuf_tensor("kdec_cols", [128, 16], F32))
        gc_cols = ent(nc.sbuf_tensor("gc_cols", [128, 16], F32))
        wrep = ent(nc.sbuf_tensor("wrep", [128, 128], F32))
        bq_c = ent(nc.sbuf_tensor("bq_c", [128, 16], F32))
        bk_c = ent(nc.sbuf_tensor("bk_c", [128, 16], F32))
        bv_c = ent(nc.sbuf_tensor("bv_c", [128, 16], F32))
        bg_c = ent(nc.sbuf_tensor("bg_c", [128, 16], F32))
        gnw_c = ent(nc.sbuf_tensor("gnw_c", [128, 16], F32))
        gnb_c = ent(nc.sbuf_tensor("gnb_c", [128, 16], F32))
        su_c = ent(nc.sbuf_tensor("su_c", [128, 16], F32))
        sup_c = ent(nc.sbuf_tensor("sup_c", [128, 64], F32))
        sgt_c = ent(nc.sbuf_tensor("sgt_c", [128, 64], F32))
        row16 = ent(nc.sbuf_tensor("row16", [1, 16], F32))
        row128 = ent(nc.sbuf_tensor("row128", [1, 128], F32))
        stat16 = ent(nc.sbuf_tensor("stat16", [16, 8], F32))
        gsc_c = ent(nc.sbuf_tensor("gsc_c", [128, 16], F32))
        gbi_c = ent(nc.sbuf_tensor("gbi_c", [128, 16], F32))
        accs = ent(nc.sbuf_tensor("accs", [128, 32], F32))
        a32 = ent(nc.sbuf_tensor("a32", [32, 4], F32))

        block = ent(nc.Block())
        P = Prog(nc, block, sems)

        v_hp = P.dma("sync", "Ls", hp[:], h128p[:, :])
        v_hn = P.dma("sync", "Ls", hn[:], h128n[:, :])
        v_hu = P.dma("sync", "Ls", hu[:], h128u[:, :])
        colpairs = [(bq_c, bq), (bk_c, bk), (bv_c, bv), (bg_c, bg),
                    (gnw_c, gnw), (gnb_c, gnb), (su_c, su_ug)]
        v_cols = 0
        for dst, src in colpairs:
            v_cols = P.dma("sync", "Ls", dst[:], src.rearrange("(c p) -> p c", p=128),
                           allow_slow_non_contiguous=True)
        v_sup = P.dma("sync", "Ls", sup_c[:], ws_ug[0:INTER].rearrange("(c p) -> p c", p=128),
                      allow_slow_non_contiguous=True)
        v_sgt = P.dma("sync", "Ls", sgt_c[:], ws_ug[INTER:2 * INTER].rearrange("(c p) -> p c", p=128),
                      allow_slow_non_contiguous=True)
        sudp_c = sup_c
        v_lngrow = P.dma("sync", "Ls", row16[0:1, :], gam.rearrange("(o f) -> o f", o=1))
        v_em = P.dma("sync", "Ls", row128[0:1, 0:NC], emask.rearrange("(o f) -> o f", o=1))
        v_ee = P.dma("sync", "Ls", row128[0:1, 8:8 + NC], eexp.rearrange("(o f) -> o f", o=1))

        grow = ent(nc.sbuf_tensor("grow", [16, 128], F32))
        gbrow = ent(nc.sbuf_tensor("gbrow", [16, 128], F32))
        srow = ent(nc.sbuf_tensor("srow", [16, 128], F32))
        brow = ent(nc.sbuf_tensor("brow", [16, 128], F32))
        v_grow = P.dma("sync", "Ls", grow[:], gnw.rearrange("(p f) -> p f", p=16))
        v_gbrow = P.dma("sync", "Ls", gbrow[:], gnb.rearrange("(p f) -> p f", p=16))
        sudp_cols = ent(nc.sbuf_tensor("sudp_cols", [128, 64], F32))
        v_sudp = P.dma("sync", "Ls", sudp_cols[:], su_dp.rearrange("(c p) -> p c", p=128),
                       allow_slow_non_contiguous=True)

        c_ones = P.inc("gpsimd", "gp", lambda e: e.memset(ones1[:], 1.0))
        c_onesb = P.inc("gpsimd", "gp", lambda e: e.memset(onesb[:], 1.0))
        c_eps = P.inc("gpsimd", "gp", lambda e: e.memset(epsc[:], EPS))
        c_nh = P.inc("gpsimd", "gp", lambda e: e.memset(nhalf[:], -0.5))
        c_iojmp = P.inc("gpsimd", "gp", lambda e: e.iota(
            io_jmp[:], pattern=[[1, 128]], base=0, channel_multiplier=-1,
            allow_small_or_imprecise_dtypes=True))
        c_iojp1 = P.inc("gpsimd", "gp", lambda e: e.iota(
            io_jp1[:], pattern=[[1, 128]], base=1, channel_multiplier=0,
            allow_small_or_imprecise_dtypes=True))
        c_iokd = P.inc("gpsimd", "gp", lambda e: e.iota(
            io_kd[:], pattern=[[1, 1]], base=127, channel_multiplier=-1,
            allow_small_or_imprecise_dtypes=True))
        idf = ent(nc.sbuf_tensor("idf", [128, 128], F32))
        c_idf0 = P.inc("gpsimd", "gp", lambda e: e.memset(idf[:], 1.0))
        c_idf = P.inc("gpsimd", "gp", lambda e: e.affine_select(
            idf[:], idf[:], pattern=[[1, 128]], compare_op=ALU.is_equal,
            fill=0.0, base=0, channel_multiplier=-1))
        P.wait("vector", "gp", c_idf)
        c_i128b = P.inc("vector", "dv", lambda e: e.tensor_copy(i128b[:], idf[:]))

        P.wait("scalar", "Ls", v_lngrow)
        c_lng = P.inc("scalar", "ac", lambda e: e.activation(lng_row[0:1, :], row16[0:1, :], AF.Ln))
        P.wait("tensor", "gp", c_ones)
        P.wait("tensor", "ac", c_lng)
        P.use_bank("pb0")
        c_mm1 = P.inc("tensor", "pe", lambda e: e.matmul(
            PB[0][:, 0:16], ones1[0:1, 0:128], lng_row[0:1, :], start=True, stop=True))
        P.wait("vector", "pe", c_mm1)
        c_lngc = P.inc("vector", "dv", lambda e: e.tensor_copy(lng_cols[:], PB[0][:, 0:16]))
        P.free_bank("pb0", "dv", c_lngc)

        P.wait("scalar", "gp", c_iokd)
        P.wait("scalar", "dv", c_lngc)
        c_kd = 0
        for h in range(H):
            c_kd = P.inc("scalar", "ac", lambda e, h=h: e.activation(
                kdec_cols[:, h:h + 1], io_kd[:], AF.Exp, scale=lng_cols[:, h:h + 1]))
        c_l128 = P.inc("vector", "dv", lambda e: e.tensor_scalar_mul(row16[0:1, :], lng_row[0:1, :], 128.0))
        P.wait("scalar", "dv", c_l128)
        c_g128 = P.inc("scalar", "ac", lambda e: e.activation(row16[0:1, :], row16[0:1, :], AF.Exp))
        P.wait("tensor", "ac", c_g128)
        P.use_bank("pb0")
        c_mm2 = P.inc("tensor", "pe", lambda e: e.matmul(
            PB[0][:, 0:16], ones1[0:1, 0:128], row16[0:1, :], start=True, stop=True))
        P.wait("vector", "pe", c_mm2)
        c_gcc = P.inc("vector", "dv", lambda e: e.tensor_copy(gc_cols[:], PB[0][:, 0:16]))
        P.free_bank("pb0", "dv", c_gcc)

        lng16 = ent(nc.sbuf_tensor("lng16", [16, 1], F32))
        P.wait("sync", "Ls", v_lngrow)
        v_g16 = P.dma("sync", "Ls", lng16[:], gam.rearrange("(p o) -> p o", o=1))
        P.wait("scalar", "Ls", v_g16)
        c_lng16 = P.inc("scalar", "ac", lambda e: e.activation(lng16[:], lng16[:], AF.Ln))
        P.wait("vector", "ac", c_lng16)
        c_lng16b = P.inc("vector", "dv", lambda e: e.tensor_scalar_mul(lng16[:], lng16[:], 1024.0))
        P.wait("tensor", "Ls", v_ee)
        P.use_bank("pb1")
        c_mm3 = P.inc("tensor", "pe", lambda e: e.matmul(
            PB[1][0:16, 0:8], ones1[0:1, 0:16], row128[0:1, 8:8 + NC], start=True, stop=True))
        c_mm3b = P.inc("tensor", "pe", lambda e: e.matmul(
            PB[1][0:16, 8:16], ones1[0:1, 0:16], row128[0:1, 0:NC], start=True, stop=True))
        P.wait("scalar", "pe", c_mm3b)
        P.wait("scalar", "dv", c_lng16b)
        c_w16 = P.inc("scalar", "ac", lambda e: e.activation(
            stat16[:, 0:8], PB[1][0:16, 0:8], AF.Exp, scale=lng16[:]))
        P.wait("vector", "ac", c_w16)
        c_w16b = P.inc("vector", "dv", lambda e: e.tensor_mul(stat16[:, 0:8], stat16[:, 0:8], PB[1][0:16, 8:16]))
        P.free_bank("pb1", "dv", c_w16b)
        P.wait("gpsimd", "dv", c_w16b)
        v_fl1 = P.dma("gpsimd", "Lg", dsc[0:128].rearrange("(p f) -> p f", p=16), stat16[:, 0:8],
                      allow_slow_non_contiguous=True)
        P.wait("gpsimd", "Lg", v_fl1)
        v_fl2 = P.dma("gpsimd", "Lg", row128[0:1, :], dsc[0:128].rearrange("(o f) -> o f", o=1))
        P.wait("tensor", "Lg", v_fl2)
        P.use_bank("pb1")
        c_mm4 = P.inc("tensor", "pe", lambda e: e.matmul(
            PB[1][:, 0:128], ones1[0:1, 0:128], row128[0:1, :], start=True, stop=True))
        P.wait("vector", "pe", c_mm4)
        c_wrep = P.inc("vector", "dv", lambda e: e.tensor_copy(wrep[:], PB[1][:, 0:128]))
        P.free_bank("pb1", "dv", c_wrep)

        P.wait("vector", "Ls", max(v_sup, v_sgt, v_sudp))
        c_sup = P.inc("vector", "dv", lambda e: e.tensor_scalar_mul(sup_c[:], sup_c[:], SCALE))
        c_sup2 = P.inc("vector", "dv", lambda e: e.tensor_mul(sup_c[:], sup_c[:], sudp_cols[:]))
        c_sgt = P.inc("vector", "dv", lambda e: e.tensor_scalar_mul(sgt_c[:], sgt_c[:], SCALE))

        P.barrier()
        P.flush()

        with ExitStack() as ph_g:
            g_fm = ph_g.enter_context(nc.sbuf_tensor("g_fm", [128, H * 1024], BF16))
            wosl = ph_g.enter_context(nc.sbuf_tensor("wosl", [128, 16 * 512], BF16))
            bo_tab = ph_g.enter_context(nc.sbuf_tensor("bo_tab", [128, 2048], BF16))
            ret = ph_g.enter_context(nc.sbuf_tensor("ret", [128, H * 1024], BF16))
            sin = ph_g.enter_context(nc.sbuf_tensor("sin", [128, H * 128], F32))

            with ExitStack() as ph_a:
                ea = ph_a.enter_context
                x_fm = ea(nc.sbuf_tensor("x_fm", [128, 16 * 1024], BF16))
                qr = ea(nc.sbuf_tensor("qr", [128, 4 * 1024], BF16))
                kr = ea(nc.sbuf_tensor("kr", [128, 4 * 1024], BF16))
                vr = ea(nc.sbuf_tensor("vr", [128, 4 * 1024], BF16))
                vfm = ea(nc.sbuf_tensor("vfm", [128, 1024], BF16))
                wsl = [ea(nc.sbuf_tensor(f"wsl{i}", [128, 2048], BF16)) for i in range(6)]
                xst = ea(nc.sbuf_tensor("xst", [128, 2048], F32))
                lamb = [ea(nc.sbuf_tensor(f"lam{i}", [128, 128], F32)) for i in range(2)]
                dtab = [ea(nc.sbuf_tensor(f"dtab{i}", [128, 128], F32)) for i in range(2)]
                qdec = [ea(nc.sbuf_tensor(f"qdec{i}", [128, 128], BF16)) for i in range(2)]
                srun = ea(nc.sbuf_tensor("srun", [128, 128], F32))
                khA = ea(nc.sbuf_tensor("khA", [128, 1024], BF16))
                khF = ea(nc.sbuf_tensor("khF", [128, 1024], BF16))
                scmb = ea(nc.sbuf_tensor("scmb", [128, 1024], BF16))
                qdqb = ea(nc.sbuf_tensor("qdqb", [128, 1024], BF16))
                sbst = ea(nc.sbuf_tensor("sbst", [128, 8 * 128], BF16))
                gbuf = ea(nc.sbuf_tensor("gbuf", [128, 512], F32))

                for tc in range(8):
                    vx = P.dma("sync", "Ls", xst[:], xin[tc * 128:(tc + 1) * 128, :])
                    P.wait("tensor", "Ls", vx)
                    for grp in range(4):
                        bank = f"pb{2 + grp % 2}"
                        pb = PB[2 + grp % 2]
                        P.use_bank(bank)
                        c_tr = 0
                        for k in range(4):
                            fc = grp * 4 + k
                            c_tr = P.inc("tensor", "pe", lambda e, fc=fc, k=k, pb=pb: e.transpose(
                                pb[:, k * 128:(k + 1) * 128],
                                xst[:, fc * 128:(fc + 1) * 128], idf[:]))
                        P.wait("vector", "pe", c_tr)
                        dst = x_fm.ap()[:, grp * 4 * 1024 + tc * 128:]
                        dst = bass.AP(tensor=dst.tensor, offset=dst.offset,
                                      ap=[[dst.ap[0][0], 128], [1024, 4], [1, 128]])
                        c_ev = P.inc("vector", "dv", lambda e, dst=dst, pb=pb: e.tensor_copy(dst, pb[:, 0:512].rearrange("p (a b) -> p a b", a=4)))
                        P.free_bank(bank, "dv", c_ev)
                    P.wait("sync", "pe", P.cnt["pe"])
                P.wait("tensor", "dv", P.cnt["dv"])

                LAG = 2
                fullpass_state = {}

                def emit_wload(h):
                    if h >= H:
                        return
                    bufs = (wsl[0 + (h % 2)], wsl[2 + (h % 2)], wsl[4 + (h % 2)])
                    prev = fullpass_state.get(("wsl", h - 2))
                    if prev:
                        P.wait("gpsimd", "pe", prev)
                    vq = P.dma("gpsimd", "Lg", bufs[0].ap().rearrange("p (a b) -> p a b", a=16), wq_t[h].rearrange("a p b -> p a b"))
                    vk = P.dma("gpsimd", "Lg", bufs[1].ap().rearrange("p (a b) -> p a b", a=16), wk_t[h].rearrange("a p b -> p a b"))
                    vv = P.dma("gpsimd", "Lg", bufs[2].ap().rearrange("p (a b) -> p a b", a=16), wv_t[h].rearrange("a p b -> p a b"))
                    fullpass_state[("wld", h)] = (vq, vk, vv)

                def emit_proj(h):
                    slot = h % 4
                    bufs = (wsl[0 + (h % 2)], wsl[2 + (h % 2)], wsl[4 + (h % 2)])
                    vq, vk, vv = fullpass_state[("wld", h)]
                    prev2 = fullpass_state.get(("fp", h - 4))
                    if prev2:
                        P.wait("tensor", "pe", prev2)
                    c_pe_last = 0
                    for which, (wbuf, vdone, dst, bias) in enumerate(
                            [(bufs[0], vq, qr, bq_c), (bufs[1], vk, kr, bk_c), (bufs[2], vv, vfm, bv_c)]):
                        P.wait("tensor", "Lg", vdone)
                        for th in range(2):
                            bank = f"pb{(which * 2 + th) % 2}"
                            pb = PB[(which * 2 + th) % 2]
                            P.use_bank(bank)
                            for fc in range(16):
                                c_pe_last = P.inc("tensor", "pe", lambda e, pb=pb, wbuf=wbuf, fc=fc, th=th: e.matmul(
                                    pb[:], wbuf[:, fc * 128:(fc + 1) * 128],
                                    x_fm[:, fc * 1024 + th * 512: fc * 1024 + (th + 1) * 512],
                                    start=(fc == 0), stop=(fc == 15)))
                            P.wait("scalar", "pe", c_pe_last)
                            if which < 2:
                                out_ap = dst[:, slot * 1024 + th * 512: slot * 1024 + (th + 1) * 512]
                            else:
                                out_ap = dst[:, th * 512:(th + 1) * 512]
                            c_ev = P.inc("scalar", "ac", lambda e, out_ap=out_ap, pb=pb, bias=bias, h=h: e.activation(
                                out_ap, pb[:], AF.Identity, bias=bias[:, h:h + 1]))
                            P.free_bank(bank, "ac", c_ev)
                    c_evv = 0
                    for grp in range(2):
                        tb = grp % 2
                        P.use_bank(f"tb{tb}")
                        c_tr = 0
                        for k in range(4):
                            ci = grp * 4 + k
                            P.wait("tensor", "ac", P.cnt["ac"])
                            c_tr = P.inc("tensor", "pe", lambda e, ci=ci, k=k, tb=tb: e.transpose(
                                TB[tb][:, k * 128:(k + 1) * 128],
                                vfm[:, ci * 128:(ci + 1) * 128], i128b[:]))
                        P.wait("scalar", "pe", c_tr)
                        dst = vr.ap()[:, slot * 1024 + grp * 512:]
                        dst = bass.AP(tensor=dst.tensor, offset=dst.offset,
                                      ap=[[dst.ap[0][0], 128], [128, 4], [1, 128]])
                        c_evv = P.inc("scalar", "ac", lambda e, dst=dst, tb=tb: e.copy(
                            dst, TB[tb][:, 0:512].rearrange("p (a b) -> p a b", a=4)))
                        P.free_bank(f"tb{tb}", "ac", c_evv)
                    fullpass_state[("wsl", h)] = c_pe_last
                    fullpass_state[("proj", h)] = c_evv

                def emit_passA_khat(h, ast):
                    slot = h % 4
                    pv = fullpass_state.get(("vput", h - 2))
                    if pv:
                        P.wait("vector", "Lg", pv)
                    P.wait("tensor", "ac", fullpass_state[("proj", h)])
                    P.use_bank("tb0")
                    c_tr = 0
                    for ci in range(NCH):
                        c_tr = P.inc("tensor", "pe", lambda e, h=h, ci=ci, slot=slot: e.transpose(
                            TB[0][:, ci * 128:(ci + 1) * 128],
                            kr[:, slot * 1024 + ci * 128: slot * 1024 + (ci + 1) * 128], i128b[:]))
                    P.wait("scalar", "pe", c_tr)
                    c_kh = P.inc("scalar", "ac", lambda e, h=h: e.activation(
                        khA[:], TB[0][:], AF.Identity, scale=kdec_cols[:, h:h + 1]))
                    P.free_bank("tb0", "ac", c_kh)
                    ast["kh"] = c_kh

                def emit_passA_ds(h, ast):
                    slot = h % 4
                    lam = lamb[h % 2]
                    P.wait("tensor", "ac", ast["kh"])
                    c_su = 0
                    for rnd in range(2):
                        P.use_bank("pb5")
                        c_ds = 0
                        for k in range(4):
                            ci = rnd * 4 + k
                            c_ds = P.inc("tensor", "pe", lambda e, h=h, ci=ci, k=k, slot=slot: e.matmul(
                                PB[5][:, k * 128:(k + 1) * 128], khA[:, ci * 128:(ci + 1) * 128],
                                vr[:, slot * 1024 + ci * 128: slot * 1024 + (ci + 1) * 128],
                                start=True, stop=True))
                        P.wait("vector", "pe", c_ds)
                        for k in range(4):
                            ci = rnd * 4 + k
                            if ci == 0:
                                c_su = P.inc("vector", "dv", lambda e, lam=lam, k=k: e.tensor_copy(
                                    lam[:], PB[5][:, k * 128:(k + 1) * 128]))
                            else:
                                c_su = P.inc("vector", "dv", lambda e, lam=lam, h=h, k=k: e.scalar_tensor_tensor(
                                    lam[:], lam[:], gc_cols[:, h:h + 1], PB[5][:, k * 128:(k + 1) * 128],
                                    ALU.mult, ALU.add))
                        P.free_bank("pb5", "dv", c_su)
                    ast["su"] = c_su

                def emit_passA_cc(h, ast):
                    lam = lamb[h % 2]
                    P.wait("gpsimd", "dv", ast["su"])
                    vput = P.dma("gpsimd", "Lg", cc_si[h].ap(), lam[:])
                    fullpass_state[("vput", h)] = vput
                    P.wait("gpsimd", "Lg", vput)
                    c_cc = P.inc("gpsimd", "cc", lambda e, h=h: e.collective_compute(
                        "AllGather", ALU.bypass, replica_groups=[list(range(NC))],
                        ins=[cc_si[h].ap().opt()], outs=[cc_so[h].ap().opt()]))
                    fullpass_state[("cc", h)] = c_cc

                def emit_fullpass_batches(h, fst):
                    slot = h % 4
                    dt = dtab[h % 2]
                    qd = qdec[h % 2]
                    prev_fp = fullpass_state.get(("fp", h - 1), 0)
                    prev = fullpass_state.get(("dt", h - 2))
                    if prev:
                        P.wait("scalar", "dv", prev)
                    P.wait("scalar", "pe", prev_fp)
                    c_d1 = P.inc("scalar", "ac", lambda e, dt=dt, h=h: e.activation(
                        dt[:], io_jmp[:], AF.Exp, scale=lng_cols[:, h:h + 1]))
                    P.wait("gpsimd", "ac", c_d1)
                    c_d2 = P.inc("gpsimd", "gp", lambda e, dt=dt: e.affine_select(
                        dt[:], dt[:], pattern=[[1, 128]], compare_op=ALU.is_ge,
                        fill=0.0, base=0, channel_multiplier=-1))
                    c_q1 = P.inc("scalar", "ac", lambda e, qd=qd, h=h: e.activation(
                        qd[:], io_jp1[:], AF.Exp, scale=lng_cols[:, h:h + 1]))
                    P.wait("sync", "cc", fullpass_state[("cc", h)])
                    P.wait("sync", "dv", fullpass_state.get(("si", h - 1), 0))
                    P.wait("vector", "pe", prev_fp)
                    c_z = P.inc("vector", "dv", lambda e, h=h: e.memset(sin[:, h * 128:(h + 1) * 128], 0.0))
                    c_si = c_z
                    for half in range(2):
                        src_ap = cc_so[h].ap()[half * 512:(half + 1) * 512, :]
                        vgb = P.dma("sync", "Ls", gbuf.ap().rearrange("p (c dv) -> p c dv", c=4),
                                    src_ap.rearrange("(c dk) dv -> dk c dv", dk=128))
                        P.wait("vector", "Ls", vgb)
                        for cp4 in range(4):
                            cp = half * 4 + cp4
                            c_si = P.inc("vector", "dv", lambda e, h=h, cp=cp, cp4=cp4: e.scalar_tensor_tensor(
                                sin[:, h * 128:(h + 1) * 128], gbuf[:, cp4 * 128:(cp4 + 1) * 128],
                                wrep[:, h * 8 + cp: h * 8 + cp + 1], sin[:, h * 128:(h + 1) * 128],
                                ALU.mult, ALU.add))
                        if half == 0:
                            P.wait("sync", "dv", c_si)
                    fullpass_state[("si", h)] = c_si
                    P.wait("vector", "ac", fullpass_state.get(("stc", h - 1), 0))
                    c_sr = P.inc("vector", "dv", lambda e, h=h: e.tensor_copy(srun[:], sin[:, h * 128:(h + 1) * 128]))
                    P.wait("scalar", "dv", c_sr)
                    c_sb0 = P.inc("scalar", "ac", lambda e: e.activation(sbst[:, 0:128], srun[:], AF.Copy))
                    c_scm = 0
                    for rnd in range(2):
                        P.use_bank("pb2")
                        if rnd == 0:
                            P.wait("tensor", "gp", c_d2)
                        c_s = 0
                        for k in range(4):
                            ci = rnd * 4 + k
                            qsl = qr[:, slot * 1024 + ci * 128: slot * 1024 + (ci + 1) * 128]
                            ksl = kr[:, slot * 1024 + ci * 128: slot * 1024 + (ci + 1) * 128]
                            c_s = P.inc("tensor", "pe", lambda e, ksl=ksl, qsl=qsl, k=k: e.matmul(
                                PB[2][:, k * 128:(k + 1) * 128], ksl, qsl, start=True, stop=True))
                        P.wait("vector", "pe", c_s)
                        for k in range(4):
                            ci = rnd * 4 + k
                            c_scm = P.inc("vector", "dv", lambda e, dt=dt, ci=ci, k=k: e.tensor_mul(
                                scmb[:, ci * 128:(ci + 1) * 128], PB[2][:, k * 128:(k + 1) * 128], dt[:]))
                        P.free_bank("pb2", "dv", c_scm)
                    P.wait("vector", "ac", c_q1)
                    c_qd = 0
                    for ci in range(NCH):
                        qsl = qr[:, slot * 1024 + ci * 128: slot * 1024 + (ci + 1) * 128]
                        c_qd = P.inc("vector", "dv", lambda e, qsl=qsl, qd=qd, ci=ci: e.tensor_mul(
                            qdqb[:, ci * 128:(ci + 1) * 128], qsl, qd[:]))
                    P.use_bank("tb1")
                    c_tr = 0
                    for ci in range(NCH - 1):
                        ksl = kr[:, slot * 1024 + ci * 128: slot * 1024 + (ci + 1) * 128]
                        c_tr = P.inc("tensor", "pe", lambda e, ksl=ksl, ci=ci: e.transpose(
                            TB[1][:, ci * 128:(ci + 1) * 128], ksl, i128b[:]))
                    P.wait("scalar", "pe", c_tr)
                    c_kh = P.inc("scalar", "ac", lambda e, h=h: e.activation(
                        khF[:, 0:896], TB[1][:, 0:896], AF.Identity, scale=kdec_cols[:, h:h + 1]))
                    P.free_bank("tb1", "ac", c_kh)
                    fst.update(kh=c_kh, sb0=c_sb0, qd=c_qd, scm=c_scm)

                def emit_fullpass_chain(h, fst):
                    slot = h % 4
                    P.wait("tensor", "ac", fst["kh"])
                    c_stc = fst["sb0"]
                    for rnd in range(2):
                        nk = 4 if rnd == 0 else 3
                        P.use_bank("pb4")
                        c_ds = 0
                        for k in range(nk):
                            ci = rnd * 4 + k
                            vsl = vr[:, slot * 1024 + ci * 128: slot * 1024 + (ci + 1) * 128]
                            c_ds = P.inc("tensor", "pe", lambda e, h=h, ci=ci, k=k, vsl=vsl: e.matmul(
                                PB[4][:, k * 128:(k + 1) * 128], khF[:, ci * 128:(ci + 1) * 128],
                                vsl, start=True, stop=True))
                        P.wait("vector", "pe", c_ds)
                        c_su2 = 0
                        for k in range(nk):
                            ci = rnd * 4 + k
                            P.wait("vector", "ac", c_stc)
                            c_su2 = P.inc("vector", "dv", lambda e, h=h, k=k: e.scalar_tensor_tensor(
                                srun[:], srun[:], gc_cols[:, h:h + 1], PB[4][:, k * 128:(k + 1) * 128],
                                ALU.mult, ALU.add))
                            P.wait("scalar", "dv", c_su2)
                            c_stc = P.inc("scalar", "ac", lambda e, ci=ci: e.activation(
                                sbst[:, (ci + 1) * 128:(ci + 2) * 128], srun[:], AF.Copy))
                        P.free_bank("pb4", "dv", c_su2)
                    fst["stc"] = c_stc

                def emit_fullpass_ret(h, fst):
                    slot = h % 4
                    c_stc = fst["stc"]
                    P.wait("tensor", "ac", c_stc)
                    P.wait("tensor", "dv", max(fst["qd"], fst["scm"]))
                    last_pe = 0
                    c_rv = 0
                    for rnd in range(2):
                        P.use_bank("pb3")
                        for k in range(4):
                            ci = rnd * 4 + k
                            vsl = vr[:, slot * 1024 + ci * 128: slot * 1024 + (ci + 1) * 128]
                            P.inc("tensor", "pe", lambda e, vsl=vsl, ci=ci, k=k: e.matmul(
                                PB[3][:, k * 128:(k + 1) * 128], scmb[:, ci * 128:(ci + 1) * 128],
                                vsl, start=True, stop=False))
                            last_pe = P.inc("tensor", "pe", lambda e, ci=ci, k=k: e.matmul(
                                PB[3][:, k * 128:(k + 1) * 128], qdqb[:, ci * 128:(ci + 1) * 128],
                                sbst[:, ci * 128:(ci + 1) * 128], start=False, stop=True))
                        P.wait("scalar", "pe", last_pe)
                        c_rv = P.inc("scalar", "ac", lambda e, h=h, rnd=rnd: e.copy(
                            ret[:, h * 1024 + rnd * 512: h * 1024 + (rnd + 1) * 512], PB[3][:]))
                        P.free_bank("pb3", "ac", c_rv)
                    P.wait("vector", "ac", c_rv)
                    c_st1 = P.inc("vector", "dv", lambda e, h=h: e.tensor_scalar(
                        xst[:, 0:1024], ret[:, h * 1024:(h + 1) * 1024], 0.0, None, ALU.add, ALU.add,
                        accum_out=accs[:, h:h + 1]))
                    c_st2 = P.inc("vector", "dv", lambda e, h=h: e.scalar_tensor_tensor(
                        xst[:, 0:1024], ret[:, h * 1024:(h + 1) * 1024], 1.0,
                        ret[:, h * 1024:(h + 1) * 1024], ALU.mult, ALU.mult,
                        accum_out=accs[:, 16 + h:17 + h]))
                    fullpass_state[("dt", h)] = c_st2
                    fullpass_state[("fp", h)] = last_pe
                    fullpass_state[("stc", h)] = c_stc

                emit_wload(0)
                for h in range(H):
                    emit_wload(h + 1)
                    emit_proj(h)
                    fp = h - LAG if h >= LAG else None
                    fst = fullpass_state.get(("fst", fp)) if fp is not None else None
                    if fp is not None:
                        fst = {}
                        emit_fullpass_batches(fp, fst)
                    ast = {}
                    emit_passA_khat(h, ast)
                    if fp is not None:
                        emit_fullpass_chain(fp, fst)
                    emit_passA_ds(h, ast)
                    if fp is not None:
                        emit_fullpass_ret(fp, fst)
                    emit_passA_cc(h, ast)
                for h in range(H - LAG, H):
                    fst = {}
                    emit_fullpass_batches(h, fst)
                    emit_fullpass_chain(h, fst)
                    emit_fullpass_ret(h, fst)

                for oc in range(16):
                    wbuf = wsl[oc % 2]
                    if oc < 2:
                        P.wait("gpsimd", "pe", fullpass_state.get(("wsl", 14 + oc), 0))
                    else:
                        P.wait("gpsimd", "pe", fullpass_state.get(("gw", oc - 2), 0))
                    vw = P.dma("gpsimd", "Lg", wbuf.ap().rearrange("p (a b) -> p a b", a=16), wg_t[oc].rearrange("a p b -> p a b"))
                    P.wait("tensor", "Lg", vw)
                    c_pe = 0
                    for th in range(2):
                        bank = f"pb{th}"
                        pb = PB[th]
                        P.use_bank(bank)
                        for fc in range(16):
                            c_pe = P.inc("tensor", "pe", lambda e, pb=pb, wbuf=wbuf, fc=fc, th=th: e.matmul(
                                pb[:], wbuf[:, fc * 128:(fc + 1) * 128],
                                x_fm[:, fc * 1024 + th * 512: fc * 1024 + (th + 1) * 512],
                                start=(fc == 0), stop=(fc == 15)))
                        P.wait("scalar", "pe", c_pe)
                        c_g = P.inc("scalar", "ac", lambda e, oc=oc, th=th, pb=pb: e.activation(
                            g_fm[:, oc * 1024 + th * 512: oc * 1024 + (th + 1) * 512],
                            pb[:], AF.Silu, bias=bg_c[:, oc:oc + 1]))
                        P.free_bank(bank, "ac", c_g)
                    fullpass_state[("gw", oc)] = c_pe

                P.barrier(("tensor", "vector", "scalar", "gpsimd"))
                P.use_bank("pb5")
                c_tr = P.inc("tensor", "pe", lambda e: e.transpose(
                    PB[5][0:32, 0:128], accs[:], idf[:]))
                P.wait("vector", "pe", c_tr)
                c_red = P.inc("vector", "dv", lambda e: e.tensor_reduce(
                    a32[:, 0:1], PB[5][0:32, 0:128], mybir.AxisListType.X, ALU.add))
                P.free_bank("pb5", "dv", c_red)
                P.wait("gpsimd", "dv", c_red)
                v_st = P.dma("gpsimd", "Lg", cc_ti.ap().rearrange("(p o) -> p o", o=1), a32[:, 0:1])
                P.wait("gpsimd", "Lg", v_st)
                c_ccs = P.inc("gpsimd", "cc", lambda e: e.collective_compute(
                    "AllReduce", ALU.add, replica_groups=[list(range(NC))],
                    ins=[cc_ti.ap().opt()], outs=[cc_to.ap().opt()]))
                P.wait("sync", "cc", c_ccs)
                v_sb = P.dma("sync", "Ls", stat16[:, 0:2], cc_to.ap().rearrange("(b a) -> a b", b=2),
                             allow_slow_non_contiguous=True)
                NSTAT = float(S * DK)
                P.wait("vector", "Ls", v_sb)
                c_m = P.inc("vector", "dv", lambda e: e.tensor_scalar_mul(stat16[:, 0:2], stat16[:, 0:2], 1.0 / NSTAT))
                P.wait("scalar", "dv", c_m)
                c_m2 = P.inc("scalar", "ac", lambda e: e.activation(stat16[:, 2:3], stat16[:, 0:1], AF.Square))
                P.wait("vector", "ac", c_m2)
                c_v = P.inc("vector", "dv", lambda e: e.tensor_sub(stat16[:, 1:2], stat16[:, 1:2], stat16[:, 2:3]))
                P.wait("scalar", "dv", c_v)
                c_sq = P.inc("scalar", "ac", lambda e: e.activation(stat16[:, 1:2], stat16[:, 1:2], AF.Ln, bias=epsc[0:16, :]))
                c_rc = P.inc("scalar", "ac", lambda e: e.activation(stat16[:, 1:2], stat16[:, 1:2], AF.Exp, scale=nhalf[0:16, :]))
                P.wait("vector", "ac", c_rc)
                c_sr2 = P.inc("vector", "dv", lambda e: e.tensor_scalar_mul(srow[:], grow[:], stat16[:, 1:2]))
                c_br1 = P.inc("vector", "dv", lambda e: e.tensor_scalar_mul(brow[:], srow[:], stat16[:, 0:1]))
                c_br2 = P.inc("vector", "dv", lambda e: e.tensor_sub(brow[:], gbrow[:], brow[:]))
                P.use_bank("pb5")
                P.wait("tensor", "dv", c_br2)
                c_t1x = P.inc("tensor", "pe", lambda e: e.transpose(
                    PB[5][:, 0:16], srow[:], idf[0:16, 0:16]))
                c_t2x = P.inc("tensor", "pe", lambda e: e.transpose(
                    PB[5][:, 16:32], brow[:], idf[0:16, 0:16]))
                P.wait("vector", "pe", c_t2x)
                c_gs = P.inc("vector", "dv", lambda e: e.tensor_copy(gsc_c[:], PB[5][:, 0:16]))
                c_gb2 = P.inc("vector", "dv", lambda e: e.tensor_copy(gbi_c[:], PB[5][:, 16:32]))
                P.free_bank("pb5", "dv", c_gb2)
                bo_row = xst
                v_borow = P.dma("sync", "Ls", bo_row[0:1, 0:2048], bo.rearrange("(o f) -> o f", o=1))
                v_wo0 = P.dma("gpsimd", "Lg", wosl.ap().rearrange("p (a b) -> p a b", a=16),
                              wo_t[0].rearrange("a p b -> p a b"))
                P.wait("tensor", "Ls", v_borow)
                for gg in range(4):
                    P.use_bank("pb5")
                    c_bo = P.inc("tensor", "pe", lambda e, gg=gg: e.matmul(
                        PB[5][:, 0:512], ones1[0:1, 0:128], bo_row[0:1, gg * 512:(gg + 1) * 512],
                        start=True, stop=True))
                    P.wait("vector", "pe", c_bo)
                    c_ev = P.inc("vector", "dv", lambda e, gg=gg: e.tensor_copy(
                        bo_tab[:, gg * 512:(gg + 1) * 512], PB[5][:, 0:512]))
                    P.free_bank("pb5", "dv", c_ev)

                P.barrier(("tensor", "vector", "scalar"))
                for h in range(H):
                    for cq in range(2):
                        tb = (h * 2 + cq) % 2
                        P.use_bank(f"tb{tb}")
                        c_tr = 0
                        for k in range(4):
                            ci = cq * 4 + k
                            c_tr = P.inc("tensor", "pe", lambda e, h=h, ci=ci, k=k, tb=tb: e.transpose(
                                TB[tb][:, k * 128:(k + 1) * 128],
                                ret[:, h * 1024 + ci * 128: h * 1024 + (ci + 1) * 128], i128b[:]))
                        P.wait("vector", "pe", c_tr)
                        c_t1 = P.inc("vector", "dv", lambda e, tb=tb, h=h: e.tensor_scalar(
                            khA[:, 0:512], TB[tb][:, 0:512], gsc_c[:, h:h + 1], gbi_c[:, h:h + 1], ALU.mult, ALU.add))
                        P.free_bank(f"tb{tb}", "dv", c_t1)
                        gsl = g_fm[:, h * 1024 + cq * 512: h * 1024 + (cq + 1) * 512]
                        c_t2 = P.inc("vector", "dv", lambda e, gsl=gsl: e.tensor_mul(gsl, gsl, khA[:, 0:512]))

                P.barrier()
                P.flush()

            with ExitStack() as ph_wo:
                ew = ph_wo.enter_context
                x2pre = ew(nc.sbuf_tensor("x2pre", [128, 8 * 2048], F32))
                wosl_b = ew(nc.sbuf_tensor("wosl_b", [128, 16 * 512], BF16))
                xrel = ew(nc.sbuf_tensor("xrel", [128, 2048], F32))
                x2s = ew(nc.sbuf_tensor("x2s", [128, 2048], BF16))
                lnst = ew(nc.sbuf_tensor("lnst", [128, 4], F32))
                lnsc = ew(nc.sbuf_tensor("lnsc", [128, 2048], F32))

                xrel_done = {}
                og_pe = {}
                vws = {0: v_wo0}
                wosl2 = [wosl, wosl_b]
                for og in range(4):
                    if og + 1 < 4:
                        P.wait("gpsimd", "pe", og_pe.get(og - 1, 0))
                        vws[og + 1] = P.dma("gpsimd", "Lg",
                                            wosl2[(og + 1) % 2].ap().rearrange("p (a b) -> p a b", a=16),
                                            wo_t[og + 1].rearrange("a p b -> p a b"))
                    wosl = wosl2[og % 2]
                    vw = vws[og]
                    P.wait("tensor", "Lg", vw)
                    for tc in range(8):
                        if og == 0:
                            pass
                        bank = f"pb{tc % 4}"
                        pb = PB[tc % 4]
                        P.use_bank(bank)
                        c_pe = 0
                        for fc in range(16):
                            c_pe = P.inc("tensor", "pe", lambda e, pb=pb, fc=fc, tc=tc, wosl=wosl: e.matmul(
                                pb[:], g_fm[:, fc * 1024 + tc * 128: fc * 1024 + (tc + 1) * 128],
                                wosl[:, fc * 512:(fc + 1) * 512],
                                start=(fc == 0), stop=(fc == 15)))
                        P.wait("vector", "pe", c_pe)
                        dst = x2pre[:, tc * 2048 + og * 512: tc * 2048 + (og + 1) * 512]
                        c_e1 = P.inc("vector", "dv", lambda e, dst=dst, pb=pb, og=og: e.tensor_add(
                            dst, pb[:], bo_tab[:, og * 512:(og + 1) * 512]))
                        P.free_bank(bank, "dv", c_e1)
                    og_pe[og] = P.cnt["pe"]

                vx2st = 0
                prev_ca = 0
                for tc in range(8):
                    P.wait("scalar", "dv", prev_ca)
                    vx = P.dma("scalar", "La", xrel[:], xin[tc * 128:(tc + 1) * 128, :])
                    P.wait("vector", "La", vx)
                    sl = x2pre[:, tc * 2048:(tc + 1) * 2048]
                    c_a = P.inc("vector", "dv", lambda e, sl=sl: e.tensor_add(sl, sl, xrel[:]))
                    prev_ca = c_a
                    P.wait("vector", "Ls", vx2st)
                    c_s1 = P.inc("vector", "dv", lambda e, sl=sl: e.tensor_scalar(
                        lnsc[:], sl, 0.0, None, ALU.add, ALU.add, accum_out=lnst[:, 0:1]))
                    c_s2 = P.inc("vector", "dv", lambda e, sl=sl: e.scalar_tensor_tensor(
                        lnsc[:], sl, 1.0, sl, ALU.mult, ALU.mult, accum_out=lnst[:, 1:2]))
                    P.wait("scalar", "dv", c_s2)
                    c_ms = P.inc("scalar", "ac", lambda e: e.mul(lnst[:, 0:1], lnst[:, 0:1], 1.0 / D))
                    c_qs = P.inc("scalar", "ac", lambda e: e.mul(lnst[:, 1:2], lnst[:, 1:2], 1.0 / D))
                    c_m2 = P.inc("scalar", "ac", lambda e: e.activation(lnst[:, 2:3], lnst[:, 0:1], AF.Square))
                    P.wait("vector", "ac", c_m2)
                    c_v = P.inc("vector", "dv", lambda e: e.tensor_sub(lnst[:, 1:2], lnst[:, 1:2], lnst[:, 2:3]))
                    P.wait("scalar", "dv", c_v)
                    c_sq = P.inc("scalar", "ac", lambda e: e.activation(lnst[:, 1:2], lnst[:, 1:2], AF.Ln, bias=epsc[:]))
                    c_rc = P.inc("scalar", "ac", lambda e: e.activation(lnst[:, 1:2], lnst[:, 1:2], AF.Exp, scale=nhalf[:]))
                    P.wait("vector", "ac", c_rc)
                    c_ap = P.inc("vector", "dv", lambda e, sl=sl: e.tensor_scalar(
                        x2s[:], sl, lnst[:, 0:1], lnst[:, 1:2], ALU.subtract, ALU.mult))
                    P.wait("sync", "dv", c_ap)
                    vx2st = P.dma("sync", "Ls", x2d[tc * 128:(tc + 1) * 128, :], x2s[:])
                    P.wait("scalar", "Ls", vx2st)

                P.barrier()
                P.flush()

        with ExitStack() as ph_f:
            ef = ph_f.enter_context
            x3pre = ef(nc.sbuf_tensor("x3pre", [128, 8 * 2048], BF16))
            sdp_row = ef(nc.sbuf_tensor("sdp_row", [1, 2048], F32))
            abuf = ef(nc.sbuf_tensor("abuf", [128, 64 * TG], BF16))
            sdp_tab = ef(nc.sbuf_tensor("sdp_tab", [128, 2048], F32))
            silu_sb = ef(nc.sbuf_tensor("silu_sb", [128, 512], BF16))
            lnst2 = ef(nc.sbuf_tensor("lnst2", [128, 4], F32))
            lnsc2 = ef(nc.sbuf_tensor("lnsc2", [128, 2048], F32))
            x2g = ef(nc.sbuf_tensor("x2g", [128, 4 * 2048], BF16))

            v_sdprow = P.dma("sync", "Ls", sdp_row[0:1, :], ws_dp.rearrange("(o f) -> o f", o=1))
            P.wait("tensor", "Ls", v_sdprow)
            for gg in range(4):
                P.use_bank("pb5")
                c_mm = P.inc("tensor", "pe", lambda e, gg=gg: e.matmul(
                    PB[5][:, 0:512], ones1[0:1, 0:128], sdp_row[0:1, gg * 512:(gg + 1) * 512],
                    start=True, stop=True))
                P.wait("vector", "pe", c_mm)
                c_ev = P.inc("vector", "dv", lambda e, gg=gg: e.tensor_scalar_mul(
                    sdp_tab[:, gg * 512:(gg + 1) * 512], PB[5][:, 0:512], SCALE))
                P.free_bank("pb5", "dv", c_ev)
            P.flush()

            for g in range(NG):
                with ExitStack() as ph_g2:
                    eg = ph_g2.enter_context
                    h1 = eg(nc.sbuf_tensor(f"h1_{g}", [128, 16 * TG], BF16))
                    wdsl = [eg(nc.sbuf_tensor(f"wd{g}_{i}", [128, 4 * 512], BF16)) for i in range(2)]

                    with ExitStack() as ph_xt:
                        xt = ph_xt.enter_context(nc.sbuf_tensor(f"xt_{g}", [128, 16 * TG], BF16))
                        vx2l = 0
                        for tcl in range(TG // 128):
                            tc = g * (TG // 128) + tcl
                            vx2l = P.dma("sync", "Ls", x2g[:, tcl * 2048:(tcl + 1) * 2048],
                                         x2d[tc * 128:(tc + 1) * 128, :])
                        P.wait("tensor", "Ls", vx2l)
                        for fc in range(16):
                            tb = fc % 2
                            P.use_bank(f"tb{tb}")
                            c_tr = 0
                            for tcl in range(4):
                                c_tr = P.inc("tensor", "pe", lambda e, tcl=tcl, fc=fc, tb=tb: e.transpose(
                                    TB[tb][:, tcl * 128:(tcl + 1) * 128],
                                    x2g[:, tcl * 2048 + fc * 128: tcl * 2048 + (fc + 1) * 128], i128b[:]))
                            P.wait("vector", "pe", c_tr)
                            c_ev = P.inc("vector", "dv", lambda e, xt=xt, fc=fc, tb=tb: e.tensor_scalar_mul(
                                xt[:, fc * TG:(fc + 1) * TG], TB[tb][:, 0:512], su_c[:, fc:fc + 1]))
                            P.free_bank(f"tb{tb}", "dv", c_ev)
                        for oc in range(16):
                            for th in range(TG // 512):
                                bank = f"pb{oc % 4}"
                                pb = PB[oc % 4]
                                P.use_bank(bank)
                                c_pe = 0
                                for ic in range(16):
                                    hh = hp if H16S[ic, oc] > 0 else hn
                                    c_pe = P.inc("tensor", "pe", lambda e, pb=pb, hh=hh, ic=ic, th=th: e.matmul(
                                        pb[:], hh[:], xt[:, ic * TG + th * 512: ic * TG + (th + 1) * 512],
                                        start=(ic == 0), stop=(ic == 15)))
                                P.wait("vector", "pe", c_pe)
                                c_ev = P.inc("vector", "dv", lambda e, oc=oc, th=th, pb=pb: e.tensor_copy(
                                    h1[:, oc * TG + th * 512: oc * TG + (th + 1) * 512], pb[:]))
                                P.free_bank(bank, "dv", c_ev)
                        P.barrier(("tensor", "vector", "scalar", "gpsimd", "sync"))
                        P.flush()

                    with ExitStack() as ph_ug:
                        wupsl = [ph_ug.enter_context(nc.sbuf_tensor(f"wup{g}_{i}", [128, 16 * 256], BF16)) for i in range(2)]
                        wgtsl = [ph_ug.enter_context(nc.sbuf_tensor(f"wgt{g}_{i}", [128, 16 * 256], BF16)) for i in range(2)]
                        cons = {}
                        prev_sil = 0
                        for icp in range(32):
                            bidx = icp % 2
                            prevc = cons.get(icp - 2)
                            if prevc:
                                P.wait("gpsimd", "pe", prevc)
                            vu = P.dma("gpsimd", "Lg", wupsl[bidx].ap().rearrange("p (a b) -> p a b", a=16), wup_t[icp].rearrange("a p b -> p a b"))
                            vg = P.dma("gpsimd", "Lg", wgtsl[bidx].ap().rearrange("p (a b) -> p a b", a=16), wgt_t[icp].rearrange("a p b -> p a b"))
                            P.wait("tensor", "Lg", vg)
                            c_pe = 0
                            for ih in range(2):
                                ic = icp * 2 + ih
                                pu = PB[0 + (ic % 2) * 2]
                                pg = PB[1 + (ic % 2) * 2]
                                bu = f"pb{0 + (ic % 2) * 2}"
                                bgk = f"pb{1 + (ic % 2) * 2}"
                                P.use_bank(bu)
                                for fc in range(16):
                                    P.inc("tensor", "pe", lambda e, pu=pu, w=wupsl[bidx], fc=fc, ih=ih: e.matmul(
                                        pu[:], w[:, fc * 256 + ih * 128: fc * 256 + (ih + 1) * 128],
                                        h1[:, fc * TG: fc * TG + 512],
                                        start=(fc == 0), stop=(fc == 15)))
                                P.use_bank(bgk)
                                for fc in range(16):
                                    c_pe = P.inc("tensor", "pe", lambda e, pg=pg, w=wgtsl[bidx], fc=fc, ih=ih: e.matmul(
                                        pg[:], w[:, fc * 256 + ih * 128: fc * 256 + (ih + 1) * 128],
                                        h1[:, fc * TG: fc * TG + 512],
                                        start=(fc == 0), stop=(fc == 15)))
                                P.wait("scalar", "pe", c_pe)
                                P.wait("scalar", "dv", prev_sil)
                                c_si = P.inc("scalar", "ac", lambda e, pg=pg, ic=ic: e.activation(
                                    silu_sb[:], pg[:], AF.Silu, scale=sgt_c[:, ic:ic + 1]))
                                P.free_bank(bgk, "ac", c_si)
                                P.wait("vector", "ac", c_si)
                                c_a = P.inc("vector", "dv", lambda e, pu=pu, ic=ic: e.scalar_tensor_tensor(
                                    abuf[:, ic * 512:(ic + 1) * 512], pu[:], sup_c[:, ic:ic + 1],
                                    silu_sb[:], ALU.mult, ALU.mult))
                                P.free_bank(bu, "dv", c_a)
                                prev_sil = c_a
                            cons[icp] = c_pe
                        P.barrier(("gpsimd", "tensor", "vector", "scalar"))
                        P.flush()

                    dn_prefetch = {}
                    for icg0 in range(2):
                        v = P.dma("gpsimd", "Lg", wdsl[icg0].ap().rearrange("p (a b) -> p a b", a=4),
                                  wdn_t[0, icg0])
                        dn_prefetch[(0, icg0)] = v
                    with ExitStack() as ph_bf:
                        pong = ph_bf.enter_context(nc.sbuf_tensor(f"pong{g}", [128, 64 * 128], BF16))
                        for qt in range(TG // 128):
                            src_is_a = True
                            c_last = P.cnt["dv"]
                            for hstep in (32, 16, 8, 4, 2, 1):
                                npair = 64 // (2 * hstep)

                                def mk(eng_fn, dstbuf, srcbuf, qt=qt, hstep=hstep, npair=npair, off_s=0, off_d=0):
                                    pass
                                if src_is_a:
                                    def s_ap(cc0, qt=qt):
                                        return None
                                def apv(buf, start_c, qt=qt, hstep=hstep, npair=npair, width=None):
                                    base_ap = buf.ap()
                                    stride = TG if buf is abuf else 128
                                    off = base_ap.offset + start_c * stride + qt * 128 * (1 if buf is abuf else 0)
                                    if buf is abuf:
                                        off = base_ap.offset + start_c * TG + qt * 128
                                        return bass.AP(tensor=base_ap.tensor, offset=off,
                                                       ap=[[base_ap.ap[0][0], 128], [2 * hstep * TG, npair], [TG, hstep], [1, 128]])
                                    else:
                                        off = base_ap.offset + start_c * 128
                                        return bass.AP(tensor=base_ap.tensor, offset=off,
                                                       ap=[[base_ap.ap[0][0], 128], [2 * hstep * 128, npair], [128, hstep], [1, 128]])
                                src = abuf if src_is_a else pong
                                dstb = pong if src_is_a else abuf
                                a0 = apv(src, 0)
                                a1 = apv(src, hstep)
                                d0 = apv(dstb, 0)
                                d1 = apv(dstb, hstep)
                                c1 = P.inc("vector", "dv", lambda e, d0=d0, a0=a0, a1=a1: e.tensor_add(d0, a0, a1))
                                c_last = P.inc("vector", "dv", lambda e, d1=d1, a0=a0, a1=a1: e.tensor_sub(d1, a0, a1))
                                src_is_a = not src_is_a
                        P.barrier(("vector", "tensor"))
                        P.flush()

                    for c in range(64):
                        bank = f"pb{c % 4}"
                        pb = PB[c % 4]
                        P.use_bank(bank)
                        c_pe = P.inc("tensor", "pe", lambda e, pb=pb, c=c: e.matmul(
                            pb[:], hu[:], abuf[:, c * 512:(c + 1) * 512], start=True, stop=True))
                        P.wait("vector", "pe", c_pe)
                        c_ev = P.inc("vector", "dv", lambda e, pb=pb, c=c: e.tensor_copy(
                            abuf[:, c * 512:(c + 1) * 512], pb[:]))
                        P.free_bank(bank, "dv", c_ev)

                    with ExitStack() as ph_dn:
                        P.barrier(("tensor", "vector", "gpsimd"))
                        wdsl_hist = []
                        for og in range(4):
                            consumed = {}
                            for icg in range(16):
                                bidx = icg % 2
                                if (og, icg) in dn_prefetch:
                                    vw = dn_prefetch[(og, icg)]
                                else:
                                    if len(wdsl_hist) >= 2:
                                        P.wait("gpsimd", "pe", wdsl_hist[-2])
                                    vw = P.dma("gpsimd", "Lg", wdsl[bidx].ap().rearrange("p (a b) -> p a b", a=4),
                                               wdn_t[og, icg])
                                P.wait("tensor", "Lg", vw)
                                c_pe = 0
                                for ics in range(4):
                                    ic = icg * 4 + ics
                                    for tcl in range(4):
                                        pb = PB[tcl]
                                        if icg == 0 and ics == 0:
                                            P.use_bank(f"pb{tcl}")
                                        c_pe = P.inc("tensor", "pe", lambda e, pb=pb, bidx=bidx, ics=ics, ic=ic, tcl=tcl: e.matmul(
                                            pb[:], abuf[:, ic * 512 + tcl * 128: ic * 512 + (tcl + 1) * 128],
                                            wdsl[bidx][:, ics * 512:(ics + 1) * 512],
                                            start=(ic == 0), stop=(ic == 63)))
                                consumed[icg] = c_pe
                                wdsl_hist.append(c_pe)
                            P.wait("vector", "pe", P.cnt["pe"])
                            for tcl in range(4):
                                pb = PB[tcl]
                                tc = g * 4 + tcl
                                dst = x3pre[:, tc * 2048 + og * 512: tc * 2048 + (og + 1) * 512]
                                c_ev = P.inc("vector", "dv", lambda e, dst=dst, pb=pb, og=og: e.tensor_mul(
                                    dst, pb[:], sdp_tab[:, og * 512:(og + 1) * 512]))
                                P.free_bank(f"pb{tcl}", "dv", c_ev)
                        for tcl in range(4):
                            tc = g * 4 + tcl
                            sl = x3pre[:, tc * 2048:(tc + 1) * 2048]
                            c_a = P.inc("vector", "dv", lambda e, sl=sl, tcl=tcl: e.tensor_add(
                                lnsc2[:], sl, x2g[:, tcl * 2048:(tcl + 1) * 2048]))
                            c_s1 = P.inc("vector", "dv", lambda e, sl=sl: e.tensor_scalar(
                                sl, lnsc2[:], 0.0, None, ALU.add, ALU.add, accum_out=lnst2[:, 0:1]))
                            c_s2 = P.inc("vector", "dv", lambda e, sl=sl: e.scalar_tensor_tensor(
                                sl, lnsc2[:], 1.0, lnsc2[:], ALU.mult, ALU.mult, accum_out=lnst2[:, 1:2]))
                            P.wait("scalar", "dv", c_s2)
                            c_ms = P.inc("scalar", "ac", lambda e: e.mul(lnst2[:, 0:1], lnst2[:, 0:1], 1.0 / D))
                            c_qs = P.inc("scalar", "ac", lambda e: e.mul(lnst2[:, 1:2], lnst2[:, 1:2], 1.0 / D))
                            c_m2 = P.inc("scalar", "ac", lambda e: e.activation(lnst2[:, 2:3], lnst2[:, 0:1], AF.Square))
                            P.wait("vector", "ac", c_m2)
                            c_v = P.inc("vector", "dv", lambda e: e.tensor_sub(lnst2[:, 1:2], lnst2[:, 1:2], lnst2[:, 2:3]))
                            P.wait("scalar", "dv", c_v)
                            c_sq = P.inc("scalar", "ac", lambda e: e.activation(lnst2[:, 1:2], lnst2[:, 1:2], AF.Ln, bias=epsc[:]))
                            c_rc = P.inc("scalar", "ac", lambda e: e.activation(lnst2[:, 1:2], lnst2[:, 1:2], AF.Exp, scale=nhalf[:]))
                            P.wait("vector", "ac", c_rc)
                            c_ap = P.inc("vector", "dv", lambda e: e.tensor_scalar(
                                lnsc2[:], lnsc2[:], lnst2[:, 0:1], lnst2[:, 1:2], ALU.subtract, ALU.mult))
                            P.wait("scalar", "dv", c_ap)
                            vo = P.dma("scalar", "La", yout[tc * 128:(tc + 1) * 128, :], lnsc2[:])
                            P.wait("vector", "La", vo)
                        P.barrier()
                        P.flush()

        P.wait("sync", "La", P.cnt["La"])
        P.flush()

    return nc


_NC_CACHE = None


def kernel(**inputs):
    global _NC_CACHE
    ins = {k: np.asarray(v) for k, v in inputs.items()}
    x = ins["L0_attn_x"].reshape(S, D)

    def colmajor_ok(a):
        return np.ascontiguousarray(a, dtype=np.float32)

    wq = ins["L0_attn_wq"]; wk = ins["L0_attn_wk"]; wv = ins["L0_attn_wv"]
    wg = ins["L0_attn_wg"]; wo = ins["L0_attn_wo"]

    def tile_qkv(w):
        return np.ascontiguousarray(w.reshape(16, 128, 16, 128).transpose(2, 0, 1, 3))

    wq_t = tile_qkv(wq); wk_t = tile_qkv(wk); wv_t = tile_qkv(wv); wg_t = tile_qkv(wg)
    wo_t = np.ascontiguousarray(wo.reshape(16, 128, 4, 512).transpose(2, 0, 1, 3))
    wup_t = np.ascontiguousarray(ins["L0_ffn_W_up"].T.reshape(16, 128, 32, 256).transpose(2, 0, 1, 3))
    wgt_t = np.ascontiguousarray(ins["L0_ffn_W_gate"].T.reshape(16, 128, 32, 256).transpose(2, 0, 1, 3))
    wdn_t = np.ascontiguousarray(ins["L0_ffn_W_down"].T.reshape(16, 4, 128, 4, 512).transpose(3, 0, 2, 1, 4))

    H128 = _hadamard(128)
    h128p = (H128 / (SCALE * np.sqrt(np.float32(D)))).astype(BF)
    h128n = (-H128 / (SCALE * np.sqrt(np.float32(D)))).astype(BF)
    h128u = (H128 / (SCALE * np.sqrt(np.float32(INTER)))).astype(BF)

    if _NC_CACHE is None:
        _NC_CACHE = build_nc()
    nc = _NC_CACHE

    in_maps = []
    for c in range(NC):
        emask = (np.arange(NC) < c).astype(np.float32)
        eexp = np.maximum(c - 1 - np.arange(NC), 0).astype(np.float32)
        in_maps.append({
            "xin": np.ascontiguousarray(x[c * T:(c + 1) * T]),
            "wq_t": wq_t, "wk_t": wk_t, "wv_t": wv_t, "wg_t": wg_t, "wo_t": wo_t,
            "wup_t": wup_t, "wgt_t": wgt_t, "wdn_t": wdn_t,
            "bq": ins["L0_attn_bq"], "bk": ins["L0_attn_bk"], "bv": ins["L0_attn_bv"],
            "bg": ins["L0_attn_bg"], "bo": ins["L0_attn_bo"],
            "gnw": ins["L0_attn_gn_w"], "gnb": ins["L0_attn_gn_b"],
            "gam": ins["L0_attn_gamma"],
            "su_ug": ins["L0_ffn_SU_ug"], "su_dp": ins["L0_ffn_SU_dp"],
            "ws_ug": ins["L0_ffn_Wscale_ug"], "ws_dp": ins["L0_ffn_Wscale_dp"],
            "h128p": h128p, "h128n": h128n, "h128u": h128u,
            "emask": emask, "eexp": eexp,
        })
    res = run_bass_kernel_spmd(nc, in_maps, core_ids=list(range(NC)))
    out = np.concatenate([res.results[c]["y"] for c in range(NC)], axis=0)
    return out.reshape(1, S, D).astype(np.float32)


# version 17
# speedup vs baseline: 1.1404x; 1.0255x over previous
import numpy as np
import ml_dtypes
from contextlib import ExitStack

import concourse.bass as bass
import concourse.mybir as mybir
from concourse.bass_utils import run_bass_kernel_spmd

F32 = mybir.dt.float32
BF16 = mybir.dt.bfloat16
AF = mybir.ActivationFunctionType
ALU = mybir.AluOpType
BF = ml_dtypes.bfloat16

NC = 8
S = 8192
D = 2048
H = 16
DK = 128
T = S // NC
CH = 128
NCH = T // CH
INTER = 8192
SCALE = 64.0
EPS = 1e-5
TG = 512
NG = T // TG

ENGINES = ("sync", "gpsimd", "tensor", "vector", "scalar")


def _hadamard(n):
    h = np.array([[1.0]], np.float32)
    while h.shape[0] < n:
        h = np.block([[h, h], [h, -h]])
    return h


H16S = _hadamard(16)


class Prog:

    def __init__(self, nc, block, sems):
        self.nc = nc
        self.block = block
        self.sems = sems
        self.cnt = {k: 0 for k in sems}
        self.q = {e: [] for e in ENGINES}
        self.bank_free = {}

    def add(self, eng, fn):
        self.q[eng].append(fn)

    def wait(self, eng, sem, v):
        if v <= 0:
            return
        s = self.sems[sem]
        self.q[eng].append(lambda e, s=s, v=v: e.wait_ge(s, v))

    def inc(self, eng, sem, fn, n=1):
        self.cnt[sem] += n
        s = self.sems[sem]
        self.q[eng].append(lambda e, fn=fn, s=s, n=n: fn(e).then_inc(s, n))
        return self.cnt[sem]

    def run(self, eng, fn):
        self.q[eng].append(fn)

    def dma(self, eng, sem, out, in_, **kw):
        return self.inc(eng, sem, lambda e: e.dma_start(out=out, in_=in_, **kw), 16)

    def use_bank(self, bank):
        ent = self.bank_free.get(bank)
        if ent is not None:
            self.wait("tensor", ent[0], ent[1])

    def free_bank(self, bank, sem, v):
        self.bank_free[bank] = (sem, v)

    def barrier(self, engines=ENGINES):
        snap = dict(self.cnt)
        for e in engines:
            for sname, v in snap.items():
                if v > 0:
                    self.wait(e, sname, v)

    def flush(self):
        blk = self.block
        for eng in ENGINES:
            ops = self.q[eng]
            if not ops:
                continue
            deco = getattr(blk, eng)

            def body(e, ops=ops):
                for f in ops:
                    f(e)
            deco(body)
            self.q[eng] = []


def build_nc():
    nc = bass.Bass("TRN2", target_bir_lowering=False, debug=False)

    xin = nc.dram_tensor("xin", [T, D], F32, kind="ExternalInput").ap()
    wq_t = nc.dram_tensor("wq_t", [H, 16, 128, 128], F32, kind="ExternalInput").ap()
    wk_t = nc.dram_tensor("wk_t", [H, 16, 128, 128], F32, kind="ExternalInput").ap()
    wv_t = nc.dram_tensor("wv_t", [H, 16, 128, 128], F32, kind="ExternalInput").ap()
    wg_t = nc.dram_tensor("wg_t", [16, 16, 128, 128], F32, kind="ExternalInput").ap()
    wo_t = nc.dram_tensor("wo_t", [4, 16, 128, 512], F32, kind="ExternalInput").ap()
    wup_t = nc.dram_tensor("wup_t", [32, 16, 128, 256], F32, kind="ExternalInput").ap()
    wgt_t = nc.dram_tensor("wgt_t", [32, 16, 128, 256], F32, kind="ExternalInput").ap()
    wdn_t = nc.dram_tensor("wdn_t", [4, 16, 128, 4, 512], F32, kind="ExternalInput").ap()
    bq = nc.dram_tensor("bq", [D], F32, kind="ExternalInput").ap()
    bk = nc.dram_tensor("bk", [D], F32, kind="ExternalInput").ap()
    bv = nc.dram_tensor("bv", [D], F32, kind="ExternalInput").ap()
    bg = nc.dram_tensor("bg", [D], F32, kind="ExternalInput").ap()
    bo = nc.dram_tensor("bo", [D], F32, kind="ExternalInput").ap()
    gnw = nc.dram_tensor("gnw", [D], F32, kind="ExternalInput").ap()
    gnb = nc.dram_tensor("gnb", [D], F32, kind="ExternalInput").ap()
    gam = nc.dram_tensor("gam", [H], F32, kind="ExternalInput").ap()
    su_ug = nc.dram_tensor("su_ug", [D], F32, kind="ExternalInput").ap()
    su_dp = nc.dram_tensor("su_dp", [INTER], F32, kind="ExternalInput").ap()
    ws_ug = nc.dram_tensor("ws_ug", [2 * INTER], F32, kind="ExternalInput").ap()
    ws_dp = nc.dram_tensor("ws_dp", [D], F32, kind="ExternalInput").ap()
    h128p = nc.dram_tensor("h128p", [128, 128], BF16, kind="ExternalInput").ap()
    h128n = nc.dram_tensor("h128n", [128, 128], BF16, kind="ExternalInput").ap()
    h128u = nc.dram_tensor("h128u", [128, 128], BF16, kind="ExternalInput").ap()
    emask = nc.dram_tensor("emask", [NC], F32, kind="ExternalInput").ap()
    eexp = nc.dram_tensor("eexp", [NC], F32, kind="ExternalInput").ap()
    yout = nc.dram_tensor("y", [T, D], F32, kind="ExternalOutput").ap()

    x2d = nc.dram_tensor("x2d", [T, D], BF16).ap()
    dsc = nc.dram_tensor("dsc", [256], F32).ap()
    cc_si = [nc.dram_tensor(f"cc_si{h}", [128, 128], F32) for h in range(H)]
    cc_so = [nc.dram_tensor(f"cc_so{h}", [NC * 128, 128], F32) for h in range(H)]
    cc_ti = nc.dram_tensor("cc_ti", [32], F32)
    cc_to = nc.dram_tensor("cc_to", [32], F32)

    with ExitStack() as outer:
        ent = outer.enter_context
        semnames = ["Ls", "Lg", "La", "pe", "dv", "ac", "gp", "cc"]
        sems = {n: ent(nc.semaphore(n)) for n in semnames}

        PB = [ent(nc.psum_tensor(f"pb{i}", [128, 512], F32)) for i in range(6)]
        TB = [ent(nc.psum_tensor(f"tb{i}", [128, 1024], BF16)) for i in range(2)]

        ones1 = ent(nc.sbuf_tensor("ones1", [1, 512], F32))
        onesb = ent(nc.sbuf_tensor("onesb", [128, 1], F32))
        epsc = ent(nc.sbuf_tensor("epsc", [128, 1], F32))
        nhalf = ent(nc.sbuf_tensor("nhalf", [128, 1], F32))
        i128b = ent(nc.sbuf_tensor("i128b", [128, 128], BF16))
        hp = ent(nc.sbuf_tensor("hp", [128, 128], BF16))
        hn = ent(nc.sbuf_tensor("hn", [128, 128], BF16))
        hu = ent(nc.sbuf_tensor("hu", [128, 128], BF16))
        io_jmp = ent(nc.sbuf_tensor("io_jmp", [128, 128], F32))
        io_jp1 = ent(nc.sbuf_tensor("io_jp1", [128, 128], F32))
        io_kd = ent(nc.sbuf_tensor("io_kd", [128, 1], F32))
        lng_row = ent(nc.sbuf_tensor("lng_row", [1, 16], F32))
        lng_cols = ent(nc.sbuf_tensor("lng_cols", [128, 16], F32))
        kdec_cols = ent(nc.sbuf_tensor("kdec_cols", [128, 16], F32))
        gc_cols = ent(nc.sbuf_tensor("gc_cols", [128, 16], F32))
        wrep = ent(nc.sbuf_tensor("wrep", [128, 128], F32))
        bq_c = ent(nc.sbuf_tensor("bq_c", [128, 16], F32))
        bk_c = ent(nc.sbuf_tensor("bk_c", [128, 16], F32))
        bv_c = ent(nc.sbuf_tensor("bv_c", [128, 16], F32))
        bg_c = ent(nc.sbuf_tensor("bg_c", [128, 16], F32))
        gnw_c = ent(nc.sbuf_tensor("gnw_c", [128, 16], F32))
        gnb_c = ent(nc.sbuf_tensor("gnb_c", [128, 16], F32))
        su_c = ent(nc.sbuf_tensor("su_c", [128, 16], F32))
        sup_c = ent(nc.sbuf_tensor("sup_c", [128, 64], F32))
        sgt_c = ent(nc.sbuf_tensor("sgt_c", [128, 64], F32))
        row16 = ent(nc.sbuf_tensor("row16", [1, 16], F32))
        row128 = ent(nc.sbuf_tensor("row128", [1, 128], F32))
        stat16 = ent(nc.sbuf_tensor("stat16", [16, 8], F32))
        gsc_c = ent(nc.sbuf_tensor("gsc_c", [128, 16], F32))
        gbi_c = ent(nc.sbuf_tensor("gbi_c", [128, 16], F32))
        accs = ent(nc.sbuf_tensor("accs", [128, 32], F32))
        a32 = ent(nc.sbuf_tensor("a32", [32, 4], F32))

        block = ent(nc.Block())
        P = Prog(nc, block, sems)

        v_hp = P.dma("sync", "Ls", hp[:], h128p[:, :])
        v_hn = P.dma("sync", "Ls", hn[:], h128n[:, :])
        v_hu = P.dma("sync", "Ls", hu[:], h128u[:, :])
        colpairs = [(bq_c, bq), (bk_c, bk), (bv_c, bv), (bg_c, bg),
                    (gnw_c, gnw), (gnb_c, gnb), (su_c, su_ug)]
        v_cols = 0
        for dst, src in colpairs:
            v_cols = P.dma("sync", "Ls", dst[:], src.rearrange("(c p) -> p c", p=128),
                           allow_slow_non_contiguous=True)
        v_sup = P.dma("sync", "Ls", sup_c[:], ws_ug[0:INTER].rearrange("(c p) -> p c", p=128),
                      allow_slow_non_contiguous=True)
        v_sgt = P.dma("sync", "Ls", sgt_c[:], ws_ug[INTER:2 * INTER].rearrange("(c p) -> p c", p=128),
                      allow_slow_non_contiguous=True)
        sudp_c = sup_c
        v_lngrow = P.dma("sync", "Ls", row16[0:1, :], gam.rearrange("(o f) -> o f", o=1))
        v_em = P.dma("sync", "Ls", row128[0:1, 0:NC], emask.rearrange("(o f) -> o f", o=1))
        v_ee = P.dma("sync", "Ls", row128[0:1, 8:8 + NC], eexp.rearrange("(o f) -> o f", o=1))

        grow = ent(nc.sbuf_tensor("grow", [16, 128], F32))
        gbrow = ent(nc.sbuf_tensor("gbrow", [16, 128], F32))
        srow = ent(nc.sbuf_tensor("srow", [16, 128], F32))
        brow = ent(nc.sbuf_tensor("brow", [16, 128], F32))
        v_grow = P.dma("sync", "Ls", grow[:], gnw.rearrange("(p f) -> p f", p=16))
        v_gbrow = P.dma("sync", "Ls", gbrow[:], gnb.rearrange("(p f) -> p f", p=16))
        sudp_cols = ent(nc.sbuf_tensor("sudp_cols", [128, 64], F32))
        v_sudp = P.dma("sync", "Ls", sudp_cols[:], su_dp.rearrange("(c p) -> p c", p=128),
                       allow_slow_non_contiguous=True)

        c_ones = P.inc("gpsimd", "gp", lambda e: e.memset(ones1[:], 1.0))
        c_onesb = P.inc("gpsimd", "gp", lambda e: e.memset(onesb[:], 1.0))
        c_eps = P.inc("gpsimd", "gp", lambda e: e.memset(epsc[:], EPS))
        c_nh = P.inc("gpsimd", "gp", lambda e: e.memset(nhalf[:], -0.5))
        c_iojmp = P.inc("gpsimd", "gp", lambda e: e.iota(
            io_jmp[:], pattern=[[1, 128]], base=0, channel_multiplier=-1,
            allow_small_or_imprecise_dtypes=True))
        c_iojp1 = P.inc("gpsimd", "gp", lambda e: e.iota(
            io_jp1[:], pattern=[[1, 128]], base=1, channel_multiplier=0,
            allow_small_or_imprecise_dtypes=True))
        c_iokd = P.inc("gpsimd", "gp", lambda e: e.iota(
            io_kd[:], pattern=[[1, 1]], base=127, channel_multiplier=-1,
            allow_small_or_imprecise_dtypes=True))
        idf = ent(nc.sbuf_tensor("idf", [128, 128], F32))
        c_idf0 = P.inc("gpsimd", "gp", lambda e: e.memset(idf[:], 1.0))
        c_idf = P.inc("gpsimd", "gp", lambda e: e.affine_select(
            idf[:], idf[:], pattern=[[1, 128]], compare_op=ALU.is_equal,
            fill=0.0, base=0, channel_multiplier=-1))
        P.wait("vector", "gp", c_idf)
        c_i128b = P.inc("vector", "dv", lambda e: e.tensor_copy(i128b[:], idf[:]))

        P.wait("scalar", "Ls", v_lngrow)
        c_lng = P.inc("scalar", "ac", lambda e: e.activation(lng_row[0:1, :], row16[0:1, :], AF.Ln))
        P.wait("tensor", "gp", c_ones)
        P.wait("tensor", "ac", c_lng)
        P.use_bank("pb0")
        c_mm1 = P.inc("tensor", "pe", lambda e: e.matmul(
            PB[0][:, 0:16], ones1[0:1, 0:128], lng_row[0:1, :], start=True, stop=True))
        P.wait("vector", "pe", c_mm1)
        c_lngc = P.inc("vector", "dv", lambda e: e.tensor_copy(lng_cols[:], PB[0][:, 0:16]))
        P.free_bank("pb0", "dv", c_lngc)

        P.wait("scalar", "gp", c_iokd)
        P.wait("scalar", "dv", c_lngc)
        c_kd = 0
        for h in range(H):
            c_kd = P.inc("scalar", "ac", lambda e, h=h: e.activation(
                kdec_cols[:, h:h + 1], io_kd[:], AF.Exp, scale=lng_cols[:, h:h + 1]))
        c_l128 = P.inc("vector", "dv", lambda e: e.tensor_scalar_mul(row16[0:1, :], lng_row[0:1, :], 128.0))
        P.wait("scalar", "dv", c_l128)
        c_g128 = P.inc("scalar", "ac", lambda e: e.activation(row16[0:1, :], row16[0:1, :], AF.Exp))
        P.wait("tensor", "ac", c_g128)
        P.use_bank("pb0")
        c_mm2 = P.inc("tensor", "pe", lambda e: e.matmul(
            PB[0][:, 0:16], ones1[0:1, 0:128], row16[0:1, :], start=True, stop=True))
        P.wait("vector", "pe", c_mm2)
        c_gcc = P.inc("vector", "dv", lambda e: e.tensor_copy(gc_cols[:], PB[0][:, 0:16]))
        P.free_bank("pb0", "dv", c_gcc)

        lng16 = ent(nc.sbuf_tensor("lng16", [16, 1], F32))
        P.wait("sync", "Ls", v_lngrow)
        v_g16 = P.dma("sync", "Ls", lng16[:], gam.rearrange("(p o) -> p o", o=1))
        P.wait("scalar", "Ls", v_g16)
        c_lng16 = P.inc("scalar", "ac", lambda e: e.activation(lng16[:], lng16[:], AF.Ln))
        P.wait("vector", "ac", c_lng16)
        c_lng16b = P.inc("vector", "dv", lambda e: e.tensor_scalar_mul(lng16[:], lng16[:], 1024.0))
        P.wait("tensor", "Ls", v_ee)
        P.use_bank("pb1")
        c_mm3 = P.inc("tensor", "pe", lambda e: e.matmul(
            PB[1][0:16, 0:8], ones1[0:1, 0:16], row128[0:1, 8:8 + NC], start=True, stop=True))
        c_mm3b = P.inc("tensor", "pe", lambda e: e.matmul(
            PB[1][0:16, 8:16], ones1[0:1, 0:16], row128[0:1, 0:NC], start=True, stop=True))
        P.wait("scalar", "pe", c_mm3b)
        P.wait("scalar", "dv", c_lng16b)
        c_w16 = P.inc("scalar", "ac", lambda e: e.activation(
            stat16[:, 0:8], PB[1][0:16, 0:8], AF.Exp, scale=lng16[:]))
        P.wait("vector", "ac", c_w16)
        c_w16b = P.inc("vector", "dv", lambda e: e.tensor_mul(stat16[:, 0:8], stat16[:, 0:8], PB[1][0:16, 8:16]))
        P.free_bank("pb1", "dv", c_w16b)
        P.wait("gpsimd", "dv", c_w16b)
        v_fl1 = P.dma("gpsimd", "Lg", dsc[0:128].rearrange("(p f) -> p f", p=16), stat16[:, 0:8],
                      allow_slow_non_contiguous=True)
        P.wait("gpsimd", "Lg", v_fl1)
        v_fl2 = P.dma("gpsimd", "Lg", row128[0:1, :], dsc[0:128].rearrange("(o f) -> o f", o=1))
        P.wait("tensor", "Lg", v_fl2)
        P.use_bank("pb1")
        c_mm4 = P.inc("tensor", "pe", lambda e: e.matmul(
            PB[1][:, 0:128], ones1[0:1, 0:128], row128[0:1, :], start=True, stop=True))
        P.wait("vector", "pe", c_mm4)
        c_wrep = P.inc("vector", "dv", lambda e: e.tensor_copy(wrep[:], PB[1][:, 0:128]))
        P.free_bank("pb1", "dv", c_wrep)

        P.wait("vector", "Ls", max(v_sup, v_sgt, v_sudp))
        c_sup = P.inc("vector", "dv", lambda e: e.tensor_scalar_mul(sup_c[:], sup_c[:], SCALE))
        c_sup2 = P.inc("vector", "dv", lambda e: e.tensor_mul(sup_c[:], sup_c[:], sudp_cols[:]))
        c_sgt = P.inc("vector", "dv", lambda e: e.tensor_scalar_mul(sgt_c[:], sgt_c[:], SCALE))

        P.barrier()
        P.flush()

        with ExitStack() as ph_g:
            g_fm = ph_g.enter_context(nc.sbuf_tensor("g_fm", [128, H * 1024], BF16))
            wosl = ph_g.enter_context(nc.sbuf_tensor("wosl", [128, 16 * 512], BF16))
            bo_tab = ph_g.enter_context(nc.sbuf_tensor("bo_tab", [128, 2048], BF16))
            ret = ph_g.enter_context(nc.sbuf_tensor("ret", [128, H * 1024], BF16))
            sin = ph_g.enter_context(nc.sbuf_tensor("sin", [128, H * 128], F32))

            with ExitStack() as ph_a:
                ea = ph_a.enter_context
                x_fm = ea(nc.sbuf_tensor("x_fm", [128, 16 * 1024], BF16))
                qr = ea(nc.sbuf_tensor("qr", [128, 4 * 1024], BF16))
                kr = ea(nc.sbuf_tensor("kr", [128, 4 * 1024], BF16))
                vr = ea(nc.sbuf_tensor("vr", [128, 4 * 1024], BF16))
                vfm = ea(nc.sbuf_tensor("vfm", [128, 1024], BF16))
                wsl = [ea(nc.sbuf_tensor(f"wsl{i}", [128, 2048], BF16)) for i in range(6)]
                xst = ea(nc.sbuf_tensor("xst", [128, 2048], F32))
                lamb = [ea(nc.sbuf_tensor(f"lam{i}", [128, 128], F32)) for i in range(2)]
                dtab = [ea(nc.sbuf_tensor(f"dtab{i}", [128, 128], F32)) for i in range(2)]
                qdec = [ea(nc.sbuf_tensor(f"qdec{i}", [128, 128], BF16)) for i in range(2)]
                srun = ea(nc.sbuf_tensor("srun", [128, 128], F32))
                khA = ea(nc.sbuf_tensor("khA", [128, 1024], BF16))
                khF = ea(nc.sbuf_tensor("khF", [128, 1024], BF16))
                scmb = ea(nc.sbuf_tensor("scmb", [128, 1024], BF16))
                qdqb = ea(nc.sbuf_tensor("qdqb", [128, 1024], BF16))
                sbst = ea(nc.sbuf_tensor("sbst", [128, 8 * 128], BF16))
                gbuf = ea(nc.sbuf_tensor("gbuf", [128, 512], F32))

                for tc in range(8):
                    vx = P.dma("sync", "Ls", xst[:], xin[tc * 128:(tc + 1) * 128, :])
                    P.wait("tensor", "Ls", vx)
                    for grp in range(4):
                        bank = f"pb{2 + grp % 2}"
                        pb = PB[2 + grp % 2]
                        P.use_bank(bank)
                        c_tr = 0
                        for k in range(4):
                            fc = grp * 4 + k
                            c_tr = P.inc("tensor", "pe", lambda e, fc=fc, k=k, pb=pb: e.transpose(
                                pb[:, k * 128:(k + 1) * 128],
                                xst[:, fc * 128:(fc + 1) * 128], idf[:]))
                        P.wait("vector", "pe", c_tr)
                        dst = x_fm.ap()[:, grp * 4 * 1024 + tc * 128:]
                        dst = bass.AP(tensor=dst.tensor, offset=dst.offset,
                                      ap=[[dst.ap[0][0], 128], [1024, 4], [1, 128]])
                        c_ev = P.inc("vector", "dv", lambda e, dst=dst, pb=pb: e.tensor_copy(dst, pb[:, 0:512].rearrange("p (a b) -> p a b", a=4)))
                        P.free_bank(bank, "dv", c_ev)
                    P.wait("sync", "pe", P.cnt["pe"])
                P.wait("tensor", "dv", P.cnt["dv"])

                LAG = 3
                fullpass_state = {}

                def emit_wload(h):
                    if h >= H:
                        return
                    bufs = (wsl[0 + (h % 2)], wsl[2 + (h % 2)], wsl[4 + (h % 2)])
                    prev = fullpass_state.get(("wsl", h - 2))
                    if prev:
                        P.wait("gpsimd", "pe", prev)
                    vq = P.dma("gpsimd", "Lg", bufs[0].ap().rearrange("p (a b) -> p a b", a=16), wq_t[h].rearrange("a p b -> p a b"))
                    vk = P.dma("gpsimd", "Lg", bufs[1].ap().rearrange("p (a b) -> p a b", a=16), wk_t[h].rearrange("a p b -> p a b"))
                    vv = P.dma("gpsimd", "Lg", bufs[2].ap().rearrange("p (a b) -> p a b", a=16), wv_t[h].rearrange("a p b -> p a b"))
                    fullpass_state[("wld", h)] = (vq, vk, vv)

                def emit_proj(h):
                    slot = h % 4
                    bufs = (wsl[0 + (h % 2)], wsl[2 + (h % 2)], wsl[4 + (h % 2)])
                    vq, vk, vv = fullpass_state[("wld", h)]
                    prev2 = fullpass_state.get(("fp", h - 4))
                    if prev2:
                        P.wait("tensor", "pe", prev2)
                    c_pe_last = 0
                    for which, (wbuf, vdone, dst, bias) in enumerate(
                            [(bufs[0], vq, qr, bq_c), (bufs[1], vk, kr, bk_c), (bufs[2], vv, vfm, bv_c)]):
                        P.wait("tensor", "Lg", vdone)
                        for th in range(2):
                            bank = f"pb{(which * 2 + th) % 2}"
                            pb = PB[(which * 2 + th) % 2]
                            P.use_bank(bank)
                            for fc in range(16):
                                c_pe_last = P.inc("tensor", "pe", lambda e, pb=pb, wbuf=wbuf, fc=fc, th=th: e.matmul(
                                    pb[:], wbuf[:, fc * 128:(fc + 1) * 128],
                                    x_fm[:, fc * 1024 + th * 512: fc * 1024 + (th + 1) * 512],
                                    start=(fc == 0), stop=(fc == 15)))
                            P.wait("scalar", "pe", c_pe_last)
                            if which < 2:
                                out_ap = dst[:, slot * 1024 + th * 512: slot * 1024 + (th + 1) * 512]
                            else:
                                out_ap = dst[:, th * 512:(th + 1) * 512]
                            c_ev = P.inc("scalar", "ac", lambda e, out_ap=out_ap, pb=pb, bias=bias, h=h: e.activation(
                                out_ap, pb[:], AF.Identity, bias=bias[:, h:h + 1]))
                            P.free_bank(bank, "ac", c_ev)
                    c_evv = 0
                    for grp in range(2):
                        tb = grp % 2
                        P.use_bank(f"tb{tb}")
                        c_tr = 0
                        for k in range(4):
                            ci = grp * 4 + k
                            P.wait("tensor", "ac", P.cnt["ac"])
                            c_tr = P.inc("tensor", "pe", lambda e, ci=ci, k=k, tb=tb: e.transpose(
                                TB[tb][:, k * 128:(k + 1) * 128],
                                vfm[:, ci * 128:(ci + 1) * 128], i128b[:]))
                        P.wait("scalar", "pe", c_tr)
                        dst = vr.ap()[:, slot * 1024 + grp * 512:]
                        dst = bass.AP(tensor=dst.tensor, offset=dst.offset,
                                      ap=[[dst.ap[0][0], 128], [128, 4], [1, 128]])
                        c_evv = P.inc("scalar", "ac", lambda e, dst=dst, tb=tb: e.copy(
                            dst, TB[tb][:, 0:512].rearrange("p (a b) -> p a b", a=4)))
                        P.free_bank(f"tb{tb}", "ac", c_evv)
                    fullpass_state[("wsl", h)] = c_pe_last
                    fullpass_state[("proj", h)] = c_evv

                def emit_passA_khat(h, ast):
                    slot = h % 4
                    pv = fullpass_state.get(("vput", h - 2))
                    if pv:
                        P.wait("vector", "Lg", pv)
                    P.wait("tensor", "ac", fullpass_state[("proj", h)])
                    P.use_bank("tb0")
                    c_tr = 0
                    for ci in range(NCH):
                        c_tr = P.inc("tensor", "pe", lambda e, h=h, ci=ci, slot=slot: e.transpose(
                            TB[0][:, ci * 128:(ci + 1) * 128],
                            kr[:, slot * 1024 + ci * 128: slot * 1024 + (ci + 1) * 128], i128b[:]))
                    P.wait("scalar", "pe", c_tr)
                    c_kh = P.inc("scalar", "ac", lambda e, h=h: e.activation(
                        khA[:], TB[0][:], AF.Identity, scale=kdec_cols[:, h:h + 1]))
                    P.free_bank("tb0", "ac", c_kh)
                    ast["kh"] = c_kh

                def emit_passA_ds(h, ast):
                    slot = h % 4
                    lam = lamb[h % 2]
                    P.wait("tensor", "ac", ast["kh"])
                    c_su = 0
                    for rnd in range(2):
                        P.use_bank("pb5")
                        c_ds = 0
                        for k in range(4):
                            ci = rnd * 4 + k
                            c_ds = P.inc("tensor", "pe", lambda e, h=h, ci=ci, k=k, slot=slot: e.matmul(
                                PB[5][:, k * 128:(k + 1) * 128], khA[:, ci * 128:(ci + 1) * 128],
                                vr[:, slot * 1024 + ci * 128: slot * 1024 + (ci + 1) * 128],
                                start=True, stop=True))
                        P.wait("vector", "pe", c_ds)
                        for k in range(4):
                            ci = rnd * 4 + k
                            if ci == 0:
                                c_su = P.inc("vector", "dv", lambda e, lam=lam, k=k: e.tensor_copy(
                                    lam[:], PB[5][:, k * 128:(k + 1) * 128]))
                            else:
                                c_su = P.inc("vector", "dv", lambda e, lam=lam, h=h, k=k: e.scalar_tensor_tensor(
                                    lam[:], lam[:], gc_cols[:, h:h + 1], PB[5][:, k * 128:(k + 1) * 128],
                                    ALU.mult, ALU.add))
                        P.free_bank("pb5", "dv", c_su)
                    ast["su"] = c_su

                def emit_passA_cc(h, ast):
                    lam = lamb[h % 2]
                    P.wait("gpsimd", "dv", ast["su"])
                    vput = P.dma("gpsimd", "Lg", cc_si[h].ap(), lam[:])
                    fullpass_state[("vput", h)] = vput
                    P.wait("gpsimd", "Lg", vput)
                    c_cc = P.inc("gpsimd", "cc", lambda e, h=h: e.collective_compute(
                        "AllGather", ALU.bypass, replica_groups=[list(range(NC))],
                        ins=[cc_si[h].ap().opt()], outs=[cc_so[h].ap().opt()]))
                    fullpass_state[("cc", h)] = c_cc

                def emit_fullpass_batches(h, fst):
                    slot = h % 4
                    dt = dtab[h % 2]
                    qd = qdec[h % 2]
                    prev_fp = fullpass_state.get(("fp", h - 1), 0)
                    prev = fullpass_state.get(("dt", h - 2))
                    if prev:
                        P.wait("scalar", "dv", prev)
                    P.wait("scalar", "pe", prev_fp)
                    c_d1 = P.inc("scalar", "ac", lambda e, dt=dt, h=h: e.activation(
                        dt[:], io_jmp[:], AF.Exp, scale=lng_cols[:, h:h + 1]))
                    P.wait("gpsimd", "ac", c_d1)
                    c_d2 = P.inc("gpsimd", "gp", lambda e, dt=dt: e.affine_select(
                        dt[:], dt[:], pattern=[[1, 128]], compare_op=ALU.is_ge,
                        fill=0.0, base=0, channel_multiplier=-1))
                    c_q1 = P.inc("scalar", "ac", lambda e, qd=qd, h=h: e.activation(
                        qd[:], io_jp1[:], AF.Exp, scale=lng_cols[:, h:h + 1]))
                    P.wait("sync", "cc", fullpass_state[("cc", h)])
                    P.wait("sync", "dv", fullpass_state.get(("si", h - 1), 0))
                    P.wait("vector", "pe", prev_fp)
                    c_z = P.inc("vector", "dv", lambda e, h=h: e.memset(sin[:, h * 128:(h + 1) * 128], 0.0))
                    c_si = c_z
                    for half in range(2):
                        src_ap = cc_so[h].ap()[half * 512:(half + 1) * 512, :]
                        vgb = P.dma("sync", "Ls", gbuf.ap().rearrange("p (c dv) -> p c dv", c=4),
                                    src_ap.rearrange("(c dk) dv -> dk c dv", dk=128))
                        P.wait("vector", "Ls", vgb)
                        for cp4 in range(4):
                            cp = half * 4 + cp4
                            c_si = P.inc("vector", "dv", lambda e, h=h, cp=cp, cp4=cp4: e.scalar_tensor_tensor(
                                sin[:, h * 128:(h + 1) * 128], gbuf[:, cp4 * 128:(cp4 + 1) * 128],
                                wrep[:, h * 8 + cp: h * 8 + cp + 1], sin[:, h * 128:(h + 1) * 128],
                                ALU.mult, ALU.add))
                        if half == 0:
                            P.wait("sync", "dv", c_si)
                    fullpass_state[("si", h)] = c_si
                    P.wait("vector", "ac", fullpass_state.get(("stc", h - 1), 0))
                    c_sr = P.inc("vector", "dv", lambda e, h=h: e.tensor_copy(srun[:], sin[:, h * 128:(h + 1) * 128]))
                    P.wait("scalar", "dv", c_sr)
                    c_sb0 = P.inc("scalar", "ac", lambda e: e.activation(sbst[:, 0:128], srun[:], AF.Copy))
                    c_scm = 0
                    for rnd in range(2):
                        P.use_bank("pb2")
                        if rnd == 0:
                            P.wait("tensor", "gp", c_d2)
                        c_s = 0
                        for k in range(4):
                            ci = rnd * 4 + k
                            qsl = qr[:, slot * 1024 + ci * 128: slot * 1024 + (ci + 1) * 128]
                            ksl = kr[:, slot * 1024 + ci * 128: slot * 1024 + (ci + 1) * 128]
                            c_s = P.inc("tensor", "pe", lambda e, ksl=ksl, qsl=qsl, k=k: e.matmul(
                                PB[2][:, k * 128:(k + 1) * 128], ksl, qsl, start=True, stop=True))
                        P.wait("vector", "pe", c_s)
                        for k in range(4):
                            ci = rnd * 4 + k
                            c_scm = P.inc("vector", "dv", lambda e, dt=dt, ci=ci, k=k: e.tensor_mul(
                                scmb[:, ci * 128:(ci + 1) * 128], PB[2][:, k * 128:(k + 1) * 128], dt[:]))
                        P.free_bank("pb2", "dv", c_scm)
                    P.wait("vector", "ac", c_q1)
                    c_qd = 0
                    for ci in range(NCH):
                        qsl = qr[:, slot * 1024 + ci * 128: slot * 1024 + (ci + 1) * 128]
                        c_qd = P.inc("vector", "dv", lambda e, qsl=qsl, qd=qd, ci=ci: e.tensor_mul(
                            qdqb[:, ci * 128:(ci + 1) * 128], qsl, qd[:]))
                    P.use_bank("tb1")
                    c_tr = 0
                    for ci in range(NCH - 1):
                        ksl = kr[:, slot * 1024 + ci * 128: slot * 1024 + (ci + 1) * 128]
                        c_tr = P.inc("tensor", "pe", lambda e, ksl=ksl, ci=ci: e.transpose(
                            TB[1][:, ci * 128:(ci + 1) * 128], ksl, i128b[:]))
                    P.wait("scalar", "pe", c_tr)
                    c_kh = P.inc("scalar", "ac", lambda e, h=h: e.activation(
                        khF[:, 0:896], TB[1][:, 0:896], AF.Identity, scale=kdec_cols[:, h:h + 1]))
                    P.free_bank("tb1", "ac", c_kh)
                    fst.update(kh=c_kh, sb0=c_sb0, qd=c_qd, scm=c_scm)

                def emit_fullpass_chain(h, fst):
                    slot = h % 4
                    P.wait("tensor", "ac", fst["kh"])
                    c_stc = fst["sb0"]
                    for rnd in range(2):
                        nk = 4 if rnd == 0 else 3
                        P.use_bank("pb4")
                        c_ds = 0
                        for k in range(nk):
                            ci = rnd * 4 + k
                            vsl = vr[:, slot * 1024 + ci * 128: slot * 1024 + (ci + 1) * 128]
                            c_ds = P.inc("tensor", "pe", lambda e, h=h, ci=ci, k=k, vsl=vsl: e.matmul(
                                PB[4][:, k * 128:(k + 1) * 128], khF[:, ci * 128:(ci + 1) * 128],
                                vsl, start=True, stop=True))
                        P.wait("vector", "pe", c_ds)
                        c_su2 = 0
                        for k in range(nk):
                            ci = rnd * 4 + k
                            P.wait("vector", "ac", c_stc)
                            c_su2 = P.inc("vector", "dv", lambda e, h=h, k=k: e.scalar_tensor_tensor(
                                srun[:], srun[:], gc_cols[:, h:h + 1], PB[4][:, k * 128:(k + 1) * 128],
                                ALU.mult, ALU.add))
                            P.wait("scalar", "dv", c_su2)
                            c_stc = P.inc("scalar", "ac", lambda e, ci=ci: e.activation(
                                sbst[:, (ci + 1) * 128:(ci + 2) * 128], srun[:], AF.Copy))
                        P.free_bank("pb4", "dv", c_su2)
                    fst["stc"] = c_stc

                def emit_fullpass_ret(h, fst):
                    slot = h % 4
                    c_stc = fst["stc"]
                    P.wait("tensor", "ac", c_stc)
                    P.wait("tensor", "dv", max(fst["qd"], fst["scm"]))
                    last_pe = 0
                    c_rv = 0
                    for rnd in range(2):
                        P.use_bank("pb3")
                        for k in range(4):
                            ci = rnd * 4 + k
                            vsl = vr[:, slot * 1024 + ci * 128: slot * 1024 + (ci + 1) * 128]
                            P.inc("tensor", "pe", lambda e, vsl=vsl, ci=ci, k=k: e.matmul(
                                PB[3][:, k * 128:(k + 1) * 128], scmb[:, ci * 128:(ci + 1) * 128],
                                vsl, start=True, stop=False))
                            last_pe = P.inc("tensor", "pe", lambda e, ci=ci, k=k: e.matmul(
                                PB[3][:, k * 128:(k + 1) * 128], qdqb[:, ci * 128:(ci + 1) * 128],
                                sbst[:, ci * 128:(ci + 1) * 128], start=False, stop=True))
                        P.wait("scalar", "pe", last_pe)
                        c_rv = P.inc("scalar", "ac", lambda e, h=h, rnd=rnd: e.copy(
                            ret[:, h * 1024 + rnd * 512: h * 1024 + (rnd + 1) * 512], PB[3][:]))
                        P.free_bank("pb3", "ac", c_rv)
                    P.wait("vector", "ac", c_rv)
                    c_st1 = P.inc("vector", "dv", lambda e, h=h: e.tensor_scalar(
                        xst[:, 0:1024], ret[:, h * 1024:(h + 1) * 1024], 0.0, None, ALU.add, ALU.add,
                        accum_out=accs[:, h:h + 1]))
                    c_st2 = P.inc("vector", "dv", lambda e, h=h: e.scalar_tensor_tensor(
                        xst[:, 0:1024], ret[:, h * 1024:(h + 1) * 1024], 1.0,
                        ret[:, h * 1024:(h + 1) * 1024], ALU.mult, ALU.mult,
                        accum_out=accs[:, 16 + h:17 + h]))
                    fullpass_state[("dt", h)] = c_st2
                    fullpass_state[("fp", h)] = last_pe
                    fullpass_state[("stc", h)] = c_stc

                emit_wload(0)
                for h in range(H):
                    emit_wload(h + 1)
                    emit_proj(h)
                    fp = h - LAG if h >= LAG else None
                    fst = fullpass_state.get(("fst", fp)) if fp is not None else None
                    if fp is not None:
                        fst = {}
                        emit_fullpass_batches(fp, fst)
                    ast = {}
                    emit_passA_khat(h, ast)
                    if fp is not None:
                        emit_fullpass_chain(fp, fst)
                    emit_passA_ds(h, ast)
                    if fp is not None:
                        emit_fullpass_ret(fp, fst)
                    emit_passA_cc(h, ast)
                for h in range(H - LAG, H):
                    fst = {}
                    emit_fullpass_batches(h, fst)
                    emit_fullpass_chain(h, fst)
                    emit_fullpass_ret(h, fst)

                for oc in range(16):
                    wbuf = wsl[oc % 2]
                    if oc < 2:
                        P.wait("gpsimd", "pe", fullpass_state.get(("wsl", 14 + oc), 0))
                    else:
                        P.wait("gpsimd", "pe", fullpass_state.get(("gw", oc - 2), 0))
                    vw = P.dma("gpsimd", "Lg", wbuf.ap().rearrange("p (a b) -> p a b", a=16), wg_t[oc].rearrange("a p b -> p a b"))
                    P.wait("tensor", "Lg", vw)
                    c_pe = 0
                    for th in range(2):
                        bank = f"pb{th}"
                        pb = PB[th]
                        P.use_bank(bank)
                        for fc in range(16):
                            c_pe = P.inc("tensor", "pe", lambda e, pb=pb, wbuf=wbuf, fc=fc, th=th: e.matmul(
                                pb[:], wbuf[:, fc * 128:(fc + 1) * 128],
                                x_fm[:, fc * 1024 + th * 512: fc * 1024 + (th + 1) * 512],
                                start=(fc == 0), stop=(fc == 15)))
                        P.wait("scalar", "pe", c_pe)
                        c_g = P.inc("scalar", "ac", lambda e, oc=oc, th=th, pb=pb: e.activation(
                            g_fm[:, oc * 1024 + th * 512: oc * 1024 + (th + 1) * 512],
                            pb[:], AF.Silu, bias=bg_c[:, oc:oc + 1]))
                        P.free_bank(bank, "ac", c_g)
                    fullpass_state[("gw", oc)] = c_pe

                P.barrier(("tensor", "vector", "scalar", "gpsimd"))
                P.use_bank("pb5")
                c_tr = P.inc("tensor", "pe", lambda e: e.transpose(
                    PB[5][0:32, 0:128], accs[:], idf[:]))
                P.wait("vector", "pe", c_tr)
                c_red = P.inc("vector", "dv", lambda e: e.tensor_reduce(
                    a32[:, 0:1], PB[5][0:32, 0:128], mybir.AxisListType.X, ALU.add))
                P.free_bank("pb5", "dv", c_red)
                P.wait("gpsimd", "dv", c_red)
                v_st = P.dma("gpsimd", "Lg", cc_ti.ap().rearrange("(p o) -> p o", o=1), a32[:, 0:1])
                P.wait("gpsimd", "Lg", v_st)
                c_ccs = P.inc("gpsimd", "cc", lambda e: e.collective_compute(
                    "AllReduce", ALU.add, replica_groups=[list(range(NC))],
                    ins=[cc_ti.ap().opt()], outs=[cc_to.ap().opt()]))
                P.wait("sync", "cc", c_ccs)
                v_sb = P.dma("sync", "Ls", stat16[:, 0:2], cc_to.ap().rearrange("(b a) -> a b", b=2),
                             allow_slow_non_contiguous=True)
                NSTAT = float(S * DK)
                P.wait("vector", "Ls", v_sb)
                c_m = P.inc("vector", "dv", lambda e: e.tensor_scalar_mul(stat16[:, 0:2], stat16[:, 0:2], 1.0 / NSTAT))
                P.wait("scalar", "dv", c_m)
                c_m2 = P.inc("scalar", "ac", lambda e: e.activation(stat16[:, 2:3], stat16[:, 0:1], AF.Square))
                P.wait("vector", "ac", c_m2)
                c_v = P.inc("vector", "dv", lambda e: e.tensor_sub(stat16[:, 1:2], stat16[:, 1:2], stat16[:, 2:3]))
                P.wait("scalar", "dv", c_v)
                c_sq = P.inc("scalar", "ac", lambda e: e.activation(stat16[:, 1:2], stat16[:, 1:2], AF.Ln, bias=epsc[0:16, :]))
                c_rc = P.inc("scalar", "ac", lambda e: e.activation(stat16[:, 1:2], stat16[:, 1:2], AF.Exp, scale=nhalf[0:16, :]))
                P.wait("vector", "ac", c_rc)
                c_sr2 = P.inc("vector", "dv", lambda e: e.tensor_scalar_mul(srow[:], grow[:], stat16[:, 1:2]))
                c_br1 = P.inc("vector", "dv", lambda e: e.tensor_scalar_mul(brow[:], srow[:], stat16[:, 0:1]))
                c_br2 = P.inc("vector", "dv", lambda e: e.tensor_sub(brow[:], gbrow[:], brow[:]))
                P.use_bank("pb5")
                P.wait("tensor", "dv", c_br2)
                c_t1x = P.inc("tensor", "pe", lambda e: e.transpose(
                    PB[5][:, 0:16], srow[:], idf[0:16, 0:16]))
                c_t2x = P.inc("tensor", "pe", lambda e: e.transpose(
                    PB[5][:, 16:32], brow[:], idf[0:16, 0:16]))
                P.wait("vector", "pe", c_t2x)
                c_gs = P.inc("vector", "dv", lambda e: e.tensor_copy(gsc_c[:], PB[5][:, 0:16]))
                c_gb2 = P.inc("vector", "dv", lambda e: e.tensor_copy(gbi_c[:], PB[5][:, 16:32]))
                P.free_bank("pb5", "dv", c_gb2)
                bo_row = xst
                v_borow = P.dma("sync", "Ls", bo_row[0:1, 0:2048], bo.rearrange("(o f) -> o f", o=1))
                v_wo0 = P.dma("gpsimd", "Lg", wosl.ap().rearrange("p (a b) -> p a b", a=16),
                              wo_t[0].rearrange("a p b -> p a b"))
                P.wait("tensor", "Ls", v_borow)
                for gg in range(4):
                    P.use_bank("pb5")
                    c_bo = P.inc("tensor", "pe", lambda e, gg=gg: e.matmul(
                        PB[5][:, 0:512], ones1[0:1, 0:128], bo_row[0:1, gg * 512:(gg + 1) * 512],
                        start=True, stop=True))
                    P.wait("vector", "pe", c_bo)
                    c_ev = P.inc("vector", "dv", lambda e, gg=gg: e.tensor_copy(
                        bo_tab[:, gg * 512:(gg + 1) * 512], PB[5][:, 0:512]))
                    P.free_bank("pb5", "dv", c_ev)

                P.barrier(("tensor", "vector", "scalar"))
                for h in range(H):
                    for cq in range(2):
                        tb = (h * 2 + cq) % 2
                        P.use_bank(f"tb{tb}")
                        c_tr = 0
                        for k in range(4):
                            ci = cq * 4 + k
                            c_tr = P.inc("tensor", "pe", lambda e, h=h, ci=ci, k=k, tb=tb: e.transpose(
                                TB[tb][:, k * 128:(k + 1) * 128],
                                ret[:, h * 1024 + ci * 128: h * 1024 + (ci + 1) * 128], i128b[:]))
                        P.wait("vector", "pe", c_tr)
                        c_t1 = P.inc("vector", "dv", lambda e, tb=tb, h=h: e.tensor_scalar(
                            khA[:, 0:512], TB[tb][:, 0:512], gsc_c[:, h:h + 1], gbi_c[:, h:h + 1], ALU.mult, ALU.add))
                        P.free_bank(f"tb{tb}", "dv", c_t1)
                        gsl = g_fm[:, h * 1024 + cq * 512: h * 1024 + (cq + 1) * 512]
                        c_t2 = P.inc("vector", "dv", lambda e, gsl=gsl: e.tensor_mul(gsl, gsl, khA[:, 0:512]))

                P.barrier()
                P.flush()

            with ExitStack() as ph_wo:
                ew = ph_wo.enter_context
                x2pre = ew(nc.sbuf_tensor("x2pre", [128, 8 * 2048], F32))
                wosl_b = ew(nc.sbuf_tensor("wosl_b", [128, 16 * 512], BF16))
                xrel = ew(nc.sbuf_tensor("xrel", [128, 2048], F32))
                x2s = ew(nc.sbuf_tensor("x2s", [128, 2048], BF16))
                lnst = ew(nc.sbuf_tensor("lnst", [128, 32], F32))
                lnsc = ew(nc.sbuf_tensor("lnsc", [128, 2048], F32))

                cx_done = {}
                for tc in range(8):
                    xr = xrel if tc % 2 == 0 else lnsc
                    if tc >= 2:
                        P.wait("scalar", "dv", cx_done[tc - 2])
                    vx = P.dma("scalar", "La", xr[:], xin[tc * 128:(tc + 1) * 128, :])
                    P.wait("vector", "La", vx)
                    c_x = P.inc("vector", "dv", lambda e, tc=tc, xr=xr: e.tensor_add(
                        x2pre[:, tc * 2048:(tc + 1) * 2048], xr[:], bo_tab[:]))
                    cx_done[tc] = c_x
                og_pe = {}
                vws = {0: v_wo0}
                wosl2 = [wosl, wosl_b]
                ln_stats = {}
                for og in range(4):
                    if og + 1 < 4:
                        P.wait("gpsimd", "pe", og_pe.get(og - 1, 0))
                        vws[og + 1] = P.dma("gpsimd", "Lg",
                                            wosl2[(og + 1) % 2].ap().rearrange("p (a b) -> p a b", a=16),
                                            wo_t[og + 1].rearrange("a p b -> p a b"))
                    wosl = wosl2[og % 2]
                    vw = vws[og]
                    P.wait("tensor", "Lg", vw)
                    for tc in range(8):
                        bank = f"pb{tc % 4}"
                        pb = PB[tc % 4]
                        P.use_bank(bank)
                        c_pe = 0
                        for fc in range(16):
                            c_pe = P.inc("tensor", "pe", lambda e, pb=pb, fc=fc, tc=tc, wosl=wosl: e.matmul(
                                pb[:], g_fm[:, fc * 1024 + tc * 128: fc * 1024 + (tc + 1) * 128],
                                wosl[:, fc * 512:(fc + 1) * 512],
                                start=(fc == 0), stop=(fc == 15)))
                        P.wait("vector", "pe", c_pe)
                        if og == 0:
                            P.wait("vector", "dv", cx_done[tc])
                        dst = x2pre[:, tc * 2048 + og * 512: tc * 2048 + (og + 1) * 512]
                        c_e1 = P.inc("vector", "dv", lambda e, dst=dst, pb=pb: e.tensor_add(
                            dst, dst, pb[:]))
                        P.free_bank(bank, "dv", c_e1)
                        if og == 3:
                            sl = x2pre[:, tc * 2048:(tc + 1) * 2048]
                            c_s1 = P.inc("vector", "dv", lambda e, sl=sl, tc=tc: e.tensor_scalar(
                                lnsc[:], sl, 0.0, None, ALU.add, ALU.add, accum_out=lnst[:, 4 * tc:4 * tc + 1]))
                            c_s2 = P.inc("vector", "dv", lambda e, sl=sl, tc=tc: e.scalar_tensor_tensor(
                                lnsc[:], sl, 1.0, sl, ALU.mult, ALU.mult, accum_out=lnst[:, 4 * tc + 1:4 * tc + 2]))
                            ln_stats[tc] = c_s2
                    og_pe[og] = P.cnt["pe"]

                lap = lnst.ap()
                mcols = bass.AP(tensor=lap.tensor, offset=lap.offset, ap=[[lap.ap[0][0], 128], [4, 8]])
                qcols = bass.AP(tensor=lap.tensor, offset=lap.offset + 1, ap=[[lap.ap[0][0], 128], [4, 8]])
                scols = bass.AP(tensor=lap.tensor, offset=lap.offset + 2, ap=[[lap.ap[0][0], 128], [4, 8]])
                P.wait("scalar", "dv", ln_stats[7])
                c_ms = P.inc("scalar", "ac", lambda e: e.mul(mcols, mcols, 1.0 / D))
                c_qs = P.inc("scalar", "ac", lambda e: e.mul(qcols, qcols, 1.0 / D))
                c_m2 = P.inc("scalar", "ac", lambda e: e.activation(scols, mcols, AF.Square))
                P.wait("vector", "ac", c_m2)
                c_v = P.inc("vector", "dv", lambda e: e.tensor_sub(qcols, qcols, scols))
                P.wait("scalar", "dv", c_v)
                c_sq = P.inc("scalar", "ac", lambda e: e.activation(qcols, qcols, AF.Ln, bias=epsc[:]))
                c_rc = P.inc("scalar", "ac", lambda e: e.activation(qcols, qcols, AF.Exp, scale=nhalf[:]))
                P.wait("vector", "ac", c_rc)
                vx2st = 0
                for tc in range(8):
                    sl = x2pre[:, tc * 2048:(tc + 1) * 2048]
                    P.wait("vector", "Ls", vx2st)
                    c_ap = P.inc("vector", "dv", lambda e, sl=sl, tc=tc: e.tensor_scalar(
                        x2s[:], sl, lnst[:, 4 * tc:4 * tc + 1], lnst[:, 4 * tc + 1:4 * tc + 2],
                        ALU.subtract, ALU.mult))
                    P.wait("sync", "dv", c_ap)
                    vx2st = P.dma("sync", "Ls", x2d[tc * 128:(tc + 1) * 128, :], x2s[:])

                P.barrier()
                P.flush()

        with ExitStack() as ph_f:
            ef = ph_f.enter_context
            x3pre = ef(nc.sbuf_tensor("x3pre", [128, 8 * 2048], BF16))
            sdp_row = ef(nc.sbuf_tensor("sdp_row", [1, 2048], F32))
            abuf = ef(nc.sbuf_tensor("abuf", [128, 64 * TG], BF16))
            sdp_tab = ef(nc.sbuf_tensor("sdp_tab", [128, 2048], F32))
            silu_sb2 = [ef(nc.sbuf_tensor(f"silu_sb{i}", [128, 512], BF16)) for i in range(2)]
            lnst2 = ef(nc.sbuf_tensor("lnst2", [128, 16], F32))
            lnsc2 = ef(nc.sbuf_tensor("lnsc2", [128, 2048], F32))
            x2g = ef(nc.sbuf_tensor("x2g", [128, 4 * 2048], BF16))

            v_sdprow = P.dma("sync", "Ls", sdp_row[0:1, :], ws_dp.rearrange("(o f) -> o f", o=1))
            P.wait("tensor", "Ls", v_sdprow)
            for gg in range(4):
                P.use_bank("pb5")
                c_mm = P.inc("tensor", "pe", lambda e, gg=gg: e.matmul(
                    PB[5][:, 0:512], ones1[0:1, 0:128], sdp_row[0:1, gg * 512:(gg + 1) * 512],
                    start=True, stop=True))
                P.wait("vector", "pe", c_mm)
                c_ev = P.inc("vector", "dv", lambda e, gg=gg: e.tensor_scalar_mul(
                    sdp_tab[:, gg * 512:(gg + 1) * 512], PB[5][:, 0:512], SCALE))
                P.free_bank("pb5", "dv", c_ev)
            P.flush()

            for g in range(NG):
                with ExitStack() as ph_g2:
                    eg = ph_g2.enter_context
                    h1 = eg(nc.sbuf_tensor(f"h1_{g}", [128, 16 * TG], BF16))
                    wdsl = [eg(nc.sbuf_tensor(f"wd{g}_{i}", [128, 4 * 512], BF16)) for i in range(2)]

                    with ExitStack() as ph_xt:
                        xt = ph_xt.enter_context(nc.sbuf_tensor(f"xt_{g}", [128, 16 * TG], BF16))
                        vx2l = 0
                        for tcl in range(TG // 128):
                            tc = g * (TG // 128) + tcl
                            vx2l = P.dma("sync", "Ls", x2g[:, tcl * 2048:(tcl + 1) * 2048],
                                         x2d[tc * 128:(tc + 1) * 128, :])
                        P.wait("tensor", "Ls", vx2l)
                        for fc in range(16):
                            tb = fc % 2
                            P.use_bank(f"tb{tb}")
                            c_tr = 0
                            for tcl in range(4):
                                c_tr = P.inc("tensor", "pe", lambda e, tcl=tcl, fc=fc, tb=tb: e.transpose(
                                    TB[tb][:, tcl * 128:(tcl + 1) * 128],
                                    x2g[:, tcl * 2048 + fc * 128: tcl * 2048 + (fc + 1) * 128], i128b[:]))
                            P.wait("vector", "pe", c_tr)
                            c_ev = P.inc("vector", "dv", lambda e, xt=xt, fc=fc, tb=tb: e.tensor_scalar_mul(
                                xt[:, fc * TG:(fc + 1) * TG], TB[tb][:, 0:512], su_c[:, fc:fc + 1]))
                            P.free_bank(f"tb{tb}", "dv", c_ev)
                        for oc in range(16):
                            for th in range(TG // 512):
                                bank = f"pb{oc % 4}"
                                pb = PB[oc % 4]
                                P.use_bank(bank)
                                c_pe = 0
                                for ic in range(16):
                                    hh = hp if H16S[ic, oc] > 0 else hn
                                    c_pe = P.inc("tensor", "pe", lambda e, pb=pb, hh=hh, ic=ic, th=th: e.matmul(
                                        pb[:], hh[:], xt[:, ic * TG + th * 512: ic * TG + (th + 1) * 512],
                                        start=(ic == 0), stop=(ic == 15)))
                                P.wait("vector", "pe", c_pe)
                                c_ev = P.inc("vector", "dv", lambda e, oc=oc, th=th, pb=pb: e.tensor_copy(
                                    h1[:, oc * TG + th * 512: oc * TG + (th + 1) * 512], pb[:]))
                                P.free_bank(bank, "dv", c_ev)
                        P.barrier(("tensor", "vector", "scalar", "gpsimd", "sync"))
                        P.flush()

                    with ExitStack() as ph_ug:
                        wupsl = [ph_ug.enter_context(nc.sbuf_tensor(f"wup{g}_{i}", [128, 16 * 256], BF16)) for i in range(2)]
                        wgtsl = [ph_ug.enter_context(nc.sbuf_tensor(f"wgt{g}_{i}", [128, 16 * 256], BF16)) for i in range(2)]
                        cons = {}
                        prev_sil = 0
                        sil_hist = []
                        for icp in range(32):
                            bidx = icp % 2
                            prevc = cons.get(icp - 2)
                            if prevc:
                                P.wait("gpsimd", "pe", prevc)
                            vu = P.dma("gpsimd", "Lg", wupsl[bidx].ap().rearrange("p (a b) -> p a b", a=16), wup_t[icp].rearrange("a p b -> p a b"))
                            vg = P.dma("gpsimd", "Lg", wgtsl[bidx].ap().rearrange("p (a b) -> p a b", a=16), wgt_t[icp].rearrange("a p b -> p a b"))
                            P.wait("tensor", "Lg", vg)
                            c_pe = 0
                            for ih in range(2):
                                ic = icp * 2 + ih
                                pu = PB[0 + (ic % 2) * 2]
                                pg = PB[1 + (ic % 2) * 2]
                                bu = f"pb{0 + (ic % 2) * 2}"
                                bgk = f"pb{1 + (ic % 2) * 2}"
                                P.use_bank(bu)
                                for fc in range(16):
                                    P.inc("tensor", "pe", lambda e, pu=pu, w=wupsl[bidx], fc=fc, ih=ih: e.matmul(
                                        pu[:], w[:, fc * 256 + ih * 128: fc * 256 + (ih + 1) * 128],
                                        h1[:, fc * TG: fc * TG + 512],
                                        start=(fc == 0), stop=(fc == 15)))
                                P.use_bank(bgk)
                                for fc in range(16):
                                    c_pe = P.inc("tensor", "pe", lambda e, pg=pg, w=wgtsl[bidx], fc=fc, ih=ih: e.matmul(
                                        pg[:], w[:, fc * 256 + ih * 128: fc * 256 + (ih + 1) * 128],
                                        h1[:, fc * TG: fc * TG + 512],
                                        start=(fc == 0), stop=(fc == 15)))
                                sbuf_si = silu_sb2[ic % 2]
                                P.wait("scalar", "pe", c_pe)
                                P.wait("scalar", "dv", prev_sil)
                                c_si = P.inc("scalar", "ac", lambda e, pg=pg, ic=ic, sbuf_si=sbuf_si: e.activation(
                                    sbuf_si[:], pg[:], AF.Silu, scale=sgt_c[:, ic:ic + 1]))
                                P.free_bank(bgk, "ac", c_si)
                                P.wait("vector", "ac", c_si)
                                c_a = P.inc("vector", "dv", lambda e, pu=pu, ic=ic, sbuf_si=sbuf_si: e.scalar_tensor_tensor(
                                    abuf[:, ic * 512:(ic + 1) * 512], pu[:], sup_c[:, ic:ic + 1],
                                    sbuf_si[:], ALU.mult, ALU.mult))
                                P.free_bank(bu, "dv", c_a)
                                sil_hist.append(c_a)
                                prev_sil = sil_hist[-2] if len(sil_hist) >= 2 else 0
                            cons[icp] = c_pe
                        P.barrier(("gpsimd", "tensor", "vector", "scalar"))
                        P.flush()

                    dn_prefetch = {}
                    for icg0 in range(2):
                        v = P.dma("gpsimd", "Lg", wdsl[icg0].ap().rearrange("p (a b) -> p a b", a=4),
                                  wdn_t[0, icg0])
                        dn_prefetch[(0, icg0)] = v
                    with ExitStack() as ph_bf:
                        pong = ph_bf.enter_context(nc.sbuf_tensor(f"pong{g}", [128, 64 * 128], BF16))
                        for qt in range(TG // 128):
                            src_is_a = True
                            c_last = P.cnt["dv"]
                            for hstep in (32, 16, 8, 4, 2, 1):
                                npair = 64 // (2 * hstep)

                                def mk(eng_fn, dstbuf, srcbuf, qt=qt, hstep=hstep, npair=npair, off_s=0, off_d=0):
                                    pass
                                if src_is_a:
                                    def s_ap(cc0, qt=qt):
                                        return None
                                def apv(buf, start_c, qt=qt, hstep=hstep, npair=npair, width=None):
                                    base_ap = buf.ap()
                                    stride = TG if buf is abuf else 128
                                    off = base_ap.offset + start_c * stride + qt * 128 * (1 if buf is abuf else 0)
                                    if buf is abuf:
                                        off = base_ap.offset + start_c * TG + qt * 128
                                        return bass.AP(tensor=base_ap.tensor, offset=off,
                                                       ap=[[base_ap.ap[0][0], 128], [2 * hstep * TG, npair], [TG, hstep], [1, 128]])
                                    else:
                                        off = base_ap.offset + start_c * 128
                                        return bass.AP(tensor=base_ap.tensor, offset=off,
                                                       ap=[[base_ap.ap[0][0], 128], [2 * hstep * 128, npair], [128, hstep], [1, 128]])
                                src = abuf if src_is_a else pong
                                dstb = pong if src_is_a else abuf
                                a0 = apv(src, 0)
                                a1 = apv(src, hstep)
                                d0 = apv(dstb, 0)
                                d1 = apv(dstb, hstep)
                                c1 = P.inc("vector", "dv", lambda e, d0=d0, a0=a0, a1=a1: e.tensor_add(d0, a0, a1))
                                c_last = P.inc("vector", "dv", lambda e, d1=d1, a0=a0, a1=a1: e.tensor_sub(d1, a0, a1))
                                src_is_a = not src_is_a
                        P.barrier(("vector", "tensor"))
                        P.flush()

                    for c in range(64):
                        bank = f"pb{c % 4}"
                        pb = PB[c % 4]
                        P.use_bank(bank)
                        c_pe = P.inc("tensor", "pe", lambda e, pb=pb, c=c: e.matmul(
                            pb[:], hu[:], abuf[:, c * 512:(c + 1) * 512], start=True, stop=True))
                        P.wait("vector", "pe", c_pe)
                        c_ev = P.inc("vector", "dv", lambda e, pb=pb, c=c: e.tensor_copy(
                            abuf[:, c * 512:(c + 1) * 512], pb[:]))
                        P.free_bank(bank, "dv", c_ev)

                    with ExitStack() as ph_dn:
                        P.barrier(("tensor", "vector", "gpsimd"))
                        wdsl_hist = []
                        for og in range(4):
                            consumed = {}
                            for icg in range(16):
                                bidx = icg % 2
                                if (og, icg) in dn_prefetch:
                                    vw = dn_prefetch[(og, icg)]
                                else:
                                    if len(wdsl_hist) >= 2:
                                        P.wait("gpsimd", "pe", wdsl_hist[-2])
                                    vw = P.dma("gpsimd", "Lg", wdsl[bidx].ap().rearrange("p (a b) -> p a b", a=4),
                                               wdn_t[og, icg])
                                P.wait("tensor", "Lg", vw)
                                c_pe = 0
                                for ics in range(4):
                                    ic = icg * 4 + ics
                                    for tcl in range(4):
                                        pb = PB[tcl]
                                        if icg == 0 and ics == 0:
                                            P.use_bank(f"pb{tcl}")
                                        c_pe = P.inc("tensor", "pe", lambda e, pb=pb, bidx=bidx, ics=ics, ic=ic, tcl=tcl: e.matmul(
                                            pb[:], abuf[:, ic * 512 + tcl * 128: ic * 512 + (tcl + 1) * 128],
                                            wdsl[bidx][:, ics * 512:(ics + 1) * 512],
                                            start=(ic == 0), stop=(ic == 63)))
                                consumed[icg] = c_pe
                                wdsl_hist.append(c_pe)
                            P.wait("vector", "pe", P.cnt["pe"])
                            for tcl in range(4):
                                pb = PB[tcl]
                                tc = g * 4 + tcl
                                dst = x3pre[:, tc * 2048 + og * 512: tc * 2048 + (og + 1) * 512]
                                c_ev = P.inc("vector", "dv", lambda e, dst=dst, pb=pb, og=og: e.tensor_mul(
                                    dst, pb[:], sdp_tab[:, og * 512:(og + 1) * 512]))
                                P.free_bank(f"pb{tcl}", "dv", c_ev)
                        ln2_last = 0
                        for tcl in range(4):
                            tc = g * 4 + tcl
                            sl = x3pre[:, tc * 2048:(tc + 1) * 2048]
                            c_a = P.inc("vector", "dv", lambda e, sl=sl, tcl=tcl: e.tensor_add(
                                sl, sl, x2g[:, tcl * 2048:(tcl + 1) * 2048]))
                            c_s1 = P.inc("vector", "dv", lambda e, sl=sl, tcl=tcl: e.tensor_scalar(
                                lnsc2[:], sl, 0.0, None, ALU.add, ALU.add, accum_out=lnst2[:, 4 * tcl:4 * tcl + 1]))
                            ln2_last = P.inc("vector", "dv", lambda e, sl=sl, tcl=tcl: e.scalar_tensor_tensor(
                                lnsc2[:], sl, 1.0, sl, ALU.mult, ALU.mult, accum_out=lnst2[:, 4 * tcl + 1:4 * tcl + 2]))
                        lap2 = lnst2.ap()
                        m2 = bass.AP(tensor=lap2.tensor, offset=lap2.offset, ap=[[lap2.ap[0][0], 128], [4, 4]])
                        q2 = bass.AP(tensor=lap2.tensor, offset=lap2.offset + 1, ap=[[lap2.ap[0][0], 128], [4, 4]])
                        s2 = bass.AP(tensor=lap2.tensor, offset=lap2.offset + 2, ap=[[lap2.ap[0][0], 128], [4, 4]])
                        P.wait("scalar", "dv", ln2_last)
                        P.inc("scalar", "ac", lambda e: e.mul(m2, m2, 1.0 / D))
                        P.inc("scalar", "ac", lambda e: e.mul(q2, q2, 1.0 / D))
                        c_m2 = P.inc("scalar", "ac", lambda e: e.activation(s2, m2, AF.Square))
                        P.wait("vector", "ac", c_m2)
                        c_v = P.inc("vector", "dv", lambda e: e.tensor_sub(q2, q2, s2))
                        P.wait("scalar", "dv", c_v)
                        P.inc("scalar", "ac", lambda e: e.activation(q2, q2, AF.Ln, bias=epsc[:]))
                        c_rc = P.inc("scalar", "ac", lambda e: e.activation(q2, q2, AF.Exp, scale=nhalf[:]))
                        P.wait("vector", "ac", c_rc)
                        vo_prev = {}
                        for tcl in range(4):
                            tc = g * 4 + tcl
                            sl = x3pre[:, tc * 2048:(tc + 1) * 2048]
                            if tcl >= 1:
                                P.wait("vector", "La", vo_prev[tcl - 1])
                            c_ap = P.inc("vector", "dv", lambda e, sl=sl, tcl=tcl: e.tensor_scalar(
                                lnsc2[:], sl, lnst2[:, 4 * tcl:4 * tcl + 1], lnst2[:, 4 * tcl + 1:4 * tcl + 2],
                                ALU.subtract, ALU.mult))
                            P.wait("scalar", "dv", c_ap)
                            vo_prev[tcl] = P.dma("scalar", "La", yout[tc * 128:(tc + 1) * 128, :], lnsc2[:])
                        P.barrier()
                        P.flush()

        P.wait("sync", "La", P.cnt["La"])
        P.flush()

    return nc


_NC_CACHE = None


def kernel(**inputs):
    global _NC_CACHE
    ins = {k: np.asarray(v) for k, v in inputs.items()}
    x = ins["L0_attn_x"].reshape(S, D)

    def colmajor_ok(a):
        return np.ascontiguousarray(a, dtype=np.float32)

    wq = ins["L0_attn_wq"]; wk = ins["L0_attn_wk"]; wv = ins["L0_attn_wv"]
    wg = ins["L0_attn_wg"]; wo = ins["L0_attn_wo"]

    def tile_qkv(w):
        return np.ascontiguousarray(w.reshape(16, 128, 16, 128).transpose(2, 0, 1, 3))

    wq_t = tile_qkv(wq); wk_t = tile_qkv(wk); wv_t = tile_qkv(wv); wg_t = tile_qkv(wg)
    wo_t = np.ascontiguousarray(wo.reshape(16, 128, 4, 512).transpose(2, 0, 1, 3))
    wup_t = np.ascontiguousarray(ins["L0_ffn_W_up"].T.reshape(16, 128, 32, 256).transpose(2, 0, 1, 3))
    wgt_t = np.ascontiguousarray(ins["L0_ffn_W_gate"].T.reshape(16, 128, 32, 256).transpose(2, 0, 1, 3))
    wdn_t = np.ascontiguousarray(ins["L0_ffn_W_down"].T.reshape(16, 4, 128, 4, 512).transpose(3, 0, 2, 1, 4))

    H128 = _hadamard(128)
    h128p = (H128 / (SCALE * np.sqrt(np.float32(D)))).astype(BF)
    h128n = (-H128 / (SCALE * np.sqrt(np.float32(D)))).astype(BF)
    h128u = (H128 / (SCALE * np.sqrt(np.float32(INTER)))).astype(BF)

    if _NC_CACHE is None:
        _NC_CACHE = build_nc()
    nc = _NC_CACHE

    in_maps = []
    for c in range(NC):
        emask = (np.arange(NC) < c).astype(np.float32)
        eexp = np.maximum(c - 1 - np.arange(NC), 0).astype(np.float32)
        in_maps.append({
            "xin": np.ascontiguousarray(x[c * T:(c + 1) * T]),
            "wq_t": wq_t, "wk_t": wk_t, "wv_t": wv_t, "wg_t": wg_t, "wo_t": wo_t,
            "wup_t": wup_t, "wgt_t": wgt_t, "wdn_t": wdn_t,
            "bq": ins["L0_attn_bq"], "bk": ins["L0_attn_bk"], "bv": ins["L0_attn_bv"],
            "bg": ins["L0_attn_bg"], "bo": ins["L0_attn_bo"],
            "gnw": ins["L0_attn_gn_w"], "gnb": ins["L0_attn_gn_b"],
            "gam": ins["L0_attn_gamma"],
            "su_ug": ins["L0_ffn_SU_ug"], "su_dp": ins["L0_ffn_SU_dp"],
            "ws_ug": ins["L0_ffn_Wscale_ug"], "ws_dp": ins["L0_ffn_Wscale_dp"],
            "h128p": h128p, "h128n": h128n, "h128u": h128u,
            "emask": emask, "eexp": eexp,
        })
    res = run_bass_kernel_spmd(nc, in_maps, core_ids=list(range(NC)))
    out = np.concatenate([res.results[c]["y"] for c in range(NC)], axis=0)
    return out.reshape(1, S, D).astype(np.float32)
